# version 15
# baseline (speedup 1.0000x reference)
import numpy as np
import concourse.bass as bass
import concourse.mybir as mybir
from concourse.bass_utils import run_bass_kernel_spmd

F32 = mybir.dt.float32
BF16 = mybir.dt.bfloat16
AF = mybir.ActivationFunctionType
ALU = mybir.AluOpType
AX = mybir.AxisListType
AP = bass.AP

N_CORES = 8
D = 1024
NH, HD, NE, EH = 16, 64, 8, 256
T_ST = 256
PT = 2
NSB = 4

ENGS = ("tensor", "vector", "scalar", "sync", "gpsimd")


class Sched:

    def __init__(self):
        self.q = {e: [] for e in ENGS}
        self.cnt = {e: 0 for e in ENGS}
        self.dcnt = {e: 0 for e in ENGS}
        self.used = set()

    def _deps(self, deps):
        dd = tuple(d for d in deps if d is not None)
        for d in dd:
            self.used.add(d)
        return dd

    def op(self, eng, fn, deps=()):
        self.q[eng].append(("w", self._deps(deps), self.cnt[eng]))
        self.q[eng].append(("i", fn, self.cnt[eng] + 1))
        self.cnt[eng] += 1
        return (eng, self.cnt[eng])

    def dma(self, eng, fn, deps=()):
        self.q[eng].append(("w", self._deps(deps), self.cnt[eng]))
        self.q[eng].append(("d", fn))
        self.dcnt[eng] += 1
        return (eng + "_dma", 16 * self.dcnt[eng])


def build_program(n_tokens, debug=False):
    n_st = n_tokens // T_ST
    nc = bass.Bass("TRN2", target_bir_lowering=False,
                   detect_race_conditions=False)

    x_in = nc.dram_tensor("x", [n_tokens, D], F32, kind="ExternalInput")
    wqkv_d = nc.dram_tensor("wqkv", [D, 3 * D], F32, kind="ExternalInput")
    bqkv_d = nc.dram_tensor("bqkv", [3 * D], F32, kind="ExternalInput")
    wout_d = nc.dram_tensor("wout", [D, D], F32, kind="ExternalInput")
    bout_d = nc.dram_tensor("bout", [D], F32, kind="ExternalInput")
    wg_d = nc.dram_tensor("wg", [NE, D], F32, kind="ExternalInput")
    lns_d = nc.dram_tensor("lns", [NE, D], F32, kind="ExternalInput")
    lnb_d = nc.dram_tensor("lnb", [NE, D], F32, kind="ExternalInput")
    w1_d = nc.dram_tensor("w1", [NE, D, EH], F32, kind="ExternalInput")
    b1_d = nc.dram_tensor("b1", [NE, EH], F32, kind="ExternalInput")
    w2_d = nc.dram_tensor("w2", [NE, EH, D], F32, kind="ExternalInput")
    b2_d = nc.dram_tensor("b2", [NE, D], F32, kind="ExternalInput")
    tmask_d = nc.dram_tensor("tmaskc", [128, 512], F32, kind="ExternalInput")
    sel2_d = nc.dram_tensor("sel2c", [128, 128], F32, kind="ExternalInput")
    sel8_d = nc.dram_tensor("sel8c", [NE, NE, 128], F32, kind="ExternalInput")
    ones_d = nc.dram_tensor("onesc", [4096], F32, kind="ExternalInput")
    y_out = nc.dram_tensor("y", [n_tokens, D], F32, kind="ExternalOutput")
    vscr = nc.dram_tensor("vscr", [2 * T_ST * D], BF16, kind="Internal")
    hscr = nc.dram_tensor("hscr", [4, 128 * 64], BF16, kind="Internal")
    dbg_specs = [
        ("xlnT", [128, 8 * T_ST], BF16), ("qmov", [67, 32 * 128], BF16),
        ("kstat", [67, 64 * 128], BF16), ("bankA", [128, 512], F32),
        ("bankF", [128, 512], F32), ("ptil", [128, 512], BF16),
        ("attnb", [128, 512], BF16), ("o_dT", [128, 8 * T_ST], BF16),
        ("a_tm", [128, PT * D], BF16),         ("oh_em", [NE, T_ST], BF16), ("hm_sb", [128, 16 * T_ST], BF16),
        ("x_res", [128, PT * D], BF16),
        ("smalls", [128, 64], BF16), ("hmT", [128, 64], BF16),
    ]
    dbg_outs = {}
    if debug:
        for nm, shp, dt_ in dbg_specs:
            dbg_outs[nm] = nc.dram_tensor("dbg_" + nm, shp, dt_,
                                          kind="ExternalOutput")

    s = Sched()

    from contextlib import ExitStack
    with ExitStack() as ctx, nc.Block() as block:
        wqkv_sb = ctx.enter_context(nc.sbuf_tensor("wqkv_sb", [128, 8, 3 * D], BF16))
        wout_sb = ctx.enter_context(nc.sbuf_tensor("wout_sb", [128, 8, D], BF16))
        w1_sb = ctx.enter_context(nc.sbuf_tensor("w1_sb", [128, 8, NE, EH], BF16))
        w2_sb = ctx.enter_context(nc.sbuf_tensor("w2_sb", [128, 16, D], BF16))
        wg_sb = ctx.enter_context(nc.sbuf_tensor("wg_sb", [128, 8, NE], BF16))
        lns_sb = ctx.enter_context(nc.sbuf_tensor("lns_sb", [128, 8, NE], BF16))
        lnb_sb = ctx.enter_context(nc.sbuf_tensor("lnb_sb", [128, 8, NE], BF16))
        b1r_sb = ctx.enter_context(nc.sbuf_tensor("b1r_sb", [128, 16], F32))
        b1c_sb = ctx.enter_context(nc.sbuf_tensor("b1c_sb", [128, 16], F32))
        b2_sb = ctx.enter_context(nc.sbuf_tensor("b2_sb", [33, D], BF16))
        bqm2_sb = ctx.enter_context(nc.sbuf_tensor("bqm2_sb", [64, 16], F32))
        bk_lo = ctx.enter_context(nc.sbuf_tensor("bk_lo", [64, 16], F32))
        bout_sb = ctx.enter_context(nc.sbuf_tensor("bout_sb", [128, 8], F32))
        ones_col = ctx.enter_context(nc.sbuf_tensor("ones_col", [33, 128], BF16))
        sel8 = ctx.enter_context(nc.sbuf_tensor("sel8", [NE, NE, 128], BF16))
        SEL2 = ctx.enter_context(nc.sbuf_tensor("SEL2", [128, 128], BF16))
        tmask = ctx.enter_context(nc.sbuf_tensor("tmask", [128, 512], BF16))
        consts = ctx.enter_context(nc.sbuf_tensor("consts", [128, 12], F32))
        x_res0 = ctx.enter_context(nc.sbuf_tensor("x_res0", [128, PT, D], BF16))
        x_res1 = ctx.enter_context(nc.sbuf_tensor("x_res1", [128, PT, D], BF16))
        xlnT2 = ctx.enter_context(nc.sbuf_tensor("xlnT2", [128, 8, T_ST], BF16))
        st6b = ctx.enter_context(nc.sbuf_tensor("st6b", [128, 12], F32))
        mv2b = ctx.enter_context(nc.sbuf_tensor("mv2b", [128, 16], F32))
        acolb = ctx.enter_context(nc.sbuf_tensor("acolb", [128, 1], F32))
        bcolb = ctx.enter_context(nc.sbuf_tensor("bcolb", [128, 1], F32))
        sq4b = ctx.enter_context(nc.sbuf_tensor("sq4b", [128, 4], F32))
        sq4p = ctx.enter_context(nc.sbuf_tensor("sq4p", [128, 16], F32))
        kstat = ctx.enter_context(nc.sbuf_tensor("kstat", [67, 64, 128], BF16))
        qmov = ctx.enter_context(nc.sbuf_tensor("qmov", [67, 32, 128], BF16))
        v_tg = ctx.enter_context(nc.sbuf_tensor("v_tg", [128, 16, 64], BF16))
        sq4 = ctx.enter_context(nc.sbuf_tensor("sq4", [128, 16], F32))
        smalls = ctx.enter_context(nc.sbuf_tensor("smalls", [128, 64], BF16))
        hmT = ctx.enter_context(nc.sbuf_tensor("hmT", [128, 64], BF16))
        bankA = ctx.enter_context(nc.sbuf_tensor("bankA", [128, 512], F32))
        bankD = ctx.enter_context(nc.sbuf_tensor("bankD", [128, 512], F32))
        bankE = ctx.enter_context(nc.sbuf_tensor("bankE", [128, 512], F32))
        bankB = ctx.enter_context(nc.sbuf_tensor("bankB", [128, 512], F32))
        bankC = ctx.enter_context(nc.sbuf_tensor("bankC", [128, 512], F32))
        bankF = ctx.enter_context(nc.sbuf_tensor("bankF", [128, 512], F32))
        ptil = ctx.enter_context(nc.sbuf_tensor("ptil", [128, 512], BF16))
        attnb = ptil
        o_dT = ctx.enter_context(nc.sbuf_tensor("o_dT", [128, 8, T_ST], BF16))
        afm = ctx.enter_context(nc.sbuf_tensor("afm", [128, 2, T_ST], BF16))
        a_tm = ctx.enter_context(nc.sbuf_tensor("a_tm", [128, PT, D], BF16))
        xhatT = ctx.enter_context(nc.sbuf_tensor("xhatT", [128, 8, T_ST], BF16))
        st6 = ctx.enter_context(nc.sbuf_tensor("st6", [128, 12], F32))
        mv2 = ctx.enter_context(nc.sbuf_tensor("mv2", [128, 16], F32))
        acol = ctx.enter_context(nc.sbuf_tensor("acol", [128, 1], F32))
        bcol = ctx.enter_context(nc.sbuf_tensor("bcol", [128, 1], F32))
        mx8 = ctx.enter_context(nc.sbuf_tensor("mx8", [32, 8], F32))
        oh_em = ctx.enter_context(nc.sbuf_tensor("oh_em", [NE, T_ST], BF16))
        hm_sb = ctx.enter_context(nc.sbuf_tensor("hm_sb", [128, 16, T_ST], BF16))
        lgp = bankA[0:32, 0:T_ST]
        lgT = bankB[0:32, 0:T_ST]
        ohT = bankC[0:32, 0:T_ST]
        ohp = bankB[32:64, 0:T_ST]
        pQK0 = ctx.enter_context(nc.psum_tensor("pQK0", [128, T_ST], F32))
        pQK1 = ctx.enter_context(nc.psum_tensor("pQK1", [128, T_ST], F32))
        pV = ctx.enter_context(nc.psum_tensor("pV", [128, D], F32))
        pRN = ctx.enter_context(nc.psum_tensor("pRN", [128, 512], F32))
        pRN2 = ctx.enter_context(nc.psum_tensor("pRN2", [128, 512], F32))
        pS = ctx.enter_context(nc.psum_tensor("pS", [128, 512], F32))
        pOT = ctx.enter_context(nc.psum_tensor("pOT", [128, 512], F32))
        sm_t = ctx.enter_context(nc.semaphore("sm_t"))
        sm_v = ctx.enter_context(nc.semaphore("sm_v"))
        sm_s = ctx.enter_context(nc.semaphore("sm_s"))
        sm_sy = ctx.enter_context(nc.semaphore("sm_sy"))
        sm_g = ctx.enter_context(nc.semaphore("sm_g"))
        dm_sy = ctx.enter_context(nc.semaphore("dm_sy"))
        dm_g = ctx.enter_context(nc.semaphore("dm_g"))
        dm_sc = ctx.enter_context(nc.semaphore("dm_sc"))

        SEM = {"tensor": sm_t, "vector": sm_v, "scalar": sm_s,
               "sync": sm_sy, "gpsimd": sm_g,
               "sync_dma": dm_sy, "gpsimd_dma": dm_g, "scalar_dma": dm_sc}

        (C_EPS5, C_EPS6, C_PI2, C_M10, C_TINY, C_2, C_Q25, C_M2,
         C_M1, C_MQ25, C_HALF) = range(11)

        def cc(i, n=128):
            return consts[0:n, i:i + 1]

        tk_w = []

        def gld(dst_ap, src_ap):
            tk_w.append(s.dma("gpsimd",
                              lambda e, d=dst_ap, sr=src_ap: e.dma_start(
                                  out=d, in_=sr, allow_slow_non_contiguous=True)))

        for h in range(NH):
            for sec in range(2):
                gld(wqkv_sb[:, :, h * 128 + sec * 64:h * 128 + sec * 64 + 64],
                    AP(wqkv_d, sec * 1024 + h * 64,
                       [[3 * D, 128], [128 * 3 * D, 8], [1, 64]]))
        gld(wqkv_sb[:, :, 2 * D:3 * D],
            AP(wqkv_d, 2 * D, [[3 * D, 128], [128 * 3 * D, 8], [1, D]]))
        gld(wout_sb.ap(), wout_d.ap().rearrange("(kc p) o -> p kc o", p=128))
        for e_ in range(NE):
            gld(w1_sb[:, :, e_, :],
                w1_d.ap()[e_].rearrange("(kc p) hh -> p kc hh", p=128))
        gld(w2_sb.ap(), w2_d.ap().rearrange("e (hc p) o -> p (e hc) o", p=128))
        for e_ in range(NE):
            gld(wg_sb[:, :, e_], wg_d.ap()[e_].rearrange("(kc p) -> p kc", p=128))
            gld(lns_sb[:, :, e_], lns_d.ap()[e_].rearrange("(kc p) -> p kc", p=128))
            gld(lnb_sb[:, :, e_], lnb_d.ap()[e_].rearrange("(kc p) -> p kc", p=128))
        gld(b1r_sb.ap(), b1_d.ap().rearrange("e (hc p) -> p (e hc)", p=128))
        gld(b2_sb[0:NE, :], b2_d.ap())
        gld(bqm2_sb.ap(), bqkv_d.ap()[0:D].rearrange("(h d) -> d h", d=64))
        gld(bk_lo.ap(), bqkv_d.ap()[D:2 * D].rearrange("(h d) -> d h", d=64))
        gld(b2_sb[32:33, :], bqkv_d.ap()[2 * D:3 * D][None, :])
        gld(bout_sb.ap(), bout_d.ap().rearrange("(oc p) -> p oc", p=128))

        tk_ms = []

        def ms(ap_, val):
            tk_ms.append(s.op("vector", lambda e, a=ap_, v=val: e.memset(a, v)))

        ms(cc(C_EPS5), 1e-5)
        ms(cc(C_EPS6), 1e-6)
        ms(cc(C_PI2), float(np.pi / 2))
        ms(cc(C_M10), -10.0)
        ms(cc(C_TINY), 1e-9)
        ms(cc(C_2), 2.0)
        ms(cc(C_Q25), 0.25)
        ms(cc(C_M2), -2.0)
        ms(cc(C_M1), -1.0)
        ms(cc(C_MQ25), -0.25)
        ms(cc(C_HALF), 0.5)
        ms(ones_col[32:33, :], 1.0)
        gld(sel8.ap(), sel8_d.ap())
        gld(SEL2.ap(), sel2_d.ap())
        gld(tmask.ap(), tmask_d.ap())
        ms(kstat.ap(), 0.0)
        tk_w.append(s.dma("gpsimd", lambda e: e.dma_start(
            out=kstat[66:67, 0:32, :],
            in_=ones_d.ap().rearrange("(a c b) -> a c b", a=1, c=32))))
        ms(qmov.ap(), 0.0)
        ms(qmov[64:65, :, :], 1.0)
        ms(bankF.ap(), 0.0)

        def mk_b1c(te):
            last = None
            for hc in range(16):
                for kc in range(8):
                    last = te.matmul(
                        pQK0[:, hc:hc + 1],
                        w1_sb[:, kc, hc // 2, (hc % 2) * 128:(hc % 2) * 128 + 128],
                        lnb_sb[:, kc, hc // 2:hc // 2 + 1],
                        start=(kc == 0), stop=(kc == 7))
            return last
        tk_b1cm = s.op("tensor", mk_b1c, deps=tk_w)
        tk_b1c = s.op("vector", lambda e: e.tensor_add(
            b1c_sb[:, :], pQK0[:, 0:16], b1r_sb[:, :]), deps=[tk_b1cm])
        tk_w1f = s.op("vector", lambda e: e.tensor_mul(
            w1_sb.ap(), w1_sb.ap(),
            lns_sb[:, :, :, None].broadcast_to([128, 8, NE, EH])),
            deps=[tk_b1cm])

        setup_done = tk_w + tk_ms + [tk_b1c, tk_w1f]

        pQK = [pQK0, pQK1]
        pRNs = [pRN, pRN2]
        xres = [x_res0, x_res1]
        PVOFF = [0, 512]
        hsc_guard = {}
        last_tdmp = [None]
        ST = [dict() for _ in range(n_st)]

        scrA = (st6, sq4, mv2, acol, bcol)
        scrB = (st6b, sq4b, mv2b, acolb, bcolb)

        def layer_norm(scr, slot, src, dst, dep0, extra=()):
            st6x, sq4x, mv2x, acolx, bcolx = scr
            c0 = 8 * slot

            def col(i):
                return mv2x[:, c0 + i:c0 + i + 1]
            t1 = s.op("vector", lambda e: e.bn_stats(st6x[:, 0:6], src[:, 0:512]),
                      deps=dep0)
            t1 = s.op("vector", lambda e: e.bn_stats(st6x[:, 6:12], src[:, 512:1024]),
                      deps=[t1])
            t1 = s.op("vector", lambda e: e.tensor_reduce(
                col(0), AP(st6x, 1, [[12, 128], [3, 4]]), AX.X, ALU.add),
                deps=[t1])
            t1 = s.op("vector", lambda e: e.tensor_mul(
                sq4x[:, 0:4],
                AP(st6x, 1, [[12, 128], [3, 4]]),
                AP(st6x, 1, [[12, 128], [3, 4]])), deps=[t1])
            t1 = s.op("vector", lambda e: e.tensor_reduce(
                col(1), sq4x[:, 0:4], AX.X, ALU.add), deps=[t1])
            t1 = s.op("vector", lambda e: e.tensor_reduce(
                col(2), AP(st6x, 2, [[12, 128], [3, 4]]), AX.X, ALU.add),
                deps=[t1])
            t1 = s.op("vector", lambda e: e.tensor_scalar_mul(
                col(3), col(2), 1.0 / 1024.0), deps=[t1])
            t1 = s.op("vector", lambda e: e.scalar_tensor_tensor(
                col(3), col(1), 0.25, col(3), ALU.mult, ALU.add), deps=[t1])
            t1 = s.op("vector", lambda e: e.scalar_tensor_tensor(
                col(4), col(0), -0.0625, col(0), ALU.mult, ALU.mult),
                deps=[t1])
            t1 = s.op("vector", lambda e: e.tensor_add(
                col(5), col(3), col(4)), deps=[t1])
            t3 = s.op("scalar", lambda e: e.activation(
                acolx[:, :], col(5), AF.Sqrt, bias=cc(C_EPS5),
                scale=1.0), deps=[t1])
            t3 = s.op("vector", lambda e: e.reciprocal(
                acolx[:, :], acolx[:, :]), deps=[t3])
            t4 = s.op("vector", lambda e: e.scalar_tensor_tensor(
                bcolx[:, :], col(0), -0.25, acolx[:, :],
                ALU.mult, ALU.mult), deps=[t3])
            return s.op("scalar", lambda e: e.activation(
                dst, src, AF.Identity, bias=bcolx[:, :], scale=acolx[:, :]),
                deps=[t4] + list(extra))

        def pP0(st):
            xr = xres[st % 2]
            dep = list(setup_done) if st == 0 else (
                [] if st == 1 else [ST[st - 2]["store"]])
            ST[st]["x"] = s.dma("gpsimd", lambda e, o=st * T_ST, xr=xr:
                e.dma_start(
                    out=xr.ap(),
                    in_=x_in.ap()[o:o + T_ST, :]
                        .rearrange("(pt p) d -> p pt d", p=128)),
                deps=dep)

        def pP1(st):
            xr = xres[st % 2]
            tk_xlnT = []
            t5 = None
            for pt in range(PT):
                dep0 = [ST[st]["x"]]
                if pt == 0 and st > 0:
                    dep0 += [ST[st - 1]["ln1_last"]]
                extra = [] if (st == 0 or pt > 0) else \
                    [ST[st - 1]["mm_o_last"]]
                t5 = layer_norm(scrB, pt, xr[:, pt, :],
                                AP(o_dT, pt * D, [[2048, 128], [1, D]]),
                                dep0, extra=extra)
                gx = [] if st == 0 else [ST[st - 1]["mm_v_last"]]
                tk_xlnT.append(s.dma("sync", lambda e, pt=pt:
                    e.dma_start_transpose(
                        AP(xlnT2, pt * 128, [[8 * T_ST, 128], [T_ST, 8], [1, 128]]),
                        AP(o_dT, pt * D, [[2048, 128], [128, 8], [1, 128]])),
                    deps=[t5] + gx))
            ST[st]["ln1_last"] = t5
            ST[st]["xlnT"] = tk_xlnT

        def pP2(st):
            if st == 0:
                ev_tok = [None, None]
            else:
                ev_tok = [list(ST[st - 1]["evac_eA"]),
                          list(ST[st - 1]["evac_eB"])]
            gpv = [] if st == 0 else list(ST[st - 1]["vtm"])
            gab = [] if st == 0 else [ST[st - 1]["mm_ab_last"]]
            tk_qk_evac = []
            for h in range(NH):
                off = PVOFF[h % 2]

                def mm_pair(te, h=h, off=off):
                    last = None
                    for kc in range(8):
                        last = te.matmul(
                            AP(pV, off, [[1024, 128], [1, 256]]),
                            wqkv_sb[:, kc, h * 128:h * 128 + 128],
                            xlnT2[:, kc, :],
                            start=(kc == 0), stop=(kc == 7))
                    return last
                dd = (ST[st]["xlnT"] if h < 2 else []) + \
                    (list(ev_tok[h % 2]) if ev_tok[h % 2] else []) + \
                    (gpv if h < 2 else [])
                tmm = s.op("tensor", mm_pair, deps=dd)
                e1 = s.op("vector", lambda e, h=h, off=off: e.tensor_scalar(
                    AP(qmov, h * 128, [[32 * 128, 64], [16 * 128, 2], [1, 128]]),
                    AP(pV, off, [[1024, 64], [1, 2], [2, 128]]),
                    -2.0, bqm2_sb[:, h:h + 1], ALU.mult, ALU.add),
                    deps=[tmm] + (gab if h < 2 else []))
                e2 = s.op("vector", lambda e, h=h, off=off: e.tensor_scalar_add(
                    AP(kstat, h * 128, [[64 * 128, 64], [16 * 128, 2], [1, 128]]),
                    AP(pV, 64 * 1024 + off, [[1024, 64], [1, 2], [2, 128]]),
                    bk_lo[:, h:h + 1]),
                    deps=[tmm] + (gab if h < 2 else []))
                ev_tok[h % 2] = [e1, e2]
                tk_qk_evac += [e1, e2]
            ST[st]["evac_eA"] = ev_tok[0]
            ST[st]["evac_eB"] = ev_tok[1]
            ST[st]["evac_all"] = tk_qk_evac

        def pP2b(st):
            tk_vtm = []
            tmmv = None
            for pt in range(PT):
                def mm_v(te, pt=pt):
                    last = None
                    for j in range(2):
                        for kc in range(8):
                            last = te.matmul(
                                pV[:, j * 512:j * 512 + 512],
                                xlnT2[:, kc, pt * 128:pt * 128 + 128],
                                wqkv_sb[:, kc, 2 * D + j * 512:2 * D + j * 512 + 512],
                                start=(kc == 0), stop=False)
                        last = te.matmul(
                            pV[:, j * 512:j * 512 + 512],
                            ones_col[32:33, :],
                            b2_sb[32:33, j * 512:j * 512 + 512],
                            start=False, stop=True)
                    return last
                if tk_vtm:
                    dpv = [tk_vtm[-1]]
                else:
                    dpv = ST[st]["evac_all"][-2:] + (
                        [] if st == 0 else ST[st - 1]["yfm"][-2:])
                tmmv = s.op("tensor", mm_v, deps=ST[st]["xlnT"] + dpv)
                gv = [] if st == 0 else [ST[st - 1]["mm_y_last"]]
                tk_vtm.append(s.op("vector", lambda e, pt=pt: e.tensor_copy(
                    AP(hm_sb, pt * 1024, [[4096, 128], [1, 1024]]),
                    pV[:, :]), deps=[tmmv] + (gv if pt == 0 else [])))
            ST[st]["vtm"] = tk_vtm
            ST[st]["mm_v_last"] = tmmv
            vbuf = (st % 2) * T_ST * D
            gvd = [] if st < 2 else [ST[st - 2]["vg_last"]]
            ST[st]["vdump"] = s.dma("gpsimd", lambda e, vb=vbuf: e.dma_start(
                out=AP(vscr, vb, [[1024, 128], [131072, 2], [1, 1024]]),
                in_=AP(hm_sb, 0, [[4096, 128], [1024, 2], [1, 1024]])),
                deps=tk_vtm + gvd)

        def pP3(st):
            tk_rows = []
            evac = ST[st]["evac_all"]
            for par in range(2):
                godt = ([] if st == 0 else [ST[st - 1]["mm_o_last"]]) \
                    if par == 0 else []
                tq_x = [s.dma("sync", lambda e, par=par:
                    e.dma_start_transpose(
                        AP(o_dT, 0, [[2048, 128], [64, 16], [1, 64]]),
                        AP(qmov, (par * 16) * 128,
                           [[32 * 128, 64], [128, 16], [1, 128]])),
                    deps=evac + godt)]
                tsq = s.op("scalar", lambda e: e.activation(
                    AP(o_dT, 1024, [[2048, 128], [1, 1024]]),
                    AP(o_dT, 0, [[2048, 128], [1, 1024]]),
                    AF.Square, bias=0.0, scale=1.0), deps=tq_x)
                trd = s.op("vector", lambda e: e.tensor_reduce(
                    sq4p[:, :], AP(o_dT, 1024, [[2048, 128], [64, 16], [1, 64]]),
                    AX.X, ALU.add), deps=[tsq])
                tqs = s.op("vector", lambda e: e.tensor_scalar_mul(
                    smalls[:, 48:64], sq4p[:, :], 0.25), deps=[trd])
                tmq = s.op("scalar", lambda e: e.activation(
                    smalls[:, 32:48], sq4p[:, :], AF.Sqrt, bias=0.0,
                    scale=cc(C_Q25)), deps=[trd])
                tk_x2 = [s.dma("sync", lambda e, par=par:
                    e.dma_start_transpose(
                        AP(o_dT, 0, [[2048, 128], [64, 16], [1, 64]]),
                        AP(kstat, (par * 16) * 128,
                           [[64 * 128, 64], [128, 16], [1, 128]])),
                    deps=[tqs, tmq])]
                tsk = s.op("scalar", lambda e: e.activation(
                    AP(o_dT, 1024, [[2048, 128], [1, 1024]]),
                    AP(o_dT, 0, [[2048, 128], [1, 1024]]),
                    AF.Square, bias=0.0, scale=1.0), deps=tk_x2)
                trk = s.op("vector", lambda e: e.tensor_reduce(
                    sq4p[:, :], AP(o_dT, 1024, [[2048, 128], [64, 16], [1, 64]]),
                    AX.X, ALU.add), deps=[tsk])
                tks = s.op("vector", lambda e: e.tensor_copy(
                    smalls[:, 0:16], sq4p[:, :]), deps=[trk])
                tmk = s.op("scalar", lambda e: e.activation(
                    smalls[:, 16:32], sq4p[:, :], AF.Sqrt, bias=0.0, scale=1.0),
                    deps=[trk])
                gtd = [last_tdmp[0]] if last_tdmp[0] else []
                thm = s.op("vector", lambda e: e.transpose(
                    hmT[:, :], smalls[:, :]), deps=[tqs, tmq, tks, tmk] + gtd)
                hslot = 2 + par
                grw = ([] if st == 0 else [ST[st - 1]["mm_ab_last"]]) \
                    if par == 0 else []
                tdmp = s.dma("gpsimd", lambda e, hslot=hslot: e.dma_start(
                    out=AP(hscr, hslot * 8192, [[64, 128], [1, 64]]),
                    in_=hmT.ap()),
                    deps=[thm] + hsc_guard.get(hslot, []) + grw)
                rows_here = []
                for i in range(4):
                    rows_here.append(s.dma("gpsimd", lambda e, i=i, par=par,
                                           hslot=hslot: e.dma_start(
                        out=AP(qmov, 65 * 4096 + par * 2048 + 32 * i,
                               [[4096, 1], [128, 16], [1, 32]]),
                        in_=AP(hscr, hslot * 8192 + (32 * i) * 64 + 32,
                               [[64, 16], [1, 32]])),
                        deps=[tdmp]))
                    rows_here.append(s.dma("gpsimd", lambda e, i=i, par=par,
                                           hslot=hslot: e.dma_start(
                        out=AP(qmov, 66 * 4096 + par * 2048 + 32 * i,
                               [[4096, 1], [128, 16], [1, 32]]),
                        in_=AP(hscr, hslot * 8192 + (32 * i + 16) * 64 + 32,
                               [[64, 16], [1, 32]])),
                        deps=[tdmp]))
                    rows_here.append(s.dma("gpsimd", lambda e, i=i, par=par,
                                           hslot=hslot: e.dma_start(
                        out=AP(kstat, 64 * 8192 + par * 2048 + 32 * i,
                               [[8192, 1], [128, 16], [1, 32]]),
                        in_=AP(hscr, hslot * 8192 + (32 * i) * 64,
                               [[64, 16], [1, 32]])),
                        deps=[tdmp]))
                    rows_here.append(s.dma("gpsimd", lambda e, i=i, par=par,
                                           hslot=hslot: e.dma_start(
                        out=AP(kstat, 65 * 8192 + 4096 + par * 2048 + 32 * i,
                               [[8192, 1], [128, 16], [1, 32]]),
                        in_=AP(hscr, hslot * 8192 + (32 * i + 16) * 64,
                               [[64, 16], [1, 32]])),
                        deps=[tdmp]))
                hsc_guard[hslot] = rows_here
                last_tdmp[0] = tdmp
                tk_rows += rows_here
            ST[st]["rows"] = tk_rows

        def pP4(st):
            vbuf = (st % 2) * T_ST * D
            tk_odT = []
            tk_mm2_last = None if st == 0 else ST[st - 1]["mm2_last"]
            prevband = dict() if st == 0 else dict(ST[st - 1]["prevband"])
            mm2_extra = [] if st == 0 else [ST[st - 1]["hm14"]]
            tmpBs = [bankB, bankD]
            tmpCs = [bankC, bankE]
            tk_vg = []

            def mk_ab(SB):
                pb_rn = pRNs[SB % 2]

                def mm_ab(te, SB=SB, pb_rn=pb_rn):
                    last = None
                    for b2i in range(2):
                        for f in range(16):
                            blk0 = SB * 32 + b2i * 16 + f
                            last = te.matmul(
                                AP(pb_rn, (b2i * 64) * 512 + f * 32,
                                   [[512, 64], [1, 32]]),
                                AP(kstat, blk0, [[8192, 67], [128, 64]]),
                                AP(qmov, blk0, [[4096, 67], [128, 32]]),
                                start=True, stop=True,
                                tile_position=(0, b2i * 64))
                    return last
                return mm_ab

            ab = [None] * NSB
            ab[0] = s.op("tensor", mk_ab(0),
                         deps=ST[st]["rows"] + ST[st]["evac_all"] +
                         list(prevband.get(("rn", 0), [])))
            ab[1] = s.op("tensor", mk_ab(1),
                         deps=list(prevband.get(("rn", 1), [])))
            for p in range(2):
                tmpB = tmpBs[p]
                tmpC = tmpCs[p]
                tAcc = {0: [], 1: []}
                tA = []
                tN = []
                for sh in range(2):
                    for b2i in range(2):
                        t_ = s.op("vector", lambda e, sh=sh, b2i=b2i:
                            e.tensor_scalar_max(
                                bankA[b2i * 64 + sh * 32:b2i * 64 + sh * 32 + 32, :],
                                pRNs[sh][b2i * 64:b2i * 64 + 32, :], 0.0),
                            deps=[ab[2 * p + sh]] + prevband.get("A", []))
                        tA.append(t_)
                        tAcc[sh].append(t_)
                        t_ = s.op("vector", lambda e, sh=sh, b2i=b2i:
                            e.tensor_copy(
                                bankF[b2i * 64 + sh * 32:b2i * 64 + sh * 32 + 32, :],
                                pRNs[sh][b2i * 64 + 32:b2i * 64 + 64, :]),
                            deps=[ab[2 * p + sh]] + prevband.get("F", []))
                        tN.append(t_)
                        tAcc[sh].append(t_)
                if p == 0:
                    ab[2] = s.op("tensor", mk_ab(2), deps=tAcc[0])
                    ab[3] = s.op("tensor", mk_ab(3), deps=tAcc[1])
                tB = s.op("scalar", lambda e, tmpB=tmpB: e.activation(
                    tmpB[:, :], bankA[:, :], AF.Sqrt, bias=cc(C_EPS6), scale=1.0),
                    deps=tA + prevband.get(("B", p), []))
                tC = s.op("scalar", lambda e, tmpB=tmpB, tmpC=tmpC: e.activation(
                    tmpC[:, :], tmpB[:, :], AF.Sin, bias=cc(C_PI2),
                    scale=cc(C_MQ25)), deps=[tB] + prevband.get(("C", p), []))
                tU = s.op("vector", lambda e, tmpB=tmpB, tmpC=tmpC: e.tensor_mul(
                    tmpB[:, :], tmpC[:, :], tmpC[:, :]), deps=[tC])
                tW = s.op("vector", lambda e, tmpB=tmpB, tmpC=tmpC: e.tensor_scalar(
                    tmpC[:, :], tmpB[:, :], 2.0, -1.0, ALU.mult, ALU.add),
                    deps=[tU])
                tW = s.op("vector", lambda e, tmpC=tmpC: e.tensor_mul(
                    tmpC[:, :], tmpC[:, :], tmpC[:, :]), deps=[tW])
                tM = s.op("vector", lambda e, tmpB=tmpB, tmpC=tmpC: e.tensor_scalar(
                    tmpB[:, :], tmpC[:, :], 0.3, 0.85, ALU.mult, ALU.add),
                    deps=[tW])
                tDm = s.op("vector", lambda e, tmpB=tmpB, tmpC=tmpC: e.tensor_mul(
                    tmpC[:, :], tmpB[:, :], bankA[:, :]), deps=[tM])
                tR = s.op("scalar", lambda e, tmpB=tmpB, tmpC=tmpC: e.activation(
                    tmpB[:, :], tmpC[:, :], AF.Ln, bias=cc(C_EPS6),
                    scale=1.0), deps=[tDm])
                tR = s.op("scalar", lambda e, tmpB=tmpB: e.activation(
                    tmpB[:, :], tmpB[:, :], AF.Exp, bias=0.0,
                    scale=cc(C_M1)), deps=[tR])
                tF = s.op("vector", lambda e, tmpB=tmpB: e.tensor_mul(
                    bankF[:, :], bankF[:, :], tmpB[:, :]), deps=[tR] + tN)
                tP = s.op("scalar", lambda e: e.activation(
                    ptil[:, :], bankF[:, :], AF.Exp, bias=cc(C_M10), scale=1.0),
                    deps=[tF] + prevband.get("P", []))
                tSel = s.op("tensor", lambda e: e.matmul(
                    pS[:, :], SEL2[:, :], ptil[:, :], start=True, stop=True),
                    deps=[tP] + prevband.get("S", []))
                tRec = s.op("scalar", lambda e, tmpB=tmpB: e.activation(
                    tmpB[:, :], pS[:, :], AF.Ln, bias=cc(C_TINY),
                    scale=1.0), deps=[tSel])
                tAt0 = s.op("vector", lambda e, tmpB=tmpB, tmpC=tmpC: e.tensor_sub(
                    tmpC[:, :], bankF[:, :], tmpB[:, :]), deps=[tRec])
                tAt1 = s.op("scalar", lambda e, tmpC=tmpC: e.activation(
                    attnb[:, :], tmpC[:, :], AF.Exp, bias=cc(C_M10),
                    scale=1.0), deps=[tAt0] + prevband.get("At", []))
                tAt = s.op("vector", lambda e: e.tensor_mul(
                    attnb[:, :], attnb[:, :], tmask[:, :]), deps=[tAt1])

                tk_vg = []
                for sh in range(2):
                    for b2i in range(2):
                        for t2 in range(2):
                            tk_vg.append(s.dma("gpsimd", lambda e, p=p, sh=sh,
                                               b2i=b2i, t2=t2, vb=vbuf:
                                e.dma_start(
                                    out=AP(v_tg,
                                           (b2i * 64 + sh * 32 + 16 * t2) * 1024,
                                           [[1024, 16], [64, 16], [1, 64]]),
                                    in_=AP(vscr, vb + ((2 * p + sh) * 64 +
                                                       b2i * 32 + t2) * 1024,
                                           [[64, 16], [2048, 16], [1, 64]])),
                                deps=[ST[st]["vdump"], tk_mm2_last]))

                for sh in range(2):
                    def mm_2(te, sh=sh):
                        last = None
                        for b2i in range(2):
                            for f in range(16):
                                last = te.matmul(
                                    AP(pOT, (b2i * 64) * 512 + f * 32,
                                       [[512, 64], [1, 32]]),
                                    AP(v_tg, (b2i * 64 + sh * 32) * 1024 + f * 64,
                                       [[1024, 32], [1, 64]]),
                                    AP(attnb, (b2i * 64 + sh * 32) * 512 + f * 32,
                                       [[512, 32], [1, 32]]),
                                    start=True, stop=True,
                                    tile_position=(b2i * 64 + sh * 32, b2i * 64))
                        return last
                    godt4 = tk_odT[-4:] if tk_odT else (
                        [] if st == 0 else ST[st - 1]["odt4"])
                    tk_mm2_last = s.op("tensor", mm_2,
                                       deps=[tAt] + tk_vg + godt4 +
                                       (mm2_extra if (p == 0 and sh == 0) else []))
                    SB = 2 * p + sh
                    for b2i in range(2):
                        for hp in range(2):
                            tk_odT.append(s.op("vector",
                                lambda e, SB=SB, b2i=b2i, hp=hp: e.tensor_copy(
                                    AP(o_dT, (hp * 64) * (8 * T_ST) + SB * 64 + b2i * 32,
                                       [[8 * T_ST, 64], [2, 16], [1, 2], [T_ST, 8]]),
                                    AP(pOT, (b2i * 64) * 512 + hp,
                                       [[512, 64], [32, 16], [16, 2], [2, 8]])),
                                deps=[tk_mm2_last]))

                prevband.update({
                    ("rn", 0): tAcc[0],
                    ("rn", 1): tAcc[1],
                    ("B", p): [tAt0],
                    ("C", p): [tAt1],
                    "A": [tDm],
                    "F": [tAt0],
                    "P": [tSel, tk_mm2_last], "S": [tRec], "At": [tk_mm2_last],
                })
            ST[st]["mm_ab_last"] = ab[NSB - 1]
            ST[st]["mm2_last"] = tk_mm2_last
            ST[st]["odT"] = tk_odT
            ST[st]["odt4"] = tk_odT[-4:]
            ST[st]["vg_last"] = tk_vg[-1]
            ST[st]["prevband"] = prevband

        def pP5(st):
            xr = xres[st % 2]
            tk_afm = []
            tk_atm = []
            tmo = None
            hg2 = [] if st == 0 else ST[st - 1]["hg_last2"]
            for oc in range(8):
                pb_ao = pQK[oc % 2]

                def mm_o(te, oc=oc, pb_ao=pb_ao):
                    last = None
                    for h2 in range(8):
                        last = te.matmul(
                            pb_ao[:, :],
                            wout_sb[:, h2, oc * 128:oc * 128 + 128],
                            o_dT[:, h2, :],
                            start=(h2 == 0), stop=(h2 == 7))
                    return last
                dmo = ST[st]["odT"] + hg2 if oc < 2 else [tk_afm[oc - 2]]
                tmo = s.op("tensor", mm_o, deps=dmo)
                dslot = tk_atm[oc - 2:oc - 1] if oc >= 2 else \
                    ([] if st == 0 else [ST[st - 1]["ytm_last2"][oc]])
                tk_afm.append(s.op("scalar", lambda e, oc=oc, pb_ao=pb_ao:
                    e.activation(
                        afm[:, oc % 2, :], pb_ao[:, :], AF.Identity,
                        bias=bout_sb[:, oc:oc + 1], scale=1.0),
                    deps=[tmo] + dslot))
                gres = [] if (oc >= 2 or st == 0) else [ST[st - 1]["res2"][0],
                                                        ST[st - 1]["res2"][1]]
                tk_atm.append(s.dma("sync", lambda e, oc=oc:
                    e.dma_start_transpose(
                        AP(a_tm, oc * 128, [[PT * D, 128], [D, 2], [1, 128]]),
                        AP(afm, (oc % 2) * T_ST, [[2 * T_ST, 128], [128, 2], [1, 128]])),
                    deps=[tk_afm[oc]] + gres))
            tk_res1 = []
            for pt in range(PT):
                tk_res1.append(s.op("vector", lambda e, pt=pt, xr=xr:
                    e.tensor_add(
                        xr[:, pt, :], xr[:, pt, :], a_tm[:, pt, :]),
                    deps=tk_atm))
            ST[st]["mm_o_last"] = tmo
            ST[st]["res1"] = tk_res1

        def pP67(st):
            xr = xres[st % 2]
            st6x, sq4x, mv2x, acolx, bcolx = scrA
            tk_xhatT = []
            for pt in range(PT):
                c0 = 8 * pt

                def col(i, c0=c0):
                    return mv2x[:, c0 + i:c0 + i + 1]
                src = xr[:, pt, :]
                t1 = s.op("vector", lambda e, src=src: e.bn_stats(
                    st6x[:, 0:6], src[:, 0:512]), deps=[ST[st]["res1"][pt]])
                t1 = s.op("vector", lambda e, src=src: e.bn_stats(
                    st6x[:, 6:12], src[:, 512:1024]), deps=[t1])
                t1 = s.op("vector", lambda e, col=col: e.tensor_reduce(
                    col(0), AP(st6x, 1, [[12, 128], [3, 4]]), AX.X, ALU.add),
                    deps=[t1])
                t1 = s.op("vector", lambda e: e.tensor_mul(
                    sq4x[:, 0:4],
                    AP(st6x, 1, [[12, 128], [3, 4]]),
                    AP(st6x, 1, [[12, 128], [3, 4]])), deps=[t1])
                t1 = s.op("vector", lambda e, col=col: e.tensor_reduce(
                    col(1), sq4x[:, 0:4], AX.X, ALU.add), deps=[t1])
                t1 = s.op("vector", lambda e, col=col: e.tensor_reduce(
                    col(2), AP(st6x, 2, [[12, 128], [3, 4]]), AX.X, ALU.add),
                    deps=[t1])
                t1 = s.op("vector", lambda e, col=col: e.tensor_scalar_mul(
                    col(3), col(2), 1.0 / 1024.0), deps=[t1])
                t1 = s.op("vector", lambda e, col=col: e.scalar_tensor_tensor(
                    col(3), col(1), 0.25, col(3), ALU.mult, ALU.add), deps=[t1])
                t1 = s.op("vector", lambda e, col=col: e.scalar_tensor_tensor(
                    col(4), col(0), -0.0625, col(0), ALU.mult, ALU.mult),
                    deps=[t1])
                t1 = s.op("vector", lambda e, col=col: e.tensor_add(
                    col(5), col(3), col(4)), deps=[t1])
                t3 = s.op("scalar", lambda e, col=col: e.activation(
                    acolx[:, :], col(5), AF.Sqrt, bias=cc(C_EPS5),
                    scale=1.0), deps=[t1])
                t3 = s.op("vector", lambda e: e.reciprocal(
                    acolx[:, :], acolx[:, :]), deps=[t3])
                t3 = s.op("vector", lambda e, col=col: e.tensor_mul(
                    col(6), col(5), acolx[:, :]), deps=[t3])
                t3 = s.op("vector", lambda e, col=col: e.tensor_mul(
                    col(6), col(6), acolx[:, :]), deps=[t3])
                t3 = s.op("scalar", lambda e, col=col: e.activation(
                    col(7), col(6), AF.Sqrt, bias=cc(C_EPS5), scale=1.0),
                    deps=[t3])
                t3 = s.op("vector", lambda e, col=col: e.reciprocal(
                    col(7), col(7)), deps=[t3])
                t3 = s.op("vector", lambda e, col=col: e.tensor_mul(
                    acolx[:, :], acolx[:, :], col(7)), deps=[t3])
                t4 = s.op("vector", lambda e, col=col: e.scalar_tensor_tensor(
                    bcolx[:, :], col(0), -0.25, acolx[:, :],
                    ALU.mult, ALU.mult), deps=[t3])
                u5 = s.op("scalar", lambda e, pt=pt, src=src: e.activation(
                    a_tm[:, pt, :], src, AF.Identity, bias=bcolx[:, :],
                    scale=acolx[:, :]), deps=[t4])
                tk_xhatT.append(s.dma("scalar", lambda e, pt=pt:
                    e.dma_start_transpose(
                        AP(xhatT, pt * 128, [[8 * T_ST, 128], [T_ST, 8], [1, 128]]),
                        AP(a_tm, pt * D, [[PT * D, 128], [128, 8], [1, 128]])),
                    deps=[u5]))

            def mm_g(te):
                last = None
                for kc in range(8):
                    last = te.matmul(pS[0:8, 0:T_ST], wg_sb[:, kc, :],
                                     xhatT[:, kc, :], start=(kc == 0), stop=(kc == 7))
                return last
            tg = s.op("tensor", mm_g, deps=tk_xhatT)
            g0 = s.op("vector", lambda e: e.memset(lgp, -1e30), deps=[tg])
            g1 = s.op("vector", lambda e: e.tensor_copy(
                lgp[0:8, :], pS[0:8, 0:T_ST]), deps=[tg, g0])
            g2 = s.op("vector", lambda e: e.transpose(lgT, lgp), deps=[g1])
            g3 = s.op("vector", lambda e: e.reduce_max(
                mx8[:, :], AP(bankB, 0, [[512, 32], [32, 8], [1, 8]]), AX.X),
                deps=[g2])
            g4 = s.op("vector", lambda e: e.tensor_tensor(
                AP(bankC, 0, [[512, 32], [32, 8], [1, 8]]),
                AP(bankB, 0, [[512, 32], [32, 8], [1, 8]]),
                mx8[:, :, None].broadcast_to([32, 8, 8]), ALU.is_ge), deps=[g3])
            g5 = s.op("vector", lambda e: e.transpose(ohp, ohT), deps=[g4])
            g6 = s.op("vector", lambda e: e.tensor_copy(
                oh_em[:, :], AP(bankB, 32 * 512, [[512, 8], [1, T_ST]])),
                deps=[g5])
            pb_ = ST[st]["prevband"]
            pb_["A"] = pb_.get("A", []) + [g2]
            pb_[("B", 0)] = pb_.get(("B", 0), []) + [g6]
            pb_[("C", 0)] = pb_.get(("C", 0), []) + [g5]
            pb_["S"] = pb_.get("S", []) + [g1]
            ST[st]["xhatT"] = tk_xhatT
            ST[st]["g6"] = g6
            ST[st]["tg"] = tg

        def pP8(st, has_next):
            xr = xres[st % 2]
            tk_hm = []
            tk_hg = []
            pMs = [pOT, pS]
            for hc in range(16):
                e_ = hc // 2
                pb_h = pQK[hc % 2]
                pb_m = pMs[hc % 2]

                def mm_h(te, hc=hc, e_=e_, pb_h=pb_h, pb_m=pb_m):
                    for kc in range(8):
                        te.matmul(pb_h[:, :],
                                  w1_sb[:, kc, e_, (hc % 2) * 128:(hc % 2) * 128 + 128],
                                  xhatT[:, kc, :], start=(kc == 0), stop=(kc == 7))
                    return te.matmul(pb_m[:, 0:T_ST], sel8[:, e_, :],
                                     oh_em[:, :], start=True, stop=True)
                if hc < 2:
                    dh = ST[st]["xhatT"] + [ST[st]["g6"]]
                else:
                    dh = [tk_hm[hc - 2], tk_hg[hc - 2]]
                tmh = s.op("tensor", mm_h, deps=dh)
                th1 = s.op("scalar", lambda e, hc=hc, pb_h=pb_h: e.activation(
                    AP(v_tg, 0, [[1024, 128], [1, T_ST]]), pb_h[:, :], AF.Gelu,
                    bias=b1c_sb[:, hc:hc + 1], scale=1.0), deps=[tmh] +
                    ([tk_hm[hc - 1]] if hc >= 1 else []))
                th2 = s.op("vector", lambda e, hc=hc, pb_m=pb_m: e.tensor_mul(
                    hm_sb[:, hc, :], AP(v_tg, 0, [[1024, 128], [1, T_ST]]),
                    pb_m[:, 0:T_ST]),
                    deps=[th1])
                tk_hg.append(th1)
                tk_hm.append(th2)
            pb_ = ST[st]["prevband"]
            pb_["S"] = pb_.get("S", []) + [tk_hm[15]]
            ST[st]["hm14"] = tk_hm[14]
            ST[st]["hm15"] = tk_hm[15]

            tk_yfm = []
            tk_ytm = []
            tmy = None
            for oc in range(8):
                py_sl = (oc % 2) * 512

                def mm_y(te, oc=oc, py_sl=py_sl):
                    for hc in range(16):
                        te.matmul(pV[:, py_sl:py_sl + T_ST],
                                  w2_sb[:, hc, oc * 128:oc * 128 + 128],
                                  hm_sb[:, hc, :], start=(hc == 0), stop=False)
                    return te.matmul(pV[:, py_sl:py_sl + T_ST],
                                     b2_sb[0:NE, oc * 128:oc * 128 + 128],
                                     oh_em[:, :], start=False, stop=True)
                if oc < 2:
                    dy = tk_hm + [ST[st]["g6"]]
                    if has_next:
                        dy = dy + (ST[st + 1]["evac_eA"] if oc == 0
                                   else ST[st + 1]["evac_eB"])
                else:
                    dy = [tk_yfm[oc - 2]]
                tmy = s.op("tensor", mm_y, deps=dy)
                dslot = tk_ytm[oc - 2:oc - 1] if oc >= 2 else []
                tk_yfm.append(s.op("scalar", lambda e, oc=oc, py_sl=py_sl:
                    e.activation(
                        afm[:, oc % 2, :], pV[:, py_sl:py_sl + T_ST],
                        AF.Identity, bias=0.0, scale=1.0),
                    deps=[tmy] + dslot))
                tk_ytm.append(s.dma("scalar", lambda e, oc=oc:
                    e.dma_start_transpose(
                        AP(a_tm, oc * 128, [[PT * D, 128], [D, 2], [1, 128]]),
                        AP(afm, (oc % 2) * T_ST, [[2 * T_ST, 128], [128, 2], [1, 128]])),
                    deps=[tk_yfm[oc]]))
            tk_res2 = []
            for pt in range(PT):
                tk_res2.append(s.op("vector", lambda e, pt=pt, xr=xr:
                    e.tensor_add(
                        xr[:, pt, :], xr[:, pt, :], a_tm[:, pt, :]),
                    deps=tk_ytm))
            ST[st]["hg_last2"] = tk_hg[-2:]
            ST[st]["mm_y_last"] = tmy
            ST[st]["yfm"] = tk_yfm
            ST[st]["ytm_last2"] = {0: tk_ytm[-2], 1: tk_ytm[-1]}
            ST[st]["res2"] = tk_res2

        def pP9(st):
            xr = xres[st % 2]
            ST[st]["store"] = s.dma("gpsimd", lambda e, o=st * T_ST, xr=xr:
                e.dma_start(
                    out=y_out.ap()[o:o + T_ST, :]
                        .rearrange("(pt p) d -> p pt d", p=128),
                    in_=xr.ap()), deps=ST[st]["res2"])

        pP0(0); pP1(0); pP2(0); pP2b(0); pP3(0)
        for st in range(n_st):
            has_next = st + 1 < n_st
            if has_next:
                pP0(st + 1)
            pP4(st)
            pP5(st)
            if has_next:
                pP1(st + 1)
                pP2(st + 1)
            pP67(st)
            if has_next:
                pP3(st + 1)
            pP8(st, has_next)
            if has_next:
                pP2b(st + 1)
            pP9(st)

        tk_store_last = ST[n_st - 1]["store"]
        if debug:
            x_res = xres[(n_st - 1) % 2]
            dbg_map = {"xlnT": xlnT2, "qmov": qmov, "kstat": kstat,
                       "bankA": bankA, "bankF": bankF, "ptil": ptil,
                       "attnb": attnb, "o_dT": o_dT, "a_tm": a_tm,
                       "oh_em": oh_em, "hm_sb": hm_sb,
                       "x_res": x_res, "v_tg": v_tg, "smalls": smalls,
                       "hmT": hmT}
            last = ST[n_st - 1]
            for nm in dbg_outs:
                t_ = dbg_map[nm]
                tk_store_last = s.dma("gpsimd", lambda e, nm=nm, t_=t_:
                    e.dma_start(out=dbg_outs[nm].ap(),
                                in_=AP(t_, 0, [[int(np.prod(t_.shape[1:])),
                                                t_.shape[0]],
                                               [1, int(np.prod(t_.shape[1:]))]])),
                    deps=[tk_store_last, last["mm2_last"], last["res2"][0],
                          last["res2"][1]])

        s.q["gpsimd"].append(("w", s._deps([tk_store_last]), s.cnt["gpsimd"]))

        incmap = {}
        for name in ENGS:
            nincs = 0
            m = {}
            for item in s.q[name]:
                if item[0] == "i":
                    if (name, item[2]) in s.used:
                        nincs += 1
                    m[item[2]] = nincs
            incmap[name] = m

        def emitter(name):
            def run(eng):
                seen = {}
                m_self = incmap[name]
                for item in s.q[name]:
                    if item[0] == "w":
                        emitted = item[2]
                        for (de, dn) in item[1]:
                            if de.endswith("_dma"):
                                val = dn
                            else:
                                if de == name and dn <= emitted - 16:
                                    continue
                                val = incmap[de][dn]
                            if seen.get(de, -1) >= val:
                                continue
                            seen[de] = val
                            eng.wait_ge(SEM[de], val)
                    elif item[0] == "i":
                        inst = item[1](eng)
                        if (name, item[2]) in s.used:
                            inst.then_inc(SEM[name], 1)
                    else:
                        item[1](eng).then_inc(SEM[name + "_dma"], 16)
            return run

        @block.tensor
        def _(e):
            emitter("tensor")(e)

        @block.vector
        def _(e):
            emitter("vector")(e)

        @block.scalar
        def _(e):
            emitter("scalar")(e)

        @block.sync
        def _(e):
            emitter("sync")(e)

        @block.gpsimd
        def _(e):
            emitter("gpsimd")(e)

    return nc


_NC_CACHE = {}


def kernel(L0_attn_x, L0_attn_W_qkv, L0_attn_b_qkv, L0_attn_W_out, L0_attn_b_out,
           L0_ffn_Wg, L0_ffn_ln_scale, L0_ffn_ln_bias, L0_ffn_W1, L0_ffn_b1,
           L0_ffn_W2, L0_ffn_b2):
    x = np.ascontiguousarray(np.asarray(L0_attn_x, dtype=np.float32)
                             .reshape(-1, D))
    n_tokens = x.shape[0]
    per = n_tokens // N_CORES

    key = per
    if key not in _NC_CACHE:
        _NC_CACHE[key] = build_program(per)
    nc = _NC_CACHE[key]

    tmaskc = np.zeros((128, 512), np.float32)
    for b2i in range(2):
        for sh in range(2):
            for t2 in range(2):
                r = b2i * 64 + sh * 32 + t2 * 16
                for f in range(16):
                    tmaskc[r:r + 16,
                           f * 32 + t2 * 16:f * 32 + t2 * 16 + 16] = 1.0
    sel2c = np.zeros((128, 128), np.float32)
    for r0 in range(0, 128, 16):
        sel2c[r0:r0 + 16, r0:r0 + 16] = 1.0
    sel8c = np.zeros((NE, NE, 128), np.float32)
    for e_ in range(NE):
        sel8c[e_, e_, :] = 1.0
    onesc = np.ones((4096,), np.float32)

    w = {
        "tmaskc": tmaskc, "sel2c": sel2c, "sel8c": sel8c, "onesc": onesc,
        "wqkv": np.ascontiguousarray(np.asarray(L0_attn_W_qkv, np.float32)),
        "bqkv": np.ascontiguousarray(np.asarray(L0_attn_b_qkv, np.float32)),
        "wout": np.ascontiguousarray(np.asarray(L0_attn_W_out, np.float32)),
        "bout": np.ascontiguousarray(np.asarray(L0_attn_b_out, np.float32)),
        "wg": np.ascontiguousarray(np.asarray(L0_ffn_Wg, np.float32)),
        "lns": np.ascontiguousarray(np.asarray(L0_ffn_ln_scale, np.float32)),
        "lnb": np.ascontiguousarray(np.asarray(L0_ffn_ln_bias, np.float32)),
        "w1": np.ascontiguousarray(np.asarray(L0_ffn_W1, np.float32)),
        "b1": np.ascontiguousarray(np.asarray(L0_ffn_b1, np.float32)),
        "w2": np.ascontiguousarray(np.asarray(L0_ffn_W2, np.float32)),
        "b2": np.ascontiguousarray(np.asarray(L0_ffn_b2, np.float32)),
    }
    in_maps = []
    for c in range(N_CORES):
        m = {"x": x[c * per:(c + 1) * per]}
        m.update(w)
        in_maps.append(m)

    res = run_bass_kernel_spmd(nc, in_maps, core_ids=list(range(N_CORES)))
    out = np.concatenate([r["y"] for r in res.results], axis=0)
    return out.reshape(np.asarray(L0_attn_x).shape).astype(np.float32)


# version 16
# speedup vs baseline: 1.1592x; 1.1592x over previous
import numpy as np
import concourse.bass as bass
import concourse.mybir as mybir
from concourse.bass_utils import run_bass_kernel_spmd

F32 = mybir.dt.float32
BF16 = mybir.dt.bfloat16
AF = mybir.ActivationFunctionType
ALU = mybir.AluOpType
AX = mybir.AxisListType
AP = bass.AP

N_CORES = 8
D = 1024
NH, HD, NE, EH = 16, 64, 8, 256
T_ST = 256
PT = 2
NSB = 4

ENGS = ("tensor", "vector", "scalar", "sync", "gpsimd")


class Sched:

    def __init__(self):
        self.q = {e: [] for e in ENGS}
        self.cnt = {e: 0 for e in ENGS}
        self.dcnt = {e: 0 for e in ENGS}
        self.used = set()

    def _deps(self, deps):
        dd = tuple(d for d in deps if d is not None)
        for d in dd:
            self.used.add(d)
        return dd

    def op(self, eng, fn, deps=()):
        self.q[eng].append(("w", self._deps(deps), self.cnt[eng]))
        self.q[eng].append(("i", fn, self.cnt[eng] + 1))
        self.cnt[eng] += 1
        return (eng, self.cnt[eng])

    def dma(self, eng, fn, deps=()):
        self.q[eng].append(("w", self._deps(deps), self.cnt[eng]))
        self.q[eng].append(("d", fn))
        self.dcnt[eng] += 1
        return (eng + "_dma", 16 * self.dcnt[eng])


def build_program(n_tokens, debug=False):
    n_st = n_tokens // T_ST
    nc = bass.Bass("TRN2", target_bir_lowering=False,
                   detect_race_conditions=False)

    x_in = nc.dram_tensor("x", [n_tokens, D], F32, kind="ExternalInput")
    wqkv_d = nc.dram_tensor("wqkv", [D, 3 * D], F32, kind="ExternalInput")
    bqkv_d = nc.dram_tensor("bqkv", [3 * D], F32, kind="ExternalInput")
    wout_d = nc.dram_tensor("wout", [D, D], F32, kind="ExternalInput")
    bout_d = nc.dram_tensor("bout", [D], F32, kind="ExternalInput")
    wg_d = nc.dram_tensor("wg", [NE, D], F32, kind="ExternalInput")
    lns_d = nc.dram_tensor("lns", [NE, D], F32, kind="ExternalInput")
    lnb_d = nc.dram_tensor("lnb", [NE, D], F32, kind="ExternalInput")
    w1_d = nc.dram_tensor("w1", [NE, D, EH], F32, kind="ExternalInput")
    b1_d = nc.dram_tensor("b1", [NE, EH], F32, kind="ExternalInput")
    w2_d = nc.dram_tensor("w2", [NE, EH, D], F32, kind="ExternalInput")
    b2_d = nc.dram_tensor("b2", [NE, D], F32, kind="ExternalInput")
    tmask_d = nc.dram_tensor("tmaskc", [128, 512], F32, kind="ExternalInput")
    sel2_d = nc.dram_tensor("sel2c", [128, 128], F32, kind="ExternalInput")
    sel8_d = nc.dram_tensor("sel8c", [NE, NE, 128], F32, kind="ExternalInput")
    ones_d = nc.dram_tensor("onesc", [4096], F32, kind="ExternalInput")
    y_out = nc.dram_tensor("y", [n_tokens, D], F32, kind="ExternalOutput")
    vscr = nc.dram_tensor("vscr", [2 * T_ST * D], BF16, kind="Internal")
    hscr = nc.dram_tensor("hscr", [4, 128 * 64], BF16, kind="Internal")
    dbg_specs = [
        ("xlnT", [128, 8 * T_ST], BF16), ("qmov", [67, 32 * 128], BF16),
        ("kstat", [67, 64 * 128], BF16), ("bankA", [128, 512], F32),
        ("bankF", [128, 512], F32), ("ptil", [128, 512], BF16),
        ("attnb", [128, 512], BF16), ("o_dT", [128, 8 * T_ST], BF16),
        ("a_tm", [128, PT * D], BF16),         ("oh_em", [NE, T_ST], BF16), ("hm_sb", [128, 16 * T_ST], BF16),
        ("x_res", [128, PT * D], BF16),
        ("smalls", [128, 64], BF16), ("hmT", [128, 64], BF16),
    ]
    dbg_outs = {}
    if debug:
        for nm, shp, dt_ in dbg_specs:
            dbg_outs[nm] = nc.dram_tensor("dbg_" + nm, shp, dt_,
                                          kind="ExternalOutput")

    s = Sched()

    from contextlib import ExitStack
    with ExitStack() as ctx, nc.Block() as block:
        wqkv_sb = ctx.enter_context(nc.sbuf_tensor("wqkv_sb", [128, 8, 3 * D], BF16))
        wout_sb = ctx.enter_context(nc.sbuf_tensor("wout_sb", [128, 8, D], BF16))
        w1_sb = ctx.enter_context(nc.sbuf_tensor("w1_sb", [128, 8, NE, EH], BF16))
        w2_sb = ctx.enter_context(nc.sbuf_tensor("w2_sb", [128, 16, D], BF16))
        wg_sb = ctx.enter_context(nc.sbuf_tensor("wg_sb", [128, 8, NE], BF16))
        lns_sb = ctx.enter_context(nc.sbuf_tensor("lns_sb", [128, 8, NE], BF16))
        lnb_sb = ctx.enter_context(nc.sbuf_tensor("lnb_sb", [128, 8, NE], BF16))
        b1r_sb = ctx.enter_context(nc.sbuf_tensor("b1r_sb", [128, 16], F32))
        b1c_sb = ctx.enter_context(nc.sbuf_tensor("b1c_sb", [128, 16], F32))
        b2_sb = ctx.enter_context(nc.sbuf_tensor("b2_sb", [33, D], BF16))
        bqm2_sb = ctx.enter_context(nc.sbuf_tensor("bqm2_sb", [64, 16], F32))
        bk_lo = ctx.enter_context(nc.sbuf_tensor("bk_lo", [64, 16], F32))
        bout_sb = ctx.enter_context(nc.sbuf_tensor("bout_sb", [128, 8], F32))
        ones_col = ctx.enter_context(nc.sbuf_tensor("ones_col", [33, 128], BF16))
        sel8 = ctx.enter_context(nc.sbuf_tensor("sel8", [NE, NE, 128], BF16))
        SEL2 = ctx.enter_context(nc.sbuf_tensor("SEL2", [128, 128], BF16))
        tmask = ctx.enter_context(nc.sbuf_tensor("tmask", [128, 512], BF16))
        consts = ctx.enter_context(nc.sbuf_tensor("consts", [128, 12], F32))
        x_res0 = ctx.enter_context(nc.sbuf_tensor("x_res0", [128, PT, D], BF16))
        x_res1 = ctx.enter_context(nc.sbuf_tensor("x_res1", [128, PT, D], BF16))
        xlnT2 = ctx.enter_context(nc.sbuf_tensor("xlnT2", [128, 8, T_ST], BF16))
        st6b = ctx.enter_context(nc.sbuf_tensor("st6b", [128, 12], F32))
        mv2b = ctx.enter_context(nc.sbuf_tensor("mv2b", [128, 16], F32))
        acolb = ctx.enter_context(nc.sbuf_tensor("acolb", [128, 1], F32))
        bcolb = ctx.enter_context(nc.sbuf_tensor("bcolb", [128, 1], F32))
        sq4b = ctx.enter_context(nc.sbuf_tensor("sq4b", [128, 4], F32))
        sq4p = ctx.enter_context(nc.sbuf_tensor("sq4p", [128, 16], F32))
        kstat = ctx.enter_context(nc.sbuf_tensor("kstat", [67, 64, 128], BF16))
        qmov = ctx.enter_context(nc.sbuf_tensor("qmov", [67, 32, 128], BF16))
        v_tg = ctx.enter_context(nc.sbuf_tensor("v_tg", [128, 16, 64], BF16))
        sq4 = ctx.enter_context(nc.sbuf_tensor("sq4", [128, 16], F32))
        smalls = ctx.enter_context(nc.sbuf_tensor("smalls", [128, 64], BF16))
        hmT = ctx.enter_context(nc.sbuf_tensor("hmT", [128, 64], BF16))
        bankA = ctx.enter_context(nc.sbuf_tensor("bankA", [128, 512], F32))
        bankD = ctx.enter_context(nc.sbuf_tensor("bankD", [128, 512], F32))
        bankE = ctx.enter_context(nc.sbuf_tensor("bankE", [128, 512], F32))
        bankB = ctx.enter_context(nc.sbuf_tensor("bankB", [128, 512], F32))
        bankC = ctx.enter_context(nc.sbuf_tensor("bankC", [128, 512], F32))
        bankF = ctx.enter_context(nc.sbuf_tensor("bankF", [128, 512], F32))
        ptil = ctx.enter_context(nc.sbuf_tensor("ptil", [128, 512], BF16))
        attnb = ptil
        o_dT = ctx.enter_context(nc.sbuf_tensor("o_dT", [128, 8, T_ST], BF16))
        afm = ctx.enter_context(nc.sbuf_tensor("afm", [128, 2, T_ST], BF16))
        a_tm = ctx.enter_context(nc.sbuf_tensor("a_tm", [128, PT, D], BF16))
        xhatT = ctx.enter_context(nc.sbuf_tensor("xhatT", [128, 8, T_ST], BF16))
        st6 = ctx.enter_context(nc.sbuf_tensor("st6", [128, 12], F32))
        mv2 = ctx.enter_context(nc.sbuf_tensor("mv2", [128, 16], F32))
        acol = ctx.enter_context(nc.sbuf_tensor("acol", [128, 1], F32))
        bcol = ctx.enter_context(nc.sbuf_tensor("bcol", [128, 1], F32))
        mx8 = ctx.enter_context(nc.sbuf_tensor("mx8", [32, 8], F32))
        oh_em = ctx.enter_context(nc.sbuf_tensor("oh_em", [NE, T_ST], BF16))
        hm_sb = ctx.enter_context(nc.sbuf_tensor("hm_sb", [128, 16, T_ST], BF16))
        lgp = bankA[0:32, 0:T_ST]
        lgT = bankB[0:32, 0:T_ST]
        ohT = bankC[0:32, 0:T_ST]
        ohp = bankB[32:64, 0:T_ST]
        pQK0 = ctx.enter_context(nc.psum_tensor("pQK0", [128, T_ST], F32))
        pQK1 = ctx.enter_context(nc.psum_tensor("pQK1", [128, T_ST], F32))
        pV = ctx.enter_context(nc.psum_tensor("pV", [128, D], F32))
        pRN = ctx.enter_context(nc.psum_tensor("pRN", [128, 512], F32))
        pRN2 = ctx.enter_context(nc.psum_tensor("pRN2", [128, 512], F32))
        pS = ctx.enter_context(nc.psum_tensor("pS", [128, 512], F32))
        pOT = ctx.enter_context(nc.psum_tensor("pOT", [128, 512], F32))
        sm_t = ctx.enter_context(nc.semaphore("sm_t"))
        sm_v = ctx.enter_context(nc.semaphore("sm_v"))
        sm_s = ctx.enter_context(nc.semaphore("sm_s"))
        sm_sy = ctx.enter_context(nc.semaphore("sm_sy"))
        sm_g = ctx.enter_context(nc.semaphore("sm_g"))
        dm_sy = ctx.enter_context(nc.semaphore("dm_sy"))
        dm_g = ctx.enter_context(nc.semaphore("dm_g"))
        dm_sc = ctx.enter_context(nc.semaphore("dm_sc"))

        SEM = {"tensor": sm_t, "vector": sm_v, "scalar": sm_s,
               "sync": sm_sy, "gpsimd": sm_g,
               "sync_dma": dm_sy, "gpsimd_dma": dm_g, "scalar_dma": dm_sc}

        (C_EPS5, C_EPS6, C_PI2, C_M10, C_TINY, C_2, C_Q25, C_M2,
         C_M1, C_MQ25, C_HALF) = range(11)

        def cc(i, n=128):
            return consts[0:n, i:i + 1]

        tk_w = []

        def gld(dst_ap, src_ap):
            tk_w.append(s.dma("gpsimd",
                              lambda e, d=dst_ap, sr=src_ap: e.dma_start(
                                  out=d, in_=sr, allow_slow_non_contiguous=True)))

        for h in range(NH):
            for sec in range(2):
                gld(wqkv_sb[:, :, h * 128 + sec * 64:h * 128 + sec * 64 + 64],
                    AP(wqkv_d, sec * 1024 + h * 64,
                       [[3 * D, 128], [128 * 3 * D, 8], [1, 64]]))
        gld(wqkv_sb[:, :, 2 * D:3 * D],
            AP(wqkv_d, 2 * D, [[3 * D, 128], [128 * 3 * D, 8], [1, D]]))
        gld(wout_sb.ap(), wout_d.ap().rearrange("(kc p) o -> p kc o", p=128))
        for e_ in range(NE):
            gld(w1_sb[:, :, e_, :],
                w1_d.ap()[e_].rearrange("(kc p) hh -> p kc hh", p=128))
        gld(w2_sb.ap(), w2_d.ap().rearrange("e (hc p) o -> p (e hc) o", p=128))
        for e_ in range(NE):
            gld(wg_sb[:, :, e_], wg_d.ap()[e_].rearrange("(kc p) -> p kc", p=128))
            gld(lns_sb[:, :, e_], lns_d.ap()[e_].rearrange("(kc p) -> p kc", p=128))
            gld(lnb_sb[:, :, e_], lnb_d.ap()[e_].rearrange("(kc p) -> p kc", p=128))
        gld(b1r_sb.ap(), b1_d.ap().rearrange("e (hc p) -> p (e hc)", p=128))
        gld(b2_sb[0:NE, :], b2_d.ap())
        gld(bqm2_sb.ap(), bqkv_d.ap()[0:D].rearrange("(h d) -> d h", d=64))
        gld(bk_lo.ap(), bqkv_d.ap()[D:2 * D].rearrange("(h d) -> d h", d=64))
        gld(b2_sb[32:33, :], bqkv_d.ap()[2 * D:3 * D][None, :])
        gld(bout_sb.ap(), bout_d.ap().rearrange("(oc p) -> p oc", p=128))

        tk_ms = []

        def ms(ap_, val):
            tk_ms.append(s.op("vector", lambda e, a=ap_, v=val: e.memset(a, v)))

        ms(cc(C_EPS5), 1e-5)
        ms(cc(C_EPS6), 1e-6)
        ms(cc(C_PI2), float(np.pi / 2))
        ms(cc(C_M10), -10.0)
        ms(cc(C_TINY), 1e-9)
        ms(cc(C_2), 2.0)
        ms(cc(C_Q25), 0.25)
        ms(cc(C_M2), -2.0)
        ms(cc(C_M1), -1.0)
        ms(cc(C_MQ25), -0.25)
        ms(cc(C_HALF), 0.5)
        ms(ones_col[32:33, :], 1.0)
        gld(sel8.ap(), sel8_d.ap())
        gld(SEL2.ap(), sel2_d.ap())
        gld(tmask.ap(), tmask_d.ap())
        ms(kstat.ap(), 0.0)
        tk_w.append(s.dma("gpsimd", lambda e: e.dma_start(
            out=kstat[66:67, 0:32, :],
            in_=ones_d.ap().rearrange("(a c b) -> a c b", a=1, c=32))))
        ms(qmov.ap(), 0.0)
        ms(qmov[64:65, :, :], 1.0)
        ms(bankF.ap(), 0.0)

        def mk_b1c(te):
            last = None
            for hc in range(16):
                for kc in range(8):
                    last = te.matmul(
                        pQK0[:, hc:hc + 1],
                        w1_sb[:, kc, hc // 2, (hc % 2) * 128:(hc % 2) * 128 + 128],
                        lnb_sb[:, kc, hc // 2:hc // 2 + 1],
                        start=(kc == 0), stop=(kc == 7))
            return last
        tk_b1cm = s.op("tensor", mk_b1c, deps=tk_w)
        tk_b1c = s.op("vector", lambda e: e.tensor_add(
            b1c_sb[:, :], pQK0[:, 0:16], b1r_sb[:, :]), deps=[tk_b1cm])
        tk_w1f = s.op("vector", lambda e: e.tensor_mul(
            w1_sb.ap(), w1_sb.ap(),
            lns_sb[:, :, :, None].broadcast_to([128, 8, NE, EH])),
            deps=[tk_b1cm])

        setup_done = tk_w + tk_ms + [tk_b1c, tk_w1f]

        pQK = [pQK0, pQK1]
        pRNs = [pRN, pRN2]
        xres = [x_res0, x_res1]
        PVOFF = [0, 512]
        hsc_guard = {}
        last_tdmp = [None]
        ST = [dict() for _ in range(n_st)]

        scrA = (st6, sq4, mv2, acol, bcol)
        scrB = (st6b, sq4b, mv2b, acolb, bcolb)

        def layer_norm(scr, slot, src, dst, dep0, extra=()):
            st6x, sq4x, mv2x, acolx, bcolx = scr
            c0 = 8 * slot

            def col(i):
                return mv2x[:, c0 + i:c0 + i + 1]
            t1 = s.op("vector", lambda e: e.bn_stats(st6x[:, 0:6], src[:, 0:512]),
                      deps=dep0)
            t1 = s.op("vector", lambda e: e.bn_stats(st6x[:, 6:12], src[:, 512:1024]),
                      deps=[t1])
            t1 = s.op("vector", lambda e: e.tensor_reduce(
                col(0), AP(st6x, 1, [[12, 128], [3, 4]]), AX.X, ALU.add),
                deps=[t1])
            t1 = s.op("vector", lambda e: e.tensor_mul(
                sq4x[:, 0:4],
                AP(st6x, 1, [[12, 128], [3, 4]]),
                AP(st6x, 1, [[12, 128], [3, 4]])), deps=[t1])
            t1 = s.op("vector", lambda e: e.tensor_reduce(
                col(1), sq4x[:, 0:4], AX.X, ALU.add), deps=[t1])
            t1 = s.op("vector", lambda e: e.tensor_reduce(
                col(2), AP(st6x, 2, [[12, 128], [3, 4]]), AX.X, ALU.add),
                deps=[t1])
            t1 = s.op("vector", lambda e: e.tensor_scalar_mul(
                col(3), col(2), 1.0 / 1024.0), deps=[t1])
            t1 = s.op("vector", lambda e: e.scalar_tensor_tensor(
                col(3), col(1), 0.25, col(3), ALU.mult, ALU.add), deps=[t1])
            t1 = s.op("vector", lambda e: e.scalar_tensor_tensor(
                col(4), col(0), -0.0625, col(0), ALU.mult, ALU.mult),
                deps=[t1])
            t1 = s.op("vector", lambda e: e.tensor_add(
                col(5), col(3), col(4)), deps=[t1])
            t3 = s.op("scalar", lambda e: e.activation(
                acolx[:, :], col(5), AF.Sqrt, bias=cc(C_EPS5),
                scale=1.0), deps=[t1])
            t3 = s.op("vector", lambda e: e.reciprocal(
                acolx[:, :], acolx[:, :]), deps=[t3])
            t4 = s.op("vector", lambda e: e.scalar_tensor_tensor(
                bcolx[:, :], col(0), -0.25, acolx[:, :],
                ALU.mult, ALU.mult), deps=[t3])
            return s.op("scalar", lambda e: e.activation(
                dst, src, AF.Identity, bias=bcolx[:, :], scale=acolx[:, :]),
                deps=[t4] + list(extra))

        def pP0(st):
            xr = xres[st % 2]
            dep = list(setup_done) if st == 0 else (
                [] if st == 1 else [ST[st - 2]["store"]])
            ST[st]["x"] = s.dma("gpsimd", lambda e, o=st * T_ST, xr=xr:
                e.dma_start(
                    out=xr.ap(),
                    in_=x_in.ap()[o:o + T_ST, :]
                        .rearrange("(pt p) d -> p pt d", p=128)),
                deps=dep)

        def ln_stats(scr, slot, src, dep0):
            st6x, sq4x, mv2x, acolx, bcolx = scr
            c0 = 8 * slot

            def col(i):
                return mv2x[:, c0 + i:c0 + i + 1]
            t1 = s.op("vector", lambda e: e.bn_stats(st6x[:, 0:6], src[:, 0:512]),
                      deps=dep0)
            t1 = s.op("vector", lambda e: e.bn_stats(st6x[:, 6:12], src[:, 512:1024]),
                      deps=[t1])
            t1 = s.op("vector", lambda e: e.tensor_reduce(
                col(0), AP(st6x, 1, [[12, 128], [3, 4]]), AX.X, ALU.add),
                deps=[t1])
            t1 = s.op("vector", lambda e: e.tensor_mul(
                sq4x[:, 0:4],
                AP(st6x, 1, [[12, 128], [3, 4]]),
                AP(st6x, 1, [[12, 128], [3, 4]])), deps=[t1])
            t1 = s.op("vector", lambda e: e.tensor_reduce(
                col(1), sq4x[:, 0:4], AX.X, ALU.add), deps=[t1])
            t1 = s.op("vector", lambda e: e.tensor_reduce(
                col(2), AP(st6x, 2, [[12, 128], [3, 4]]), AX.X, ALU.add),
                deps=[t1])
            t1 = s.op("vector", lambda e: e.tensor_scalar_mul(
                col(3), col(2), 1.0 / 1024.0), deps=[t1])
            t1 = s.op("vector", lambda e: e.scalar_tensor_tensor(
                col(3), col(1), 0.25, col(3), ALU.mult, ALU.add), deps=[t1])
            t1 = s.op("vector", lambda e: e.scalar_tensor_tensor(
                col(4), col(0), -0.0625, col(0), ALU.mult, ALU.mult),
                deps=[t1])
            t1 = s.op("vector", lambda e: e.tensor_add(
                col(5), col(3), col(4)), deps=[t1])
            t3 = s.op("scalar", lambda e: e.activation(
                col(6), col(5), AF.Sqrt, bias=cc(C_EPS5),
                scale=1.0), deps=[t1])
            t3 = s.op("vector", lambda e: e.reciprocal(
                col(6), col(6)), deps=[t3])
            t4 = s.op("vector", lambda e: e.scalar_tensor_tensor(
                col(7), col(0), -0.25, col(6),
                ALU.mult, ALU.mult), deps=[t3])
            return t4

        def pP1a(st):
            xr = xres[st % 2]
            t4s = []
            for pt in range(PT):
                dep0 = [ST[st]["x"]]
                if pt == 0 and st > 0:
                    dep0 += [ST[st - 1]["ln1_last"]]
                t4s.append(ln_stats(scrB, pt, xr[:, pt, :], dep0))
            ST[st]["ln1t4"] = t4s

        def pP1b(st):
            xr = xres[st % 2]
            tk_xlnT = []
            t5 = None
            for pt in range(PT):
                extra = [ST[st]["ln1t4"][pt]]
                if st > 0 and pt == 0:
                    extra += [ST[st - 1]["mm_o_last"]]
                c0 = 8 * pt
                t5 = s.op("scalar", lambda e, pt=pt, xr=xr, c0=c0: e.activation(
                    AP(o_dT, pt * D, [[2048, 128], [1, D]]), xr[:, pt, :],
                    AF.Identity, bias=mv2b[:, c0 + 7:c0 + 8],
                    scale=mv2b[:, c0 + 6:c0 + 7]),
                    deps=extra)
                gx = [] if st == 0 else [ST[st - 1]["mm_v_last"]]
                tk_xlnT.append(s.dma("sync", lambda e, pt=pt:
                    e.dma_start_transpose(
                        AP(xlnT2, pt * 128, [[8 * T_ST, 128], [T_ST, 8], [1, 128]]),
                        AP(o_dT, pt * D, [[2048, 128], [128, 8], [1, 128]])),
                    deps=[t5] + gx))
            ST[st]["ln1_last"] = t5
            ST[st]["xlnT"] = tk_xlnT

        def pP2(st):
            if st == 0:
                ev_tok = [None, None]
            else:
                ev_tok = [list(ST[st - 1]["evac_eA"]),
                          list(ST[st - 1]["evac_eB"])]
            gpv = [] if st == 0 else list(ST[st - 1]["vtm"])
            gab = [] if st == 0 else [ST[st - 1]["mm_ab_last"]]
            tk_qk_evac = []
            for h in range(NH):
                off = PVOFF[h % 2]

                def mm_pair(te, h=h, off=off):
                    last = None
                    for kc in range(8):
                        last = te.matmul(
                            AP(pV, off, [[1024, 128], [1, 256]]),
                            wqkv_sb[:, kc, h * 128:h * 128 + 128],
                            xlnT2[:, kc, :],
                            start=(kc == 0), stop=(kc == 7))
                    return last
                dd = (ST[st]["xlnT"] if h < 2 else []) + \
                    (list(ev_tok[h % 2]) if ev_tok[h % 2] else []) + \
                    (gpv if h < 2 else [])
                tmm = s.op("tensor", mm_pair, deps=dd)
                e1 = s.op("vector", lambda e, h=h, off=off: e.tensor_scalar(
                    AP(qmov, h * 128, [[32 * 128, 64], [16 * 128, 2], [1, 128]]),
                    AP(pV, off, [[1024, 64], [1, 2], [2, 128]]),
                    -2.0, bqm2_sb[:, h:h + 1], ALU.mult, ALU.add),
                    deps=[tmm] + (gab if h < 2 else []))
                e2 = s.op("vector", lambda e, h=h, off=off: e.tensor_scalar_add(
                    AP(kstat, h * 128, [[64 * 128, 64], [16 * 128, 2], [1, 128]]),
                    AP(pV, 64 * 1024 + off, [[1024, 64], [1, 2], [2, 128]]),
                    bk_lo[:, h:h + 1]),
                    deps=[tmm] + (gab if h < 2 else []))
                ev_tok[h % 2] = [e1, e2]
                tk_qk_evac += [e1, e2]
            ST[st]["evac_eA"] = ev_tok[0]
            ST[st]["evac_eB"] = ev_tok[1]
            ST[st]["evac_all"] = tk_qk_evac

        def pP2b(st):
            tk_vtm = []
            tmmv = None
            for pt in range(PT):
                def mm_v(te, pt=pt):
                    last = None
                    for j in range(2):
                        for kc in range(8):
                            last = te.matmul(
                                pV[:, j * 512:j * 512 + 512],
                                xlnT2[:, kc, pt * 128:pt * 128 + 128],
                                wqkv_sb[:, kc, 2 * D + j * 512:2 * D + j * 512 + 512],
                                start=(kc == 0), stop=False)
                        last = te.matmul(
                            pV[:, j * 512:j * 512 + 512],
                            ones_col[32:33, :],
                            b2_sb[32:33, j * 512:j * 512 + 512],
                            start=False, stop=True)
                    return last
                if tk_vtm:
                    dpv = [tk_vtm[-1]]
                else:
                    dpv = ST[st]["evac_all"][-2:] + (
                        [] if st == 0 else ST[st - 1]["yfm"][-2:])
                tmmv = s.op("tensor", mm_v, deps=ST[st]["xlnT"] + dpv)
                gv = [] if st == 0 else [ST[st - 1]["mm_y_last"]]
                tk_vtm.append(s.op("vector", lambda e, pt=pt: e.tensor_copy(
                    AP(hm_sb, pt * 1024, [[4096, 128], [1, 1024]]),
                    pV[:, :]), deps=[tmmv] + (gv if pt == 0 else [])))
            ST[st]["vtm"] = tk_vtm
            ST[st]["mm_v_last"] = tmmv
            vbuf = (st % 2) * T_ST * D
            gvd = [] if st < 2 else [ST[st - 2]["vg_last"]]
            ST[st]["vdump"] = s.dma("gpsimd", lambda e, vb=vbuf: e.dma_start(
                out=AP(vscr, vb, [[1024, 128], [131072, 2], [1, 1024]]),
                in_=AP(hm_sb, 0, [[4096, 128], [1024, 2], [1, 1024]])),
                deps=tk_vtm + gvd)

        def pP3(st):
            tk_rows = []
            evac = ST[st]["evac_all"]
            for par in range(2):
                godt = ([] if st == 0 else [ST[st - 1]["mm_o_last"]]) \
                    if par == 0 else []
                tq_x = [s.dma("sync", lambda e, par=par:
                    e.dma_start_transpose(
                        AP(o_dT, 0, [[2048, 128], [64, 16], [1, 64]]),
                        AP(qmov, (par * 16) * 128,
                           [[32 * 128, 64], [128, 16], [1, 128]])),
                    deps=evac + godt)]
                tsq = s.op("scalar", lambda e: e.activation(
                    AP(o_dT, 1024, [[2048, 128], [1, 1024]]),
                    AP(o_dT, 0, [[2048, 128], [1, 1024]]),
                    AF.Square, bias=0.0, scale=1.0), deps=tq_x)
                trd = s.op("vector", lambda e: e.tensor_reduce(
                    sq4p[:, :], AP(o_dT, 1024, [[2048, 128], [64, 16], [1, 64]]),
                    AX.X, ALU.add), deps=[tsq])
                tqs = s.op("vector", lambda e: e.tensor_scalar_mul(
                    smalls[:, 48:64], sq4p[:, :], 0.25), deps=[trd])
                tmq = s.op("scalar", lambda e: e.activation(
                    smalls[:, 32:48], sq4p[:, :], AF.Sqrt, bias=0.0,
                    scale=cc(C_Q25)), deps=[trd])
                tk_x2 = [s.dma("sync", lambda e, par=par:
                    e.dma_start_transpose(
                        AP(o_dT, 0, [[2048, 128], [64, 16], [1, 64]]),
                        AP(kstat, (par * 16) * 128,
                           [[64 * 128, 64], [128, 16], [1, 128]])),
                    deps=[tqs, tmq])]
                tsk = s.op("scalar", lambda e: e.activation(
                    AP(o_dT, 1024, [[2048, 128], [1, 1024]]),
                    AP(o_dT, 0, [[2048, 128], [1, 1024]]),
                    AF.Square, bias=0.0, scale=1.0), deps=tk_x2)
                trk = s.op("vector", lambda e: e.tensor_reduce(
                    sq4p[:, :], AP(o_dT, 1024, [[2048, 128], [64, 16], [1, 64]]),
                    AX.X, ALU.add), deps=[tsk])
                tks = s.op("vector", lambda e: e.tensor_copy(
                    smalls[:, 0:16], sq4p[:, :]), deps=[trk])
                tmk = s.op("scalar", lambda e: e.activation(
                    smalls[:, 16:32], sq4p[:, :], AF.Sqrt, bias=0.0, scale=1.0),
                    deps=[trk])
                gtd = [last_tdmp[0]] if last_tdmp[0] else []
                thm = s.op("vector", lambda e: e.transpose(
                    hmT[:, :], smalls[:, :]), deps=[tqs, tmq, tks, tmk] + gtd)
                hslot = 2 + par
                grw = ([] if st == 0 else [ST[st - 1]["mm_ab_last"]]) \
                    if par == 0 else []
                tdmp = s.dma("gpsimd", lambda e, hslot=hslot: e.dma_start(
                    out=AP(hscr, hslot * 8192, [[64, 128], [1, 64]]),
                    in_=hmT.ap()),
                    deps=[thm] + hsc_guard.get(hslot, []) + grw)
                rows_here = []
                for i in range(4):
                    rows_here.append(s.dma("gpsimd", lambda e, i=i, par=par,
                                           hslot=hslot: e.dma_start(
                        out=AP(qmov, 65 * 4096 + par * 2048 + 32 * i,
                               [[4096, 1], [128, 16], [1, 32]]),
                        in_=AP(hscr, hslot * 8192 + (32 * i) * 64 + 32,
                               [[64, 16], [1, 32]])),
                        deps=[tdmp]))
                    rows_here.append(s.dma("gpsimd", lambda e, i=i, par=par,
                                           hslot=hslot: e.dma_start(
                        out=AP(qmov, 66 * 4096 + par * 2048 + 32 * i,
                               [[4096, 1], [128, 16], [1, 32]]),
                        in_=AP(hscr, hslot * 8192 + (32 * i + 16) * 64 + 32,
                               [[64, 16], [1, 32]])),
                        deps=[tdmp]))
                    rows_here.append(s.dma("gpsimd", lambda e, i=i, par=par,
                                           hslot=hslot: e.dma_start(
                        out=AP(kstat, 64 * 8192 + par * 2048 + 32 * i,
                               [[8192, 1], [128, 16], [1, 32]]),
                        in_=AP(hscr, hslot * 8192 + (32 * i) * 64,
                               [[64, 16], [1, 32]])),
                        deps=[tdmp]))
                    rows_here.append(s.dma("gpsimd", lambda e, i=i, par=par,
                                           hslot=hslot: e.dma_start(
                        out=AP(kstat, 65 * 8192 + 4096 + par * 2048 + 32 * i,
                               [[8192, 1], [128, 16], [1, 32]]),
                        in_=AP(hscr, hslot * 8192 + (32 * i + 16) * 64,
                               [[64, 16], [1, 32]])),
                        deps=[tdmp]))
                hsc_guard[hslot] = rows_here
                last_tdmp[0] = tdmp
                tk_rows += rows_here
            ST[st]["rows"] = tk_rows

        def pP4(st):
            vbuf = (st % 2) * T_ST * D
            tk_odT = []
            tk_mm2_last = None if st == 0 else ST[st - 1]["mm2_last"]
            prevband = dict() if st == 0 else dict(ST[st - 1]["prevband"])
            mm2_extra = [] if st == 0 else [ST[st - 1]["hm14"]]
            tmpBs = [bankB, bankD]
            tmpCs = [bankC, bankE]
            tk_vg = []

            def mk_ab(SB):
                pb_rn = pRNs[SB % 2]

                def mm_ab(te, SB=SB, pb_rn=pb_rn):
                    last = None
                    for b2i in range(2):
                        for f in range(16):
                            blk0 = SB * 32 + b2i * 16 + f
                            last = te.matmul(
                                AP(pb_rn, (b2i * 64) * 512 + f * 32,
                                   [[512, 64], [1, 32]]),
                                AP(kstat, blk0, [[8192, 67], [128, 64]]),
                                AP(qmov, blk0, [[4096, 67], [128, 32]]),
                                start=True, stop=True,
                                tile_position=(0, b2i * 64))
                    return last
                return mm_ab

            ab = [None] * NSB
            ab[0] = s.op("tensor", mk_ab(0),
                         deps=ST[st]["rows"] + ST[st]["evac_all"] +
                         list(prevband.get(("rn", 0), [])))
            ab[1] = s.op("tensor", mk_ab(1),
                         deps=list(prevband.get(("rn", 1), [])))
            for p in range(2):
                tmpB = tmpBs[p]
                tmpC = tmpCs[p]
                tAcc = {0: [], 1: []}
                tA = []
                tN = []
                for sh in range(2):
                    for b2i in range(2):
                        t_ = s.op("vector", lambda e, sh=sh, b2i=b2i:
                            e.tensor_scalar_max(
                                bankA[b2i * 64 + sh * 32:b2i * 64 + sh * 32 + 32, :],
                                pRNs[sh][b2i * 64:b2i * 64 + 32, :], 0.0),
                            deps=[ab[2 * p + sh]] + prevband.get("A", []))
                        tA.append(t_)
                        tAcc[sh].append(t_)
                        t_ = s.op("vector", lambda e, sh=sh, b2i=b2i:
                            e.tensor_copy(
                                bankF[b2i * 64 + sh * 32:b2i * 64 + sh * 32 + 32, :],
                                pRNs[sh][b2i * 64 + 32:b2i * 64 + 64, :]),
                            deps=[ab[2 * p + sh]] + prevband.get("F", []))
                        tN.append(t_)
                        tAcc[sh].append(t_)
                if p == 0:
                    ab[2] = s.op("tensor", mk_ab(2), deps=tAcc[0])
                    ab[3] = s.op("tensor", mk_ab(3), deps=tAcc[1])
                tB = s.op("scalar", lambda e, tmpB=tmpB: e.activation(
                    tmpB[:, :], bankA[:, :], AF.Sqrt, bias=cc(C_EPS6), scale=1.0),
                    deps=tA + prevband.get(("B", p), []))
                tC = s.op("scalar", lambda e, tmpB=tmpB, tmpC=tmpC: e.activation(
                    tmpC[:, :], tmpB[:, :], AF.Sin, bias=cc(C_PI2),
                    scale=cc(C_MQ25)), deps=[tB] + prevband.get(("C", p), []))
                tU = s.op("vector", lambda e, tmpB=tmpB, tmpC=tmpC: e.tensor_mul(
                    tmpB[:, :], tmpC[:, :], tmpC[:, :]), deps=[tC])
                tW = s.op("vector", lambda e, tmpB=tmpB, tmpC=tmpC: e.tensor_scalar(
                    tmpC[:, :], tmpB[:, :], 2.0, -1.0, ALU.mult, ALU.add),
                    deps=[tU])
                tW = s.op("vector", lambda e, tmpC=tmpC: e.tensor_mul(
                    tmpC[:, :], tmpC[:, :], tmpC[:, :]), deps=[tW])
                tM = s.op("vector", lambda e, tmpB=tmpB, tmpC=tmpC: e.tensor_scalar(
                    tmpB[:, :], tmpC[:, :], 0.3, 0.85, ALU.mult, ALU.add),
                    deps=[tW])
                tDm = s.op("vector", lambda e, tmpB=tmpB, tmpC=tmpC: e.tensor_mul(
                    tmpC[:, :], tmpB[:, :], bankA[:, :]), deps=[tM])
                tR = s.op("scalar", lambda e, tmpB=tmpB, tmpC=tmpC: e.activation(
                    tmpB[:, :], tmpC[:, :], AF.Ln, bias=cc(C_EPS6),
                    scale=1.0), deps=[tDm])
                tR = s.op("scalar", lambda e, tmpB=tmpB: e.activation(
                    tmpB[:, :], tmpB[:, :], AF.Exp, bias=0.0,
                    scale=cc(C_M1)), deps=[tR])
                tF = s.op("vector", lambda e, tmpB=tmpB: e.tensor_mul(
                    bankF[:, :], bankF[:, :], tmpB[:, :]), deps=[tR] + tN)
                tP = s.op("scalar", lambda e: e.activation(
                    ptil[:, :], bankF[:, :], AF.Exp, bias=cc(C_M10), scale=1.0),
                    deps=[tF] + prevband.get("P", []))
                tSel = s.op("tensor", lambda e: e.matmul(
                    pS[:, :], SEL2[:, :], ptil[:, :], start=True, stop=True),
                    deps=[tP] + prevband.get("S", []))
                tRec = s.op("scalar", lambda e, tmpB=tmpB: e.activation(
                    tmpB[:, :], pS[:, :], AF.Ln, bias=cc(C_TINY),
                    scale=1.0), deps=[tSel])
                tAt0 = s.op("vector", lambda e, tmpB=tmpB, tmpC=tmpC: e.tensor_sub(
                    tmpC[:, :], bankF[:, :], tmpB[:, :]), deps=[tRec])
                tAt1 = s.op("scalar", lambda e, tmpC=tmpC: e.activation(
                    attnb[:, :], tmpC[:, :], AF.Exp, bias=cc(C_M10),
                    scale=1.0), deps=[tAt0] + prevband.get("At", []))
                tAt = s.op("vector", lambda e: e.tensor_mul(
                    attnb[:, :], attnb[:, :], tmask[:, :]), deps=[tAt1])

                tk_vg = []
                for sh in range(2):
                    for b2i in range(2):
                        for t2 in range(2):
                            tk_vg.append(s.dma("gpsimd", lambda e, p=p, sh=sh,
                                               b2i=b2i, t2=t2, vb=vbuf:
                                e.dma_start(
                                    out=AP(v_tg,
                                           (b2i * 64 + sh * 32 + 16 * t2) * 1024,
                                           [[1024, 16], [64, 16], [1, 64]]),
                                    in_=AP(vscr, vb + ((2 * p + sh) * 64 +
                                                       b2i * 32 + t2) * 1024,
                                           [[64, 16], [2048, 16], [1, 64]])),
                                deps=[ST[st]["vdump"], tk_mm2_last]))

                for sh in range(2):
                    def mm_2(te, sh=sh):
                        last = None
                        for b2i in range(2):
                            for f in range(16):
                                last = te.matmul(
                                    AP(pOT, (b2i * 64) * 512 + f * 32,
                                       [[512, 64], [1, 32]]),
                                    AP(v_tg, (b2i * 64 + sh * 32) * 1024 + f * 64,
                                       [[1024, 32], [1, 64]]),
                                    AP(attnb, (b2i * 64 + sh * 32) * 512 + f * 32,
                                       [[512, 32], [1, 32]]),
                                    start=True, stop=True,
                                    tile_position=(b2i * 64 + sh * 32, b2i * 64))
                        return last
                    godt4 = tk_odT[-4:] if tk_odT else (
                        [] if st == 0 else ST[st - 1]["odt4"])
                    tk_mm2_last = s.op("tensor", mm_2,
                                       deps=[tAt] + tk_vg + godt4 +
                                       (mm2_extra if (p == 0 and sh == 0) else []))
                    SB = 2 * p + sh
                    for b2i in range(2):
                        for hp in range(2):
                            tk_odT.append(s.op("vector",
                                lambda e, SB=SB, b2i=b2i, hp=hp: e.tensor_copy(
                                    AP(o_dT, (hp * 64) * (8 * T_ST) + SB * 64 + b2i * 32,
                                       [[8 * T_ST, 64], [2, 16], [1, 2], [T_ST, 8]]),
                                    AP(pOT, (b2i * 64) * 512 + hp,
                                       [[512, 64], [32, 16], [16, 2], [2, 8]])),
                                deps=[tk_mm2_last]))

                prevband.update({
                    ("rn", 0): tAcc[0],
                    ("rn", 1): tAcc[1],
                    ("B", p): [tAt0],
                    ("C", p): [tAt1],
                    "A": [tDm],
                    "F": [tAt0],
                    "P": [tSel, tk_mm2_last], "S": [tRec], "At": [tk_mm2_last],
                })
            ST[st]["mm_ab_last"] = ab[NSB - 1]
            ST[st]["mm2_last"] = tk_mm2_last
            ST[st]["odT"] = tk_odT
            ST[st]["odt4"] = tk_odT[-4:]
            ST[st]["vg_last"] = tk_vg[-1]
            ST[st]["prevband"] = prevband

        def pP5(st):
            xr = xres[st % 2]
            tk_afm = []
            tk_atm = []
            tmo = None
            hg2 = [] if st == 0 else ST[st - 1]["hg_last2"]
            for oc in range(8):
                pb_ao = pQK[oc % 2]

                def mm_o(te, oc=oc, pb_ao=pb_ao):
                    last = None
                    for h2 in range(8):
                        last = te.matmul(
                            pb_ao[:, :],
                            wout_sb[:, h2, oc * 128:oc * 128 + 128],
                            o_dT[:, h2, :],
                            start=(h2 == 0), stop=(h2 == 7))
                    return last
                dmo = ST[st]["odT"] + hg2 if oc < 2 else [tk_afm[oc - 2]]
                tmo = s.op("tensor", mm_o, deps=dmo)
                dslot = tk_atm[oc - 2:oc - 1] if oc >= 2 else \
                    ([] if st == 0 else [ST[st - 1]["ytm_last2"][oc]])
                tk_afm.append(s.op("scalar", lambda e, oc=oc, pb_ao=pb_ao:
                    e.activation(
                        afm[:, oc % 2, :], pb_ao[:, :], AF.Identity,
                        bias=bout_sb[:, oc:oc + 1], scale=1.0),
                    deps=[tmo] + dslot))
                gres = [] if (oc >= 2 or st == 0) else [ST[st - 1]["res2"][0],
                                                        ST[st - 1]["res2"][1]]
                tk_atm.append(s.dma("sync", lambda e, oc=oc:
                    e.dma_start_transpose(
                        AP(a_tm, oc * 128, [[PT * D, 128], [D, 2], [1, 128]]),
                        AP(afm, (oc % 2) * T_ST, [[2 * T_ST, 128], [128, 2], [1, 128]])),
                    deps=[tk_afm[oc]] + gres))
            tk_res1 = []
            for pt in range(PT):
                tk_res1.append(s.op("vector", lambda e, pt=pt, xr=xr:
                    e.tensor_add(
                        xr[:, pt, :], xr[:, pt, :], a_tm[:, pt, :]),
                    deps=tk_atm))
            ST[st]["mm_o_last"] = tmo
            ST[st]["res1"] = tk_res1

        def pP67(st):
            xr = xres[st % 2]
            st6x, sq4x, mv2x, acolx, bcolx = scrA
            tk_xhatT = []
            for pt in range(PT):
                c0 = 8 * pt

                def col(i, c0=c0):
                    return mv2x[:, c0 + i:c0 + i + 1]
                src = xr[:, pt, :]
                t1 = s.op("vector", lambda e, src=src: e.bn_stats(
                    st6x[:, 0:6], src[:, 0:512]), deps=[ST[st]["res1"][pt]])
                t1 = s.op("vector", lambda e, src=src: e.bn_stats(
                    st6x[:, 6:12], src[:, 512:1024]), deps=[t1])
                t1 = s.op("vector", lambda e, col=col: e.tensor_reduce(
                    col(0), AP(st6x, 1, [[12, 128], [3, 4]]), AX.X, ALU.add),
                    deps=[t1])
                t1 = s.op("vector", lambda e: e.tensor_mul(
                    sq4x[:, 0:4],
                    AP(st6x, 1, [[12, 128], [3, 4]]),
                    AP(st6x, 1, [[12, 128], [3, 4]])), deps=[t1])
                t1 = s.op("vector", lambda e, col=col: e.tensor_reduce(
                    col(1), sq4x[:, 0:4], AX.X, ALU.add), deps=[t1])
                t1 = s.op("vector", lambda e, col=col: e.tensor_reduce(
                    col(2), AP(st6x, 2, [[12, 128], [3, 4]]), AX.X, ALU.add),
                    deps=[t1])
                t1 = s.op("vector", lambda e, col=col: e.tensor_scalar_mul(
                    col(3), col(2), 1.0 / 1024.0), deps=[t1])
                t1 = s.op("vector", lambda e, col=col: e.scalar_tensor_tensor(
                    col(3), col(1), 0.25, col(3), ALU.mult, ALU.add), deps=[t1])
                t1 = s.op("vector", lambda e, col=col: e.scalar_tensor_tensor(
                    col(4), col(0), -0.0625, col(0), ALU.mult, ALU.mult),
                    deps=[t1])
                t1 = s.op("vector", lambda e, col=col: e.tensor_add(
                    col(5), col(3), col(4)), deps=[t1])
                t3 = s.op("scalar", lambda e, col=col: e.activation(
                    acolx[:, :], col(5), AF.Sqrt, bias=cc(C_EPS5),
                    scale=1.0), deps=[t1])
                t3 = s.op("vector", lambda e: e.reciprocal(
                    acolx[:, :], acolx[:, :]), deps=[t3])
                t3 = s.op("vector", lambda e, col=col: e.tensor_mul(
                    col(6), col(5), acolx[:, :]), deps=[t3])
                t3 = s.op("vector", lambda e, col=col: e.tensor_mul(
                    col(6), col(6), acolx[:, :]), deps=[t3])
                t3 = s.op("scalar", lambda e, col=col: e.activation(
                    col(7), col(6), AF.Sqrt, bias=cc(C_EPS5), scale=1.0),
                    deps=[t3])
                t3 = s.op("vector", lambda e, col=col: e.reciprocal(
                    col(7), col(7)), deps=[t3])
                t3 = s.op("vector", lambda e, col=col: e.tensor_mul(
                    acolx[:, :], acolx[:, :], col(7)), deps=[t3])
                t4 = s.op("vector", lambda e, col=col: e.scalar_tensor_tensor(
                    bcolx[:, :], col(0), -0.25, acolx[:, :],
                    ALU.mult, ALU.mult), deps=[t3])
                u5 = s.op("scalar", lambda e, pt=pt, src=src: e.activation(
                    a_tm[:, pt, :], src, AF.Identity, bias=bcolx[:, :],
                    scale=acolx[:, :]), deps=[t4])
                tk_xhatT.append(s.dma("scalar", lambda e, pt=pt:
                    e.dma_start_transpose(
                        AP(xhatT, pt * 128, [[8 * T_ST, 128], [T_ST, 8], [1, 128]]),
                        AP(a_tm, pt * D, [[PT * D, 128], [128, 8], [1, 128]])),
                    deps=[u5]))

            def mm_g(te):
                last = None
                for kc in range(8):
                    last = te.matmul(pS[0:8, 0:T_ST], wg_sb[:, kc, :],
                                     xhatT[:, kc, :], start=(kc == 0), stop=(kc == 7))
                return last
            tg = s.op("tensor", mm_g, deps=tk_xhatT)
            g0 = s.op("vector", lambda e: e.memset(lgp, -1e30), deps=[tg])
            g1 = s.op("vector", lambda e: e.tensor_copy(
                lgp[0:8, :], pS[0:8, 0:T_ST]), deps=[tg, g0])
            g2 = s.op("vector", lambda e: e.transpose(lgT, lgp), deps=[g1])
            g3 = s.op("vector", lambda e: e.reduce_max(
                mx8[:, :], AP(bankB, 0, [[512, 32], [32, 8], [1, 8]]), AX.X),
                deps=[g2])
            g4 = s.op("vector", lambda e: e.tensor_tensor(
                AP(bankC, 0, [[512, 32], [32, 8], [1, 8]]),
                AP(bankB, 0, [[512, 32], [32, 8], [1, 8]]),
                mx8[:, :, None].broadcast_to([32, 8, 8]), ALU.is_ge), deps=[g3])
            g5 = s.op("vector", lambda e: e.transpose(ohp, ohT), deps=[g4])
            g6 = s.op("vector", lambda e: e.tensor_copy(
                oh_em[:, :], AP(bankB, 32 * 512, [[512, 8], [1, T_ST]])),
                deps=[g5])
            pb_ = ST[st]["prevband"]
            pb_["A"] = pb_.get("A", []) + [g2]
            pb_[("B", 0)] = pb_.get(("B", 0), []) + [g6]
            pb_[("C", 0)] = pb_.get(("C", 0), []) + [g5]
            pb_["S"] = pb_.get("S", []) + [g1]
            ST[st]["xhatT"] = tk_xhatT
            ST[st]["g6"] = g6
            ST[st]["tg"] = tg

        def pP8(st, has_next):
            xr = xres[st % 2]
            tk_hm = []
            tk_hg = []
            pMs = [pOT, pS]
            for hc in range(16):
                e_ = hc // 2
                pb_h = pQK[hc % 2]
                pb_m = pMs[hc % 2]

                def mm_h(te, hc=hc, e_=e_, pb_h=pb_h, pb_m=pb_m):
                    for kc in range(8):
                        te.matmul(pb_h[:, :],
                                  w1_sb[:, kc, e_, (hc % 2) * 128:(hc % 2) * 128 + 128],
                                  xhatT[:, kc, :], start=(kc == 0), stop=(kc == 7))
                    return te.matmul(pb_m[:, 0:T_ST], sel8[:, e_, :],
                                     oh_em[:, :], start=True, stop=True)
                if hc < 2:
                    dh = ST[st]["xhatT"] + [ST[st]["g6"]]
                else:
                    dh = [tk_hm[hc - 2], tk_hg[hc - 2]]
                tmh = s.op("tensor", mm_h, deps=dh)
                hgo = (hc % 2) * T_ST
                th1 = s.op("scalar", lambda e, hc=hc, pb_h=pb_h, hgo=hgo:
                    e.activation(
                        AP(v_tg, hgo, [[1024, 128], [1, T_ST]]), pb_h[:, :],
                        AF.Gelu, bias=b1c_sb[:, hc:hc + 1], scale=1.0),
                    deps=[tmh])
                th2 = s.op("vector", lambda e, hc=hc, pb_m=pb_m, hgo=hgo:
                    e.tensor_mul(
                        hm_sb[:, hc, :], AP(v_tg, hgo, [[1024, 128], [1, T_ST]]),
                        pb_m[:, 0:T_ST]),
                    deps=[th1])
                tk_hg.append(th1)
                tk_hm.append(th2)
            pb_ = ST[st]["prevband"]
            pb_["S"] = pb_.get("S", []) + [tk_hm[15]]
            ST[st]["hm14"] = tk_hm[14]
            ST[st]["hm15"] = tk_hm[15]

            tk_yfm = []
            tk_ytm = []
            tmy = None
            for oc in range(8):
                py_sl = (oc % 2) * 512

                def mm_y(te, oc=oc, py_sl=py_sl):
                    for hc in range(16):
                        te.matmul(pV[:, py_sl:py_sl + T_ST],
                                  w2_sb[:, hc, oc * 128:oc * 128 + 128],
                                  hm_sb[:, hc, :], start=(hc == 0), stop=False)
                    return te.matmul(pV[:, py_sl:py_sl + T_ST],
                                     b2_sb[0:NE, oc * 128:oc * 128 + 128],
                                     oh_em[:, :], start=False, stop=True)
                if oc < 2:
                    dy = tk_hm + [ST[st]["g6"]]
                    if has_next:
                        dy = dy + (ST[st + 1]["evac_eA"] if oc == 0
                                   else ST[st + 1]["evac_eB"])
                else:
                    dy = [tk_yfm[oc - 2]]
                tmy = s.op("tensor", mm_y, deps=dy)
                dslot = tk_ytm[oc - 2:oc - 1] if oc >= 2 else []
                tk_yfm.append(s.op("scalar", lambda e, oc=oc, py_sl=py_sl:
                    e.activation(
                        afm[:, oc % 2, :], pV[:, py_sl:py_sl + T_ST],
                        AF.Identity, bias=0.0, scale=1.0),
                    deps=[tmy] + dslot))
                tk_ytm.append(s.dma("scalar", lambda e, oc=oc:
                    e.dma_start_transpose(
                        AP(a_tm, oc * 128, [[PT * D, 128], [D, 2], [1, 128]]),
                        AP(afm, (oc % 2) * T_ST, [[2 * T_ST, 128], [128, 2], [1, 128]])),
                    deps=[tk_yfm[oc]]))
            tk_res2 = []
            for pt in range(PT):
                tk_res2.append(s.op("vector", lambda e, pt=pt, xr=xr:
                    e.tensor_add(
                        xr[:, pt, :], xr[:, pt, :], a_tm[:, pt, :]),
                    deps=tk_ytm))
            ST[st]["hg_last2"] = tk_hg[-2:]
            ST[st]["mm_y_last"] = tmy
            ST[st]["yfm"] = tk_yfm
            ST[st]["ytm_last2"] = {0: tk_ytm[-2], 1: tk_ytm[-1]}
            ST[st]["res2"] = tk_res2

        def pP9(st):
            xr = xres[st % 2]
            ST[st]["store"] = s.dma("gpsimd", lambda e, o=st * T_ST, xr=xr:
                e.dma_start(
                    out=y_out.ap()[o:o + T_ST, :]
                        .rearrange("(pt p) d -> p pt d", p=128),
                    in_=xr.ap()), deps=ST[st]["res2"])

        pP0(0); pP1a(0); pP1b(0); pP2(0); pP2b(0); pP3(0)
        for st in range(n_st):
            has_next = st + 1 < n_st
            if has_next:
                pP0(st + 1)
            pP4(st)
            if has_next:
                pP1a(st + 1)
            pP5(st)
            if has_next:
                pP1b(st + 1)
            pP67(st)
            if has_next:
                pP2(st + 1)
            pP8(st, has_next)
            if has_next:
                pP2b(st + 1)
                pP3(st + 1)
            pP9(st)

        tk_store_last = ST[n_st - 1]["store"]
        if debug:
            x_res = xres[(n_st - 1) % 2]
            dbg_map = {"xlnT": xlnT2, "qmov": qmov, "kstat": kstat,
                       "bankA": bankA, "bankF": bankF, "ptil": ptil,
                       "attnb": attnb, "o_dT": o_dT, "a_tm": a_tm,
                       "oh_em": oh_em, "hm_sb": hm_sb,
                       "x_res": x_res, "v_tg": v_tg, "smalls": smalls,
                       "hmT": hmT}
            last = ST[n_st - 1]
            for nm in dbg_outs:
                t_ = dbg_map[nm]
                tk_store_last = s.dma("gpsimd", lambda e, nm=nm, t_=t_:
                    e.dma_start(out=dbg_outs[nm].ap(),
                                in_=AP(t_, 0, [[int(np.prod(t_.shape[1:])),
                                                t_.shape[0]],
                                               [1, int(np.prod(t_.shape[1:]))]])),
                    deps=[tk_store_last, last["mm2_last"], last["res2"][0],
                          last["res2"][1]])

        s.q["gpsimd"].append(("w", s._deps([tk_store_last]), s.cnt["gpsimd"]))

        incmap = {}
        for name in ENGS:
            nincs = 0
            m = {}
            for item in s.q[name]:
                if item[0] == "i":
                    if (name, item[2]) in s.used:
                        nincs += 1
                    m[item[2]] = nincs
            incmap[name] = m

        def emitter(name):
            def run(eng):
                seen = {}
                m_self = incmap[name]
                for item in s.q[name]:
                    if item[0] == "w":
                        emitted = item[2]
                        for (de, dn) in item[1]:
                            if de.endswith("_dma"):
                                val = dn
                            else:
                                if de == name and dn <= emitted - 16:
                                    continue
                                val = incmap[de][dn]
                            if seen.get(de, -1) >= val:
                                continue
                            seen[de] = val
                            eng.wait_ge(SEM[de], val)
                    elif item[0] == "i":
                        inst = item[1](eng)
                        if (name, item[2]) in s.used:
                            inst.then_inc(SEM[name], 1)
                    else:
                        item[1](eng).then_inc(SEM[name + "_dma"], 16)
            return run

        @block.tensor
        def _(e):
            emitter("tensor")(e)

        @block.vector
        def _(e):
            emitter("vector")(e)

        @block.scalar
        def _(e):
            emitter("scalar")(e)

        @block.sync
        def _(e):
            emitter("sync")(e)

        @block.gpsimd
        def _(e):
            emitter("gpsimd")(e)

    return nc


_NC_CACHE = {}


def kernel(L0_attn_x, L0_attn_W_qkv, L0_attn_b_qkv, L0_attn_W_out, L0_attn_b_out,
           L0_ffn_Wg, L0_ffn_ln_scale, L0_ffn_ln_bias, L0_ffn_W1, L0_ffn_b1,
           L0_ffn_W2, L0_ffn_b2):
    x = np.ascontiguousarray(np.asarray(L0_attn_x, dtype=np.float32)
                             .reshape(-1, D))
    n_tokens = x.shape[0]
    per = n_tokens // N_CORES

    key = per
    if key not in _NC_CACHE:
        _NC_CACHE[key] = build_program(per)
    nc = _NC_CACHE[key]

    tmaskc = np.zeros((128, 512), np.float32)
    for b2i in range(2):
        for sh in range(2):
            for t2 in range(2):
                r = b2i * 64 + sh * 32 + t2 * 16
                for f in range(16):
                    tmaskc[r:r + 16,
                           f * 32 + t2 * 16:f * 32 + t2 * 16 + 16] = 1.0
    sel2c = np.zeros((128, 128), np.float32)
    for r0 in range(0, 128, 16):
        sel2c[r0:r0 + 16, r0:r0 + 16] = 1.0
    sel8c = np.zeros((NE, NE, 128), np.float32)
    for e_ in range(NE):
        sel8c[e_, e_, :] = 1.0
    onesc = np.ones((4096,), np.float32)

    w = {
        "tmaskc": tmaskc, "sel2c": sel2c, "sel8c": sel8c, "onesc": onesc,
        "wqkv": np.ascontiguousarray(np.asarray(L0_attn_W_qkv, np.float32)),
        "bqkv": np.ascontiguousarray(np.asarray(L0_attn_b_qkv, np.float32)),
        "wout": np.ascontiguousarray(np.asarray(L0_attn_W_out, np.float32)),
        "bout": np.ascontiguousarray(np.asarray(L0_attn_b_out, np.float32)),
        "wg": np.ascontiguousarray(np.asarray(L0_ffn_Wg, np.float32)),
        "lns": np.ascontiguousarray(np.asarray(L0_ffn_ln_scale, np.float32)),
        "lnb": np.ascontiguousarray(np.asarray(L0_ffn_ln_bias, np.float32)),
        "w1": np.ascontiguousarray(np.asarray(L0_ffn_W1, np.float32)),
        "b1": np.ascontiguousarray(np.asarray(L0_ffn_b1, np.float32)),
        "w2": np.ascontiguousarray(np.asarray(L0_ffn_W2, np.float32)),
        "b2": np.ascontiguousarray(np.asarray(L0_ffn_b2, np.float32)),
    }
    in_maps = []
    for c in range(N_CORES):
        m = {"x": x[c * per:(c + 1) * per]}
        m.update(w)
        in_maps.append(m)

    res = run_bass_kernel_spmd(nc, in_maps, core_ids=list(range(N_CORES)))
    out = np.concatenate([r["y"] for r in res.results], axis=0)
    return out.reshape(np.asarray(L0_attn_x).shape).astype(np.float32)


# version 17
# speedup vs baseline: 1.2135x; 1.0469x over previous
import numpy as np
import concourse.bass as bass
import concourse.mybir as mybir
from concourse.bass_utils import run_bass_kernel_spmd

F32 = mybir.dt.float32
BF16 = mybir.dt.bfloat16
AF = mybir.ActivationFunctionType
ALU = mybir.AluOpType
AX = mybir.AxisListType
AP = bass.AP

N_CORES = 8
D = 1024
NH, HD, NE, EH = 16, 64, 8, 256
T_ST = 256
PT = 2
NSB = 4

ENGS = ("tensor", "vector", "scalar", "sync", "gpsimd")


class Sched:

    def __init__(self):
        self.q = {e: [] for e in ENGS}
        self.cnt = {e: 0 for e in ENGS}
        self.dcnt = {e: 0 for e in ENGS}
        self.used = set()

    def _deps(self, deps):
        dd = tuple(d for d in deps if d is not None)
        for d in dd:
            self.used.add(d)
        return dd

    def op(self, eng, fn, deps=()):
        self.q[eng].append(("w", self._deps(deps), self.cnt[eng]))
        self.q[eng].append(("i", fn, self.cnt[eng] + 1))
        self.cnt[eng] += 1
        return (eng, self.cnt[eng])

    def dma(self, eng, fn, deps=()):
        self.q[eng].append(("w", self._deps(deps), self.cnt[eng]))
        self.q[eng].append(("d", fn))
        self.dcnt[eng] += 1
        return (eng + "_dma", 16 * self.dcnt[eng])


def build_program(n_tokens, debug=False):
    n_st = n_tokens // T_ST
    nc = bass.Bass("TRN2", target_bir_lowering=False,
                   detect_race_conditions=False)

    x_in = nc.dram_tensor("x", [n_tokens, D], F32, kind="ExternalInput")
    wqkv_d = nc.dram_tensor("wqkv", [D, 3 * D], F32, kind="ExternalInput")
    bqkv_d = nc.dram_tensor("bqkv", [3 * D], F32, kind="ExternalInput")
    wout_d = nc.dram_tensor("wout", [D, D], F32, kind="ExternalInput")
    bout_d = nc.dram_tensor("bout", [D], F32, kind="ExternalInput")
    wg_d = nc.dram_tensor("wg", [NE, D], F32, kind="ExternalInput")
    lns_d = nc.dram_tensor("lns", [NE, D], F32, kind="ExternalInput")
    lnb_d = nc.dram_tensor("lnb", [NE, D], F32, kind="ExternalInput")
    w1_d = nc.dram_tensor("w1", [NE, D, EH], F32, kind="ExternalInput")
    b1_d = nc.dram_tensor("b1", [NE, EH], F32, kind="ExternalInput")
    w2_d = nc.dram_tensor("w2", [NE, EH, D], F32, kind="ExternalInput")
    b2_d = nc.dram_tensor("b2", [NE, D], F32, kind="ExternalInput")
    tmask_d = nc.dram_tensor("tmaskc", [128, 512], F32, kind="ExternalInput")
    sel2_d = nc.dram_tensor("sel2c", [128, 128], F32, kind="ExternalInput")
    sel8_d = nc.dram_tensor("sel8c", [NE, NE, 128], F32, kind="ExternalInput")
    ones_d = nc.dram_tensor("onesc", [4096], F32, kind="ExternalInput")
    y_out = nc.dram_tensor("y", [n_tokens, D], F32, kind="ExternalOutput")
    vscr = nc.dram_tensor("vscr", [2 * T_ST * D], BF16, kind="Internal")
    hscr = nc.dram_tensor("hscr", [4, 128 * 64], BF16, kind="Internal")
    dbg_specs = [
        ("xlnT", [128, 8 * T_ST], BF16), ("qmov", [67, 32 * 128], BF16),
        ("kstat", [67, 64 * 128], BF16), ("bankA", [128, 512], F32),
        ("bankF", [128, 512], F32), ("ptil", [128, 512], BF16),
        ("attnb", [128, 512], BF16), ("o_dT", [128, 8 * T_ST], BF16),
        ("a_tm", [128, PT * D], BF16),         ("oh_em", [NE, T_ST], BF16), ("hm_sb", [128, 16 * T_ST], BF16),
        ("x_res", [128, PT * D], BF16),
        ("smalls", [128, 64], BF16), ("hmT", [128, 64], BF16),
    ]
    dbg_outs = {}
    if debug:
        for nm, shp, dt_ in dbg_specs:
            dbg_outs[nm] = nc.dram_tensor("dbg_" + nm, shp, dt_,
                                          kind="ExternalOutput")

    s = Sched()

    from contextlib import ExitStack
    with ExitStack() as ctx, nc.Block() as block:
        wqkv_sb = ctx.enter_context(nc.sbuf_tensor("wqkv_sb", [128, 8, 3 * D], BF16))
        wout_sb = ctx.enter_context(nc.sbuf_tensor("wout_sb", [128, 8, D], BF16))
        w1_sb = ctx.enter_context(nc.sbuf_tensor("w1_sb", [128, 8, NE, EH], BF16))
        w2_sb = ctx.enter_context(nc.sbuf_tensor("w2_sb", [128, 16, D], BF16))
        wg_sb = ctx.enter_context(nc.sbuf_tensor("wg_sb", [128, 8, NE], BF16))
        lns_sb = ctx.enter_context(nc.sbuf_tensor("lns_sb", [128, 8, NE], BF16))
        lnb_sb = ctx.enter_context(nc.sbuf_tensor("lnb_sb", [128, 8, NE], BF16))
        b1r_sb = ctx.enter_context(nc.sbuf_tensor("b1r_sb", [128, 16], F32))
        b1c_sb = ctx.enter_context(nc.sbuf_tensor("b1c_sb", [128, 16], F32))
        b2_sb = ctx.enter_context(nc.sbuf_tensor("b2_sb", [33, D], BF16))
        bqm2_sb = ctx.enter_context(nc.sbuf_tensor("bqm2_sb", [64, 16], F32))
        bk_lo = ctx.enter_context(nc.sbuf_tensor("bk_lo", [64, 16], F32))
        bout_sb = ctx.enter_context(nc.sbuf_tensor("bout_sb", [128, 8], F32))
        ones_col = ctx.enter_context(nc.sbuf_tensor("ones_col", [33, 128], BF16))
        sel8 = ctx.enter_context(nc.sbuf_tensor("sel8", [NE, NE, 128], BF16))
        SEL2 = ctx.enter_context(nc.sbuf_tensor("SEL2", [128, 128], BF16))
        tmask = ctx.enter_context(nc.sbuf_tensor("tmask", [128, 512], BF16))
        consts = ctx.enter_context(nc.sbuf_tensor("consts", [128, 12], F32))
        x_res0 = ctx.enter_context(nc.sbuf_tensor("x_res0", [128, PT, D], BF16))
        x_res1 = ctx.enter_context(nc.sbuf_tensor("x_res1", [128, PT, D], BF16))
        xlnT2 = ctx.enter_context(nc.sbuf_tensor("xlnT2", [128, 8, T_ST], BF16))
        st6b = ctx.enter_context(nc.sbuf_tensor("st6b", [128, 12], F32))
        mv2b = ctx.enter_context(nc.sbuf_tensor("mv2b", [128, 16], F32))
        acolb = ctx.enter_context(nc.sbuf_tensor("acolb", [128, 1], F32))
        bcolb = ctx.enter_context(nc.sbuf_tensor("bcolb", [128, 1], F32))
        sq4b = ctx.enter_context(nc.sbuf_tensor("sq4b", [128, 4], F32))
        sq4p = ctx.enter_context(nc.sbuf_tensor("sq4p", [128, 16], F32))
        kstat = ctx.enter_context(nc.sbuf_tensor("kstat", [67, 64, 128], BF16))
        qmov = ctx.enter_context(nc.sbuf_tensor("qmov", [67, 32, 128], BF16))
        v_tg = ctx.enter_context(nc.sbuf_tensor("v_tg", [128, 16, 64], BF16))
        sq4 = ctx.enter_context(nc.sbuf_tensor("sq4", [128, 16], F32))
        smalls = ctx.enter_context(nc.sbuf_tensor("smalls", [128, 64], BF16))
        hmT = ctx.enter_context(nc.sbuf_tensor("hmT", [128, 64], BF16))
        bankA = ctx.enter_context(nc.sbuf_tensor("bankA", [128, 512], F32))
        bankD = ctx.enter_context(nc.sbuf_tensor("bankD", [128, 512], F32))
        bankE = ctx.enter_context(nc.sbuf_tensor("bankE", [128, 512], F32))
        bankB = ctx.enter_context(nc.sbuf_tensor("bankB", [128, 512], F32))
        bankC = ctx.enter_context(nc.sbuf_tensor("bankC", [128, 512], F32))
        bankF = ctx.enter_context(nc.sbuf_tensor("bankF", [128, 512], F32))
        ptil = ctx.enter_context(nc.sbuf_tensor("ptil", [128, 512], BF16))
        attnb = ptil
        o_dT = ctx.enter_context(nc.sbuf_tensor("o_dT", [128, 8, T_ST], BF16))
        afm = ctx.enter_context(nc.sbuf_tensor("afm", [128, 2, T_ST], BF16))
        a_tm = ctx.enter_context(nc.sbuf_tensor("a_tm", [128, PT, D], BF16))
        xhatT = ctx.enter_context(nc.sbuf_tensor("xhatT", [128, 8, T_ST], BF16))
        st6 = ctx.enter_context(nc.sbuf_tensor("st6", [128, 12], F32))
        mv2 = ctx.enter_context(nc.sbuf_tensor("mv2", [128, 16], F32))
        acol = ctx.enter_context(nc.sbuf_tensor("acol", [128, 1], F32))
        bcol = ctx.enter_context(nc.sbuf_tensor("bcol", [128, 1], F32))
        mx8 = ctx.enter_context(nc.sbuf_tensor("mx8", [32, 8], F32))
        oh_em = ctx.enter_context(nc.sbuf_tensor("oh_em", [NE, T_ST], BF16))
        hm_sb = ctx.enter_context(nc.sbuf_tensor("hm_sb", [128, 16, T_ST], BF16))
        lgp = bankA[0:32, 0:T_ST]
        lgT = bankB[0:32, 0:T_ST]
        ohT = bankC[0:32, 0:T_ST]
        ohp = bankB[32:64, 0:T_ST]
        pQK0 = ctx.enter_context(nc.psum_tensor("pQK0", [128, T_ST], F32))
        pQK1 = ctx.enter_context(nc.psum_tensor("pQK1", [128, T_ST], F32))
        pV = ctx.enter_context(nc.psum_tensor("pV", [128, D], F32))
        pRN = ctx.enter_context(nc.psum_tensor("pRN", [128, 512], F32))
        pRN2 = ctx.enter_context(nc.psum_tensor("pRN2", [128, 512], F32))
        pS = ctx.enter_context(nc.psum_tensor("pS", [128, 512], F32))
        pOT = ctx.enter_context(nc.psum_tensor("pOT", [128, 512], F32))
        sm_t = ctx.enter_context(nc.semaphore("sm_t"))
        sm_v = ctx.enter_context(nc.semaphore("sm_v"))
        sm_s = ctx.enter_context(nc.semaphore("sm_s"))
        sm_sy = ctx.enter_context(nc.semaphore("sm_sy"))
        sm_g = ctx.enter_context(nc.semaphore("sm_g"))
        dm_sy = ctx.enter_context(nc.semaphore("dm_sy"))
        dm_g = ctx.enter_context(nc.semaphore("dm_g"))
        dm_sc = ctx.enter_context(nc.semaphore("dm_sc"))

        SEM = {"tensor": sm_t, "vector": sm_v, "scalar": sm_s,
               "sync": sm_sy, "gpsimd": sm_g,
               "sync_dma": dm_sy, "gpsimd_dma": dm_g, "scalar_dma": dm_sc}

        (C_EPS5, C_EPS6, C_PI2, C_M10, C_TINY, C_2, C_Q25, C_M2,
         C_M1, C_MQ25, C_HALF) = range(11)

        def cc(i, n=128):
            return consts[0:n, i:i + 1]

        tk_w = []

        def gld(dst_ap, src_ap):
            tk_w.append(s.dma("gpsimd",
                              lambda e, d=dst_ap, sr=src_ap: e.dma_start(
                                  out=d, in_=sr, allow_slow_non_contiguous=True)))

        for h in range(NH):
            for sec in range(2):
                gld(wqkv_sb[:, :, h * 128 + sec * 64:h * 128 + sec * 64 + 64],
                    AP(wqkv_d, sec * 1024 + h * 64,
                       [[3 * D, 128], [128 * 3 * D, 8], [1, 64]]))
        gld(wqkv_sb[:, :, 2 * D:3 * D],
            AP(wqkv_d, 2 * D, [[3 * D, 128], [128 * 3 * D, 8], [1, D]]))
        gld(wout_sb.ap(), wout_d.ap().rearrange("(kc p) o -> p kc o", p=128))
        for e_ in range(NE):
            gld(w1_sb[:, :, e_, :],
                w1_d.ap()[e_].rearrange("(kc p) hh -> p kc hh", p=128))
        gld(w2_sb.ap(), w2_d.ap().rearrange("e (hc p) o -> p (e hc) o", p=128))
        for e_ in range(NE):
            gld(wg_sb[:, :, e_], wg_d.ap()[e_].rearrange("(kc p) -> p kc", p=128))
            gld(lns_sb[:, :, e_], lns_d.ap()[e_].rearrange("(kc p) -> p kc", p=128))
            gld(lnb_sb[:, :, e_], lnb_d.ap()[e_].rearrange("(kc p) -> p kc", p=128))
        gld(b1r_sb.ap(), b1_d.ap().rearrange("e (hc p) -> p (e hc)", p=128))
        gld(b2_sb[0:NE, :], b2_d.ap())
        gld(bqm2_sb.ap(), bqkv_d.ap()[0:D].rearrange("(h d) -> d h", d=64))
        gld(bk_lo.ap(), bqkv_d.ap()[D:2 * D].rearrange("(h d) -> d h", d=64))
        gld(b2_sb[32:33, :], bqkv_d.ap()[2 * D:3 * D][None, :])
        gld(bout_sb.ap(), bout_d.ap().rearrange("(oc p) -> p oc", p=128))

        tk_ms = []

        def ms(ap_, val):
            tk_ms.append(s.op("vector", lambda e, a=ap_, v=val: e.memset(a, v)))

        ms(cc(C_EPS5), 1e-5)
        ms(cc(C_EPS6), 1e-6)
        ms(cc(C_PI2), float(np.pi / 2))
        ms(cc(C_M10), -10.0)
        ms(cc(C_TINY), 1e-9)
        ms(cc(C_2), 2.0)
        ms(cc(C_Q25), 0.25)
        ms(cc(C_M2), -2.0)
        ms(cc(C_M1), -1.0)
        ms(cc(C_MQ25), -0.25)
        ms(cc(C_HALF), 0.5)
        ms(ones_col[32:33, :], 1.0)
        gld(sel8.ap(), sel8_d.ap())
        gld(SEL2.ap(), sel2_d.ap())
        gld(tmask.ap(), tmask_d.ap())
        ms(kstat.ap(), 0.0)
        tk_w.append(s.dma("gpsimd", lambda e: e.dma_start(
            out=kstat[66:67, 0:32, :],
            in_=ones_d.ap().rearrange("(a c b) -> a c b", a=1, c=32))))
        ms(qmov.ap(), 0.0)
        ms(qmov[64:65, :, :], 1.0)
        ms(bankF.ap(), 0.0)

        def mk_b1c(te):
            last = None
            for hc in range(16):
                for kc in range(8):
                    last = te.matmul(
                        pQK0[:, hc:hc + 1],
                        w1_sb[:, kc, hc // 2, (hc % 2) * 128:(hc % 2) * 128 + 128],
                        lnb_sb[:, kc, hc // 2:hc // 2 + 1],
                        start=(kc == 0), stop=(kc == 7))
            return last
        tk_b1cm = s.op("tensor", mk_b1c, deps=tk_w)
        tk_b1c = s.op("vector", lambda e: e.tensor_add(
            b1c_sb[:, :], pQK0[:, 0:16], b1r_sb[:, :]), deps=[tk_b1cm])
        tk_w1f = s.op("vector", lambda e: e.tensor_mul(
            w1_sb.ap(), w1_sb.ap(),
            lns_sb[:, :, :, None].broadcast_to([128, 8, NE, EH])),
            deps=[tk_b1cm])

        setup_done = tk_w + tk_ms + [tk_b1c, tk_w1f]

        pQK = [pQK0, pQK1]
        pRNs = [pRN, pRN2]
        xres = [x_res0, x_res1]
        PVOFF = [0, 512]
        hsc_guard = {}
        last_tdmp = [None]
        ST = [dict() for _ in range(n_st)]

        scrA = (st6, sq4, mv2, acol, bcol)
        scrB = (st6b, sq4b, mv2b, acolb, bcolb)

        def layer_norm(scr, slot, src, dst, dep0, extra=()):
            st6x, sq4x, mv2x, acolx, bcolx = scr
            c0 = 8 * slot

            def col(i):
                return mv2x[:, c0 + i:c0 + i + 1]
            t1 = s.op("vector", lambda e: e.bn_stats(st6x[:, 0:6], src[:, 0:512]),
                      deps=dep0)
            t1 = s.op("vector", lambda e: e.bn_stats(st6x[:, 6:12], src[:, 512:1024]),
                      deps=[t1])
            t1 = s.op("vector", lambda e: e.tensor_reduce(
                col(0), AP(st6x, 1, [[12, 128], [3, 4]]), AX.X, ALU.add),
                deps=[t1])
            t1 = s.op("vector", lambda e: e.tensor_mul(
                sq4x[:, 0:4],
                AP(st6x, 1, [[12, 128], [3, 4]]),
                AP(st6x, 1, [[12, 128], [3, 4]])), deps=[t1])
            t1 = s.op("vector", lambda e: e.tensor_reduce(
                col(1), sq4x[:, 0:4], AX.X, ALU.add), deps=[t1])
            t1 = s.op("vector", lambda e: e.tensor_reduce(
                col(2), AP(st6x, 2, [[12, 128], [3, 4]]), AX.X, ALU.add),
                deps=[t1])
            t1 = s.op("vector", lambda e: e.tensor_scalar_mul(
                col(3), col(2), 1.0 / 1024.0), deps=[t1])
            t1 = s.op("vector", lambda e: e.scalar_tensor_tensor(
                col(3), col(1), 0.25, col(3), ALU.mult, ALU.add), deps=[t1])
            t1 = s.op("vector", lambda e: e.scalar_tensor_tensor(
                col(4), col(0), -0.0625, col(0), ALU.mult, ALU.mult),
                deps=[t1])
            t1 = s.op("vector", lambda e: e.tensor_add(
                col(5), col(3), col(4)), deps=[t1])
            t3 = s.op("scalar", lambda e: e.activation(
                acolx[:, :], col(5), AF.Sqrt, bias=cc(C_EPS5),
                scale=1.0), deps=[t1])
            t3 = s.op("vector", lambda e: e.reciprocal(
                acolx[:, :], acolx[:, :]), deps=[t3])
            t4 = s.op("vector", lambda e: e.scalar_tensor_tensor(
                bcolx[:, :], col(0), -0.25, acolx[:, :],
                ALU.mult, ALU.mult), deps=[t3])
            return s.op("scalar", lambda e: e.activation(
                dst, src, AF.Identity, bias=bcolx[:, :], scale=acolx[:, :]),
                deps=[t4] + list(extra))

        def pP0(st):
            xr = xres[st % 2]
            dep = list(setup_done) if st == 0 else (
                [] if st == 1 else [ST[st - 2]["store"]])
            ST[st]["x"] = s.dma("gpsimd", lambda e, o=st * T_ST, xr=xr:
                e.dma_start(
                    out=xr.ap(),
                    in_=x_in.ap()[o:o + T_ST, :]
                        .rearrange("(pt p) d -> p pt d", p=128)),
                deps=dep)

        def ln_stats(scr, slot, src, dep0):
            st6x, sq4x, mv2x, acolx, bcolx = scr
            c0 = 8 * slot

            def col(i):
                return mv2x[:, c0 + i:c0 + i + 1]
            t1 = s.op("vector", lambda e: e.bn_stats(st6x[:, 0:6], src[:, 0:512]),
                      deps=dep0)
            t1 = s.op("vector", lambda e: e.bn_stats(st6x[:, 6:12], src[:, 512:1024]),
                      deps=[t1])
            t1 = s.op("vector", lambda e: e.tensor_reduce(
                col(0), AP(st6x, 1, [[12, 128], [3, 4]]), AX.X, ALU.add),
                deps=[t1])
            t1 = s.op("vector", lambda e: e.tensor_mul(
                sq4x[:, 0:4],
                AP(st6x, 1, [[12, 128], [3, 4]]),
                AP(st6x, 1, [[12, 128], [3, 4]])), deps=[t1])
            t1 = s.op("vector", lambda e: e.tensor_reduce(
                col(1), sq4x[:, 0:4], AX.X, ALU.add), deps=[t1])
            t1 = s.op("vector", lambda e: e.tensor_reduce(
                col(2), AP(st6x, 2, [[12, 128], [3, 4]]), AX.X, ALU.add),
                deps=[t1])
            t1 = s.op("vector", lambda e: e.tensor_scalar_mul(
                col(3), col(2), 1.0 / 1024.0), deps=[t1])
            t1 = s.op("vector", lambda e: e.scalar_tensor_tensor(
                col(3), col(1), 0.25, col(3), ALU.mult, ALU.add), deps=[t1])
            t1 = s.op("vector", lambda e: e.scalar_tensor_tensor(
                col(4), col(0), -0.0625, col(0), ALU.mult, ALU.mult),
                deps=[t1])
            t1 = s.op("vector", lambda e: e.tensor_add(
                col(5), col(3), col(4)), deps=[t1])
            t3 = s.op("scalar", lambda e: e.activation(
                col(6), col(5), AF.Sqrt, bias=cc(C_EPS5),
                scale=1.0), deps=[t1])
            t3 = s.op("vector", lambda e: e.reciprocal(
                col(6), col(6)), deps=[t3])
            t4 = s.op("vector", lambda e: e.scalar_tensor_tensor(
                col(7), col(0), -0.25, col(6),
                ALU.mult, ALU.mult), deps=[t3])
            return t4

        def pP1a(st):
            xr = xres[st % 2]
            t4s = []
            for pt in range(PT):
                dep0 = [ST[st]["x"]]
                if pt == 0 and st > 0:
                    dep0 += [ST[st - 1]["ln1_last"]]
                t4s.append(ln_stats(scrB, pt, xr[:, pt, :], dep0))
            ST[st]["ln1t4"] = t4s

        def pP1b(st):
            xr = xres[st % 2]
            tk_xlnT = []
            t5 = None
            for pt in range(PT):
                extra = [ST[st]["ln1t4"][pt]]
                if st > 0 and pt == 0:
                    extra += [ST[st - 1]["mm_o_last"]]
                c0 = 8 * pt
                t5 = s.op("scalar", lambda e, pt=pt, xr=xr, c0=c0: e.activation(
                    AP(o_dT, pt * D, [[2048, 128], [1, D]]), xr[:, pt, :],
                    AF.Identity, bias=mv2b[:, c0 + 7:c0 + 8],
                    scale=mv2b[:, c0 + 6:c0 + 7]),
                    deps=extra)
                gx = [] if st == 0 else [ST[st - 1]["mm_v_last"]]
                tk_xlnT.append(s.dma("sync", lambda e, pt=pt:
                    e.dma_start_transpose(
                        AP(xlnT2, pt * 128, [[8 * T_ST, 128], [T_ST, 8], [1, 128]]),
                        AP(o_dT, pt * D, [[2048, 128], [128, 8], [1, 128]])),
                    deps=[t5] + gx))
            ST[st]["ln1_last"] = t5
            ST[st]["xlnT"] = tk_xlnT

        def pP2(st):
            if st == 0:
                ev_tok = [None, None]
            else:
                ev_tok = [list(ST[st - 1]["evac_eA"]),
                          list(ST[st - 1]["evac_eB"])]
            gpv = [] if st == 0 else list(ST[st - 1]["vtm"])
            gab = [] if st == 0 else [ST[st - 1]["mm_ab_last"]]
            tk_qk_evac = []
            for h in range(NH):
                off = PVOFF[h % 2]

                def mm_pair(te, h=h, off=off):
                    last = None
                    for kc in range(8):
                        last = te.matmul(
                            AP(pV, off, [[1024, 128], [1, 256]]),
                            wqkv_sb[:, kc, h * 128:h * 128 + 128],
                            xlnT2[:, kc, :],
                            start=(kc == 0), stop=(kc == 7))
                    return last
                dd = (ST[st]["xlnT"] if h < 2 else []) + \
                    (list(ev_tok[h % 2]) if ev_tok[h % 2] else []) + \
                    (gpv if h < 2 else [])
                tmm = s.op("tensor", mm_pair, deps=dd)
                e1 = s.op("vector", lambda e, h=h, off=off: e.tensor_scalar(
                    AP(qmov, h * 128, [[32 * 128, 64], [16 * 128, 2], [1, 128]]),
                    AP(pV, off, [[1024, 64], [1, 2], [2, 128]]),
                    -2.0, bqm2_sb[:, h:h + 1], ALU.mult, ALU.add),
                    deps=[tmm] + (gab if h < 2 else []))
                e2 = s.op("vector", lambda e, h=h, off=off: e.tensor_scalar_add(
                    AP(kstat, h * 128, [[64 * 128, 64], [16 * 128, 2], [1, 128]]),
                    AP(pV, 64 * 1024 + off, [[1024, 64], [1, 2], [2, 128]]),
                    bk_lo[:, h:h + 1]),
                    deps=[tmm] + (gab if h < 2 else []))
                ev_tok[h % 2] = [e1, e2]
                tk_qk_evac += [e1, e2]
            ST[st]["evac_eA"] = ev_tok[0]
            ST[st]["evac_eB"] = ev_tok[1]
            ST[st]["evac_all"] = tk_qk_evac

        def pP2b(st):
            tk_vtm = []
            tmmv = None
            for pt in range(PT):
                def mm_v(te, pt=pt):
                    last = None
                    for j in range(2):
                        for kc in range(8):
                            last = te.matmul(
                                pV[:, j * 512:j * 512 + 512],
                                xlnT2[:, kc, pt * 128:pt * 128 + 128],
                                wqkv_sb[:, kc, 2 * D + j * 512:2 * D + j * 512 + 512],
                                start=(kc == 0), stop=False)
                        last = te.matmul(
                            pV[:, j * 512:j * 512 + 512],
                            ones_col[32:33, :],
                            b2_sb[32:33, j * 512:j * 512 + 512],
                            start=False, stop=True)
                    return last
                if tk_vtm:
                    dpv = [tk_vtm[-1]]
                else:
                    dpv = ST[st]["evac_all"][-2:] + (
                        [] if st == 0 else ST[st - 1]["yfm"][-2:])
                tmmv = s.op("tensor", mm_v, deps=ST[st]["xlnT"] + dpv)
                gv = [] if st == 0 else [ST[st - 1]["mm_y_last"]]
                tk_vtm.append(s.op("vector", lambda e, pt=pt: e.tensor_copy(
                    AP(hm_sb, pt * 1024, [[4096, 128], [1, 1024]]),
                    pV[:, :]), deps=[tmmv] + (gv if pt == 0 else [])))
            ST[st]["vtm"] = tk_vtm
            ST[st]["mm_v_last"] = tmmv
            vbuf = (st % 2) * T_ST * D
            gvd = [] if st < 2 else [ST[st - 2]["vg_last"]]
            ST[st]["vdump"] = s.dma("gpsimd", lambda e, vb=vbuf: e.dma_start(
                out=AP(vscr, vb, [[1024, 128], [131072, 2], [1, 1024]]),
                in_=AP(hm_sb, 0, [[4096, 128], [1024, 2], [1, 1024]])),
                deps=tk_vtm + gvd)

        def pP3(st):
            tk_rows = []
            evac = ST[st]["evac_all"]
            for par in range(2):
                godt = ([] if st == 0 else [ST[st - 1]["mm_o_last"]]) \
                    if par == 0 else []
                tq_x = [s.dma("sync", lambda e, par=par:
                    e.dma_start_transpose(
                        AP(o_dT, 0, [[2048, 128], [64, 16], [1, 64]]),
                        AP(qmov, (par * 16) * 128,
                           [[32 * 128, 64], [128, 16], [1, 128]])),
                    deps=evac + godt)]
                tsq = s.op("scalar", lambda e: e.activation(
                    AP(o_dT, 1024, [[2048, 128], [1, 1024]]),
                    AP(o_dT, 0, [[2048, 128], [1, 1024]]),
                    AF.Square, bias=0.0, scale=1.0), deps=tq_x)
                trd = s.op("vector", lambda e: e.tensor_reduce(
                    sq4p[:, :], AP(o_dT, 1024, [[2048, 128], [64, 16], [1, 64]]),
                    AX.X, ALU.add), deps=[tsq])
                tqs = s.op("vector", lambda e: e.tensor_scalar_mul(
                    smalls[:, 48:64], sq4p[:, :], 0.25), deps=[trd])
                tmq = s.op("scalar", lambda e: e.activation(
                    smalls[:, 32:48], sq4p[:, :], AF.Sqrt, bias=0.0,
                    scale=cc(C_Q25)), deps=[trd])
                tk_x2 = [s.dma("sync", lambda e, par=par:
                    e.dma_start_transpose(
                        AP(o_dT, 0, [[2048, 128], [64, 16], [1, 64]]),
                        AP(kstat, (par * 16) * 128,
                           [[64 * 128, 64], [128, 16], [1, 128]])),
                    deps=[tqs, tmq])]
                tsk = s.op("scalar", lambda e: e.activation(
                    AP(o_dT, 1024, [[2048, 128], [1, 1024]]),
                    AP(o_dT, 0, [[2048, 128], [1, 1024]]),
                    AF.Square, bias=0.0, scale=1.0), deps=tk_x2)
                trk = s.op("vector", lambda e: e.tensor_reduce(
                    sq4p[:, :], AP(o_dT, 1024, [[2048, 128], [64, 16], [1, 64]]),
                    AX.X, ALU.add), deps=[tsk])
                tks = s.op("vector", lambda e: e.tensor_copy(
                    smalls[:, 0:16], sq4p[:, :]), deps=[trk])
                tmk = s.op("scalar", lambda e: e.activation(
                    smalls[:, 16:32], sq4p[:, :], AF.Sqrt, bias=0.0, scale=1.0),
                    deps=[trk])
                gtd = [last_tdmp[0]] if last_tdmp[0] else []
                thm = s.op("vector", lambda e: e.transpose(
                    hmT[:, :], smalls[:, :]), deps=[tqs, tmq, tks, tmk] + gtd)
                hslot = 2 + par
                grw = ([] if st == 0 else [ST[st - 1]["mm_ab_last"]]) \
                    if par == 0 else []
                tdmp = s.dma("gpsimd", lambda e, hslot=hslot: e.dma_start(
                    out=AP(hscr, hslot * 8192, [[64, 128], [1, 64]]),
                    in_=hmT.ap()),
                    deps=[thm] + hsc_guard.get(hslot, []) + grw)
                rows_here = []
                for i in range(4):
                    rows_here.append(s.dma("gpsimd", lambda e, i=i, par=par,
                                           hslot=hslot: e.dma_start(
                        out=AP(qmov, 65 * 4096 + par * 2048 + 32 * i,
                               [[4096, 1], [128, 16], [1, 32]]),
                        in_=AP(hscr, hslot * 8192 + (32 * i) * 64 + 32,
                               [[64, 16], [1, 32]])),
                        deps=[tdmp]))
                    rows_here.append(s.dma("gpsimd", lambda e, i=i, par=par,
                                           hslot=hslot: e.dma_start(
                        out=AP(qmov, 66 * 4096 + par * 2048 + 32 * i,
                               [[4096, 1], [128, 16], [1, 32]]),
                        in_=AP(hscr, hslot * 8192 + (32 * i + 16) * 64 + 32,
                               [[64, 16], [1, 32]])),
                        deps=[tdmp]))
                    rows_here.append(s.dma("gpsimd", lambda e, i=i, par=par,
                                           hslot=hslot: e.dma_start(
                        out=AP(kstat, 64 * 8192 + par * 2048 + 32 * i,
                               [[8192, 1], [128, 16], [1, 32]]),
                        in_=AP(hscr, hslot * 8192 + (32 * i) * 64,
                               [[64, 16], [1, 32]])),
                        deps=[tdmp]))
                    rows_here.append(s.dma("gpsimd", lambda e, i=i, par=par,
                                           hslot=hslot: e.dma_start(
                        out=AP(kstat, 65 * 8192 + 4096 + par * 2048 + 32 * i,
                               [[8192, 1], [128, 16], [1, 32]]),
                        in_=AP(hscr, hslot * 8192 + (32 * i + 16) * 64,
                               [[64, 16], [1, 32]])),
                        deps=[tdmp]))
                hsc_guard[hslot] = rows_here
                last_tdmp[0] = tdmp
                tk_rows += rows_here
            ST[st]["rows"] = tk_rows

        def pP4(st):
            vbuf = (st % 2) * T_ST * D
            tk_odT = []
            tk_mm2_last = None if st == 0 else ST[st - 1]["mm2_last"]
            prevband = dict() if st == 0 else dict(ST[st - 1]["prevband"])
            mm2_extra = [] if st == 0 else [ST[st - 1]["hm14"]]
            tmpBs = [bankB, bankD]
            tmpCs = [bankC, bankE]
            tk_vg = []

            def mk_ab(SB):
                pb_rn = pRNs[SB % 2]

                def mm_ab(te, SB=SB, pb_rn=pb_rn):
                    last = None
                    for b2i in range(2):
                        for f in range(16):
                            blk0 = SB * 32 + b2i * 16 + f
                            last = te.matmul(
                                AP(pb_rn, (b2i * 64) * 512 + f * 32,
                                   [[512, 64], [1, 32]]),
                                AP(kstat, blk0, [[8192, 67], [128, 64]]),
                                AP(qmov, blk0, [[4096, 67], [128, 32]]),
                                start=True, stop=True,
                                tile_position=(0, b2i * 64))
                    return last
                return mm_ab

            ab = [None] * NSB
            ab[0] = s.op("tensor", mk_ab(0),
                         deps=ST[st]["rows"] + ST[st]["evac_all"] +
                         list(prevband.get(("rn", 0), [])))
            ab[1] = s.op("tensor", mk_ab(1),
                         deps=list(prevband.get(("rn", 1), [])))
            for p in range(2):
                tmpB = tmpBs[p]
                tmpC = tmpCs[p]
                tAcc = {0: [], 1: []}
                tA = []
                tN = []
                for sh in range(2):
                    for b2i in range(2):
                        t_ = s.op("vector", lambda e, sh=sh, b2i=b2i:
                            e.tensor_scalar_max(
                                bankA[b2i * 64 + sh * 32:b2i * 64 + sh * 32 + 32, :],
                                pRNs[sh][b2i * 64:b2i * 64 + 32, :], 0.0),
                            deps=[ab[2 * p + sh]] + prevband.get("A", []))
                        tA.append(t_)
                        tAcc[sh].append(t_)
                        t_ = s.op("vector", lambda e, sh=sh, b2i=b2i:
                            e.tensor_copy(
                                bankF[b2i * 64 + sh * 32:b2i * 64 + sh * 32 + 32, :],
                                pRNs[sh][b2i * 64 + 32:b2i * 64 + 64, :]),
                            deps=[ab[2 * p + sh]] + prevband.get("F", []))
                        tN.append(t_)
                        tAcc[sh].append(t_)
                if p == 0:
                    ab[2] = s.op("tensor", mk_ab(2), deps=tAcc[0])
                    ab[3] = s.op("tensor", mk_ab(3), deps=tAcc[1])
                tB = s.op("scalar", lambda e, tmpB=tmpB: e.activation(
                    tmpB[:, :], bankA[:, :], AF.Sqrt, bias=cc(C_EPS6), scale=1.0),
                    deps=tA + prevband.get(("B", p), []))
                tC = s.op("scalar", lambda e, tmpB=tmpB, tmpC=tmpC: e.activation(
                    tmpC[:, :], tmpB[:, :], AF.Sin, bias=cc(C_PI2),
                    scale=cc(C_MQ25)), deps=[tB] + prevband.get(("C", p), []))
                tU = s.op("vector", lambda e, tmpB=tmpB, tmpC=tmpC: e.tensor_mul(
                    tmpB[:, :], tmpC[:, :], tmpC[:, :]), deps=[tC])
                tW = s.op("vector", lambda e, tmpB=tmpB, tmpC=tmpC: e.tensor_scalar(
                    tmpC[:, :], tmpB[:, :], 2.0, -1.0, ALU.mult, ALU.add),
                    deps=[tU])
                tW = s.op("vector", lambda e, tmpC=tmpC: e.tensor_mul(
                    tmpC[:, :], tmpC[:, :], tmpC[:, :]), deps=[tW])
                tM = s.op("vector", lambda e, tmpB=tmpB, tmpC=tmpC: e.tensor_scalar(
                    tmpB[:, :], tmpC[:, :], 0.3, 0.85, ALU.mult, ALU.add),
                    deps=[tW])
                tDm = s.op("vector", lambda e, tmpB=tmpB, tmpC=tmpC: e.tensor_mul(
                    tmpC[:, :], tmpB[:, :], bankA[:, :]), deps=[tM])
                tR = s.op("scalar", lambda e, tmpB=tmpB, tmpC=tmpC: e.activation(
                    tmpB[:, :], tmpC[:, :], AF.Ln, bias=cc(C_EPS6),
                    scale=1.0), deps=[tDm])
                tR = s.op("scalar", lambda e, tmpB=tmpB: e.activation(
                    tmpB[:, :], tmpB[:, :], AF.Exp, bias=0.0,
                    scale=cc(C_M1)), deps=[tR])
                tF = s.op("vector", lambda e, tmpB=tmpB: e.tensor_mul(
                    bankF[:, :], bankF[:, :], tmpB[:, :]), deps=[tR] + tN)
                tP = s.op("scalar", lambda e: e.activation(
                    ptil[:, :], bankF[:, :], AF.Exp, bias=cc(C_M10), scale=1.0),
                    deps=[tF] + prevband.get("P", []))
                tSel = s.op("tensor", lambda e: e.matmul(
                    pS[:, :], SEL2[:, :], ptil[:, :], start=True, stop=True),
                    deps=[tP] + prevband.get("S", []))
                tRec = s.op("scalar", lambda e, tmpB=tmpB: e.activation(
                    tmpB[:, :], pS[:, :], AF.Ln, bias=cc(C_TINY),
                    scale=1.0), deps=[tSel])
                tAt0 = s.op("vector", lambda e, tmpB=tmpB, tmpC=tmpC: e.tensor_sub(
                    tmpC[:, :], bankF[:, :], tmpB[:, :]), deps=[tRec])
                tAt1 = s.op("scalar", lambda e, tmpC=tmpC: e.activation(
                    attnb[:, :], tmpC[:, :], AF.Exp, bias=cc(C_M10),
                    scale=1.0), deps=[tAt0] + prevband.get("At", []))
                tAt = s.op("vector", lambda e: e.tensor_mul(
                    attnb[:, :], attnb[:, :], tmask[:, :]), deps=[tAt1])

                tk_vg = []
                for sh in range(2):
                    for b2i in range(2):
                        for t2 in range(2):
                            tk_vg.append(s.dma("gpsimd", lambda e, p=p, sh=sh,
                                               b2i=b2i, t2=t2, vb=vbuf:
                                e.dma_start(
                                    out=AP(v_tg,
                                           (b2i * 64 + sh * 32 + 16 * t2) * 1024,
                                           [[1024, 16], [64, 16], [1, 64]]),
                                    in_=AP(vscr, vb + ((2 * p + sh) * 64 +
                                                       b2i * 32 + t2) * 1024,
                                           [[64, 16], [2048, 16], [1, 64]])),
                                deps=[ST[st]["vdump"], tk_mm2_last]))

                for sh in range(2):
                    def mm_2(te, sh=sh):
                        last = None
                        for b2i in range(2):
                            for f in range(16):
                                last = te.matmul(
                                    AP(pOT, (b2i * 64) * 512 + f * 32,
                                       [[512, 64], [1, 32]]),
                                    AP(v_tg, (b2i * 64 + sh * 32) * 1024 + f * 64,
                                       [[1024, 32], [1, 64]]),
                                    AP(attnb, (b2i * 64 + sh * 32) * 512 + f * 32,
                                       [[512, 32], [1, 32]]),
                                    start=True, stop=True,
                                    tile_position=(b2i * 64 + sh * 32, b2i * 64))
                        return last
                    godt4 = tk_odT[-4:] if tk_odT else (
                        [] if st == 0 else ST[st - 1]["odt4"])
                    tk_mm2_last = s.op("tensor", mm_2,
                                       deps=[tAt] + tk_vg + godt4 +
                                       (mm2_extra if (p == 0 and sh == 0) else []))
                    SB = 2 * p + sh
                    for b2i in range(2):
                        for hp in range(2):
                            tk_odT.append(s.op("vector",
                                lambda e, SB=SB, b2i=b2i, hp=hp: e.tensor_copy(
                                    AP(o_dT, (hp * 64) * (8 * T_ST) + SB * 64 + b2i * 32,
                                       [[8 * T_ST, 64], [2, 16], [1, 2], [T_ST, 8]]),
                                    AP(pOT, (b2i * 64) * 512 + hp,
                                       [[512, 64], [32, 16], [16, 2], [2, 8]])),
                                deps=[tk_mm2_last]))

                prevband.update({
                    ("rn", 0): tAcc[0],
                    ("rn", 1): tAcc[1],
                    ("B", p): [tAt0],
                    ("C", p): [tAt1],
                    "A": [tDm],
                    "F": [tAt0],
                    "P": [tSel, tk_mm2_last], "S": [tRec], "At": [tk_mm2_last],
                })
            ST[st]["mm_ab_last"] = ab[NSB - 1]
            ST[st]["mm2_last"] = tk_mm2_last
            ST[st]["odT"] = tk_odT
            ST[st]["odt4"] = tk_odT[-4:]
            ST[st]["vg_last"] = tk_vg[-1]
            ST[st]["prevband"] = prevband

        def pP5(st):
            xr = xres[st % 2]
            tk_afm = []
            tk_atm = []
            tmo = None
            hg2 = [] if st == 0 else ST[st - 1]["hg_last2"]
            for oc in range(8):
                pb_ao = pQK[oc % 2]

                def mm_o(te, oc=oc, pb_ao=pb_ao):
                    last = None
                    for h2 in range(8):
                        last = te.matmul(
                            pb_ao[:, :],
                            wout_sb[:, h2, oc * 128:oc * 128 + 128],
                            o_dT[:, h2, :],
                            start=(h2 == 0), stop=(h2 == 7))
                    return last
                dmo = ST[st]["odT"] + hg2 if oc < 2 else [tk_afm[oc - 2]]
                tmo = s.op("tensor", mm_o, deps=dmo)
                dslot = tk_atm[oc - 2:oc - 1] if oc >= 2 else \
                    ([] if st == 0 else [ST[st - 1]["ytm_last2"][oc]])
                tk_afm.append(s.op("scalar", lambda e, oc=oc, pb_ao=pb_ao:
                    e.activation(
                        afm[:, oc % 2, :], pb_ao[:, :], AF.Identity,
                        bias=bout_sb[:, oc:oc + 1], scale=1.0),
                    deps=[tmo] + dslot))
                gres = [] if (oc >= 2 or st == 0) else [ST[st - 1]["res2"][0],
                                                        ST[st - 1]["res2"][1]]
                tk_atm.append(s.dma("sync", lambda e, oc=oc:
                    e.dma_start_transpose(
                        AP(a_tm, oc * 128, [[PT * D, 128], [D, 2], [1, 128]]),
                        AP(afm, (oc % 2) * T_ST, [[2 * T_ST, 128], [128, 2], [1, 128]])),
                    deps=[tk_afm[oc]] + gres))
            tk_res1 = []
            for pt in range(PT):
                tk_res1.append(s.op("vector", lambda e, pt=pt, xr=xr:
                    e.tensor_add(
                        xr[:, pt, :], xr[:, pt, :], a_tm[:, pt, :]),
                    deps=tk_atm))
            ST[st]["mm_o_last"] = tmo
            ST[st]["res1"] = tk_res1

        def pP67(st):
            xr = xres[st % 2]
            st6x, sq4x, mv2x, acolx, bcolx = scrA
            tk_xhatT = []
            for pt in range(PT):
                c0 = 8 * pt

                def col(i, c0=c0):
                    return mv2x[:, c0 + i:c0 + i + 1]
                src = xr[:, pt, :]
                t1 = s.op("vector", lambda e, src=src: e.bn_stats(
                    st6x[:, 0:6], src[:, 0:512]), deps=[ST[st]["res1"][pt]])
                t1 = s.op("vector", lambda e, src=src: e.bn_stats(
                    st6x[:, 6:12], src[:, 512:1024]), deps=[t1])
                t1 = s.op("vector", lambda e, col=col: e.tensor_reduce(
                    col(0), AP(st6x, 1, [[12, 128], [3, 4]]), AX.X, ALU.add),
                    deps=[t1])
                t1 = s.op("vector", lambda e: e.tensor_mul(
                    sq4x[:, 0:4],
                    AP(st6x, 1, [[12, 128], [3, 4]]),
                    AP(st6x, 1, [[12, 128], [3, 4]])), deps=[t1])
                t1 = s.op("vector", lambda e, col=col: e.tensor_reduce(
                    col(1), sq4x[:, 0:4], AX.X, ALU.add), deps=[t1])
                t1 = s.op("vector", lambda e, col=col: e.tensor_reduce(
                    col(2), AP(st6x, 2, [[12, 128], [3, 4]]), AX.X, ALU.add),
                    deps=[t1])
                t1 = s.op("vector", lambda e, col=col: e.tensor_scalar_mul(
                    col(3), col(2), 1.0 / 1024.0), deps=[t1])
                t1 = s.op("vector", lambda e, col=col: e.scalar_tensor_tensor(
                    col(3), col(1), 0.25, col(3), ALU.mult, ALU.add), deps=[t1])
                t1 = s.op("vector", lambda e, col=col: e.scalar_tensor_tensor(
                    col(4), col(0), -0.0625, col(0), ALU.mult, ALU.mult),
                    deps=[t1])
                t1 = s.op("vector", lambda e, col=col: e.tensor_add(
                    col(5), col(3), col(4)), deps=[t1])
                t3 = s.op("scalar", lambda e, col=col: e.activation(
                    acolx[:, :], col(5), AF.Sqrt, bias=cc(C_EPS5),
                    scale=1.0), deps=[t1])
                t3 = s.op("vector", lambda e: e.reciprocal(
                    acolx[:, :], acolx[:, :]), deps=[t3])
                t3 = s.op("vector", lambda e, col=col: e.tensor_mul(
                    col(6), col(5), acolx[:, :]), deps=[t3])
                t3 = s.op("vector", lambda e, col=col: e.tensor_mul(
                    col(6), col(6), acolx[:, :]), deps=[t3])
                t3 = s.op("scalar", lambda e, col=col: e.activation(
                    col(7), col(6), AF.Sqrt, bias=cc(C_EPS5), scale=1.0),
                    deps=[t3])
                t3 = s.op("vector", lambda e, col=col: e.reciprocal(
                    col(7), col(7)), deps=[t3])
                t3 = s.op("vector", lambda e, col=col: e.tensor_mul(
                    acolx[:, :], acolx[:, :], col(7)), deps=[t3])
                t4 = s.op("vector", lambda e, col=col: e.scalar_tensor_tensor(
                    bcolx[:, :], col(0), -0.25, acolx[:, :],
                    ALU.mult, ALU.mult), deps=[t3])
                u5 = s.op("scalar", lambda e, pt=pt, src=src: e.activation(
                    a_tm[:, pt, :], src, AF.Identity, bias=bcolx[:, :],
                    scale=acolx[:, :]), deps=[t4])
                tk_xhatT.append(s.dma("scalar", lambda e, pt=pt:
                    e.dma_start_transpose(
                        AP(xhatT, pt * 128, [[8 * T_ST, 128], [T_ST, 8], [1, 128]]),
                        AP(a_tm, pt * D, [[PT * D, 128], [128, 8], [1, 128]])),
                    deps=[u5]))

            def mm_g(te):
                last = None
                for kc in range(8):
                    last = te.matmul(pS[0:8, 0:T_ST], wg_sb[:, kc, :],
                                     xhatT[:, kc, :], start=(kc == 0), stop=(kc == 7))
                return last
            tg = s.op("tensor", mm_g, deps=tk_xhatT)
            g0 = s.op("vector", lambda e: e.memset(lgp, -1e30), deps=[tg])
            g1 = s.op("vector", lambda e: e.tensor_copy(
                lgp[0:8, :], pS[0:8, 0:T_ST]), deps=[tg, g0])
            g2 = s.op("vector", lambda e: e.transpose(lgT, lgp), deps=[g1])
            g3 = s.op("vector", lambda e: e.reduce_max(
                mx8[:, :], AP(bankB, 0, [[512, 32], [32, 8], [1, 8]]), AX.X),
                deps=[g2])
            g4 = s.op("vector", lambda e: e.tensor_tensor(
                AP(bankC, 0, [[512, 32], [32, 8], [1, 8]]),
                AP(bankB, 0, [[512, 32], [32, 8], [1, 8]]),
                mx8[:, :, None].broadcast_to([32, 8, 8]), ALU.is_ge), deps=[g3])
            g5 = s.op("vector", lambda e: e.transpose(ohp, ohT), deps=[g4])
            g6 = s.op("vector", lambda e: e.tensor_copy(
                oh_em[:, :], AP(bankB, 32 * 512, [[512, 8], [1, T_ST]])),
                deps=[g5])
            pb_ = ST[st]["prevband"]
            pb_["A"] = pb_.get("A", []) + [g2]
            pb_[("B", 0)] = pb_.get(("B", 0), []) + [g6]
            pb_[("C", 0)] = pb_.get(("C", 0), []) + [g5]
            pb_["S"] = pb_.get("S", []) + [g1]
            ST[st]["xhatT"] = tk_xhatT
            ST[st]["g6"] = g6
            ST[st]["tg"] = tg

        def pP8(st, has_next):
            xr = xres[st % 2]
            tk_hm = []
            tk_hg = []
            pMs = [pOT, pS]
            for hc in range(16):
                e_ = hc // 2
                pb_h = pQK[hc % 2]
                pb_m = pMs[hc % 2]

                def mm_h(te, hc=hc, e_=e_, pb_h=pb_h, pb_m=pb_m):
                    for kc in range(8):
                        te.matmul(pb_h[:, :],
                                  w1_sb[:, kc, e_, (hc % 2) * 128:(hc % 2) * 128 + 128],
                                  xhatT[:, kc, :], start=(kc == 0), stop=(kc == 7))
                    return te.matmul(pb_m[:, 0:T_ST], sel8[:, e_, :],
                                     oh_em[:, :], start=True, stop=True)
                if hc < 2:
                    dh = ST[st]["xhatT"] + [ST[st]["g6"]]
                else:
                    dh = [tk_hm[hc - 2], tk_hg[hc - 2]]
                tmh = s.op("tensor", mm_h, deps=dh)
                hgo = (hc % 2) * T_ST
                th1 = s.op("scalar", lambda e, hc=hc, pb_h=pb_h, hgo=hgo:
                    e.activation(
                        AP(v_tg, hgo, [[1024, 128], [1, T_ST]]), pb_h[:, :],
                        AF.Gelu, bias=b1c_sb[:, hc:hc + 1], scale=1.0),
                    deps=[tmh])
                th2 = s.op("vector", lambda e, hc=hc, pb_m=pb_m, hgo=hgo:
                    e.tensor_mul(
                        hm_sb[:, hc, :], AP(v_tg, hgo, [[1024, 128], [1, T_ST]]),
                        pb_m[:, 0:T_ST]),
                    deps=[th1])
                tk_hg.append(th1)
                tk_hm.append(th2)
            pb_ = ST[st]["prevband"]
            pb_["S"] = pb_.get("S", []) + [tk_hm[15]]
            ST[st]["hm14"] = tk_hm[14]
            ST[st]["hm15"] = tk_hm[15]

            tk_yfm = []
            tk_ytm = []
            tmy = None
            for oc in range(8):
                py_sl = (oc % 2) * 512

                def mm_y(te, oc=oc, py_sl=py_sl):
                    for hc in range(16):
                        te.matmul(pV[:, py_sl:py_sl + T_ST],
                                  w2_sb[:, hc, oc * 128:oc * 128 + 128],
                                  hm_sb[:, hc, :], start=(hc == 0), stop=False)
                    return te.matmul(pV[:, py_sl:py_sl + T_ST],
                                     b2_sb[0:NE, oc * 128:oc * 128 + 128],
                                     oh_em[:, :], start=False, stop=True)
                if oc < 2:
                    dy = tk_hm + [ST[st]["g6"]]
                    if has_next:
                        dy = dy + (ST[st + 1]["evac_eA"] if oc == 0
                                   else ST[st + 1]["evac_eB"])
                else:
                    dy = [tk_yfm[oc - 2]]
                tmy = s.op("tensor", mm_y, deps=dy)
                dslot = tk_ytm[oc - 2:oc - 1] if oc >= 2 else []
                tk_yfm.append(s.op("scalar", lambda e, oc=oc, py_sl=py_sl:
                    e.activation(
                        afm[:, oc % 2, :], pV[:, py_sl:py_sl + T_ST],
                        AF.Identity, bias=0.0, scale=1.0),
                    deps=[tmy] + dslot))
                tk_ytm.append(s.dma("scalar", lambda e, oc=oc:
                    e.dma_start_transpose(
                        AP(a_tm, oc * 128, [[PT * D, 128], [D, 2], [1, 128]]),
                        AP(afm, (oc % 2) * T_ST, [[2 * T_ST, 128], [128, 2], [1, 128]])),
                    deps=[tk_yfm[oc]]))
            tk_res2 = []
            for pt in range(PT):
                tk_res2.append(s.op("vector", lambda e, pt=pt, xr=xr:
                    e.tensor_add(
                        xr[:, pt, :], xr[:, pt, :], a_tm[:, pt, :]),
                    deps=tk_ytm))
            ST[st]["hg_last2"] = tk_hg[-2:]
            ST[st]["mm_y_last"] = tmy
            ST[st]["yfm"] = tk_yfm
            ST[st]["ytm_last2"] = {0: tk_ytm[-2], 1: tk_ytm[-1]}
            ST[st]["res2"] = tk_res2

        def pP9(st):
            xr = xres[st % 2]
            ST[st]["store"] = s.dma("gpsimd", lambda e, o=st * T_ST, xr=xr:
                e.dma_start(
                    out=y_out.ap()[o:o + T_ST, :]
                        .rearrange("(pt p) d -> p pt d", p=128),
                    in_=xr.ap()), deps=ST[st]["res2"])

        pP0(0); pP1a(0); pP1b(0); pP2(0); pP2b(0); pP3(0)
        for st in range(n_st):
            has_next = st + 1 < n_st
            if has_next:
                pP0(st + 1)
            pP4(st)
            if has_next:
                pP1a(st + 1)
            pP5(st)
            if has_next:
                pP1b(st + 1)
                pP2(st + 1)
            pP67(st)
            if has_next:
                pP3(st + 1)
            pP8(st, has_next)
            if has_next:
                pP2b(st + 1)
            pP9(st)

        tk_store_last = ST[n_st - 1]["store"]
        if debug:
            x_res = xres[(n_st - 1) % 2]
            dbg_map = {"xlnT": xlnT2, "qmov": qmov, "kstat": kstat,
                       "bankA": bankA, "bankF": bankF, "ptil": ptil,
                       "attnb": attnb, "o_dT": o_dT, "a_tm": a_tm,
                       "oh_em": oh_em, "hm_sb": hm_sb,
                       "x_res": x_res, "v_tg": v_tg, "smalls": smalls,
                       "hmT": hmT}
            last = ST[n_st - 1]
            for nm in dbg_outs:
                t_ = dbg_map[nm]
                tk_store_last = s.dma("gpsimd", lambda e, nm=nm, t_=t_:
                    e.dma_start(out=dbg_outs[nm].ap(),
                                in_=AP(t_, 0, [[int(np.prod(t_.shape[1:])),
                                                t_.shape[0]],
                                               [1, int(np.prod(t_.shape[1:]))]])),
                    deps=[tk_store_last, last["mm2_last"], last["res2"][0],
                          last["res2"][1]])

        s.q["gpsimd"].append(("w", s._deps([tk_store_last]), s.cnt["gpsimd"]))

        incmap = {}
        for name in ENGS:
            nincs = 0
            m = {}
            for item in s.q[name]:
                if item[0] == "i":
                    if (name, item[2]) in s.used:
                        nincs += 1
                    m[item[2]] = nincs
            incmap[name] = m

        def emitter(name):
            def run(eng):
                seen = {}
                m_self = incmap[name]
                for item in s.q[name]:
                    if item[0] == "w":
                        emitted = item[2]
                        for (de, dn) in item[1]:
                            if de.endswith("_dma"):
                                val = dn
                            else:
                                if de == name and dn <= emitted - 16:
                                    continue
                                val = incmap[de][dn]
                            if seen.get(de, -1) >= val:
                                continue
                            seen[de] = val
                            eng.wait_ge(SEM[de], val)
                    elif item[0] == "i":
                        inst = item[1](eng)
                        if (name, item[2]) in s.used:
                            inst.then_inc(SEM[name], 1)
                    else:
                        item[1](eng).then_inc(SEM[name + "_dma"], 16)
            return run

        @block.tensor
        def _(e):
            emitter("tensor")(e)

        @block.vector
        def _(e):
            emitter("vector")(e)

        @block.scalar
        def _(e):
            emitter("scalar")(e)

        @block.sync
        def _(e):
            emitter("sync")(e)

        @block.gpsimd
        def _(e):
            emitter("gpsimd")(e)

    return nc


_NC_CACHE = {}


def kernel(L0_attn_x, L0_attn_W_qkv, L0_attn_b_qkv, L0_attn_W_out, L0_attn_b_out,
           L0_ffn_Wg, L0_ffn_ln_scale, L0_ffn_ln_bias, L0_ffn_W1, L0_ffn_b1,
           L0_ffn_W2, L0_ffn_b2):
    x = np.ascontiguousarray(np.asarray(L0_attn_x, dtype=np.float32)
                             .reshape(-1, D))
    n_tokens = x.shape[0]
    per = n_tokens // N_CORES

    key = per
    if key not in _NC_CACHE:
        _NC_CACHE[key] = build_program(per)
    nc = _NC_CACHE[key]

    tmaskc = np.zeros((128, 512), np.float32)
    for b2i in range(2):
        for sh in range(2):
            for t2 in range(2):
                r = b2i * 64 + sh * 32 + t2 * 16
                for f in range(16):
                    tmaskc[r:r + 16,
                           f * 32 + t2 * 16:f * 32 + t2 * 16 + 16] = 1.0
    sel2c = np.zeros((128, 128), np.float32)
    for r0 in range(0, 128, 16):
        sel2c[r0:r0 + 16, r0:r0 + 16] = 1.0
    sel8c = np.zeros((NE, NE, 128), np.float32)
    for e_ in range(NE):
        sel8c[e_, e_, :] = 1.0
    onesc = np.ones((4096,), np.float32)

    w = {
        "tmaskc": tmaskc, "sel2c": sel2c, "sel8c": sel8c, "onesc": onesc,
        "wqkv": np.ascontiguousarray(np.asarray(L0_attn_W_qkv, np.float32)),
        "bqkv": np.ascontiguousarray(np.asarray(L0_attn_b_qkv, np.float32)),
        "wout": np.ascontiguousarray(np.asarray(L0_attn_W_out, np.float32)),
        "bout": np.ascontiguousarray(np.asarray(L0_attn_b_out, np.float32)),
        "wg": np.ascontiguousarray(np.asarray(L0_ffn_Wg, np.float32)),
        "lns": np.ascontiguousarray(np.asarray(L0_ffn_ln_scale, np.float32)),
        "lnb": np.ascontiguousarray(np.asarray(L0_ffn_ln_bias, np.float32)),
        "w1": np.ascontiguousarray(np.asarray(L0_ffn_W1, np.float32)),
        "b1": np.ascontiguousarray(np.asarray(L0_ffn_b1, np.float32)),
        "w2": np.ascontiguousarray(np.asarray(L0_ffn_W2, np.float32)),
        "b2": np.ascontiguousarray(np.asarray(L0_ffn_b2, np.float32)),
    }
    in_maps = []
    for c in range(N_CORES):
        m = {"x": x[c * per:(c + 1) * per]}
        m.update(w)
        in_maps.append(m)

    res = run_bass_kernel_spmd(nc, in_maps, core_ids=list(range(N_CORES)))
    out = np.concatenate([r["y"] for r in res.results], axis=0)
    return out.reshape(np.asarray(L0_attn_x).shape).astype(np.float32)


# version 18 (final)
# speedup vs baseline: 1.2146x; 1.0009x over previous
import numpy as np
import concourse.bass as bass
import concourse.mybir as mybir
from concourse.bass_utils import run_bass_kernel_spmd

F32 = mybir.dt.float32
BF16 = mybir.dt.bfloat16
AF = mybir.ActivationFunctionType
ALU = mybir.AluOpType
AX = mybir.AxisListType
AP = bass.AP

N_CORES = 8
D = 1024
NH, HD, NE, EH = 16, 64, 8, 256
T_ST = 256
PT = 2
NSB = 4

ENGS = ("tensor", "vector", "scalar", "sync", "gpsimd")


class Sched:

    def __init__(self):
        self.q = {e: [] for e in ENGS}
        self.cnt = {e: 0 for e in ENGS}
        self.dcnt = {e: 0 for e in ENGS}
        self.used = set()

    def _deps(self, deps):
        dd = tuple(d for d in deps if d is not None)
        for d in dd:
            self.used.add(d)
        return dd

    def op(self, eng, fn, deps=()):
        self.q[eng].append(("w", self._deps(deps), self.cnt[eng]))
        self.q[eng].append(("i", fn, self.cnt[eng] + 1))
        self.cnt[eng] += 1
        return (eng, self.cnt[eng])

    def dma(self, eng, fn, deps=()):
        self.q[eng].append(("w", self._deps(deps), self.cnt[eng]))
        self.q[eng].append(("d", fn))
        self.dcnt[eng] += 1
        return (eng + "_dma", 16 * self.dcnt[eng])


def build_program(n_tokens, debug=False):
    n_st = n_tokens // T_ST
    nc = bass.Bass("TRN2", target_bir_lowering=False,
                   detect_race_conditions=False)

    x_in = nc.dram_tensor("x", [n_tokens, D], F32, kind="ExternalInput")
    wqkv_d = nc.dram_tensor("wqkv", [D, 3 * D], F32, kind="ExternalInput")
    bqkv_d = nc.dram_tensor("bqkv", [3 * D], F32, kind="ExternalInput")
    wout_d = nc.dram_tensor("wout", [D, D], F32, kind="ExternalInput")
    bout_d = nc.dram_tensor("bout", [D], F32, kind="ExternalInput")
    wg_d = nc.dram_tensor("wg", [NE, D], F32, kind="ExternalInput")
    lns_d = nc.dram_tensor("lns", [NE, D], F32, kind="ExternalInput")
    lnb_d = nc.dram_tensor("lnb", [NE, D], F32, kind="ExternalInput")
    w1_d = nc.dram_tensor("w1", [NE, D, EH], F32, kind="ExternalInput")
    b1_d = nc.dram_tensor("b1", [NE, EH], F32, kind="ExternalInput")
    w2_d = nc.dram_tensor("w2", [NE, EH, D], F32, kind="ExternalInput")
    b2_d = nc.dram_tensor("b2", [NE, D], F32, kind="ExternalInput")
    tmask_d = nc.dram_tensor("tmaskc", [128, 512], F32, kind="ExternalInput")
    sel2_d = nc.dram_tensor("sel2c", [128, 128], F32, kind="ExternalInput")
    sel8_d = nc.dram_tensor("sel8c", [NE, NE, 128], F32, kind="ExternalInput")
    ones_d = nc.dram_tensor("onesc", [4096], F32, kind="ExternalInput")
    y_out = nc.dram_tensor("y", [n_tokens, D], F32, kind="ExternalOutput")
    vscr = nc.dram_tensor("vscr", [2 * T_ST * D], BF16, kind="Internal")
    hscr = nc.dram_tensor("hscr", [4, 128 * 64], BF16, kind="Internal")
    dbg_specs = [
        ("xlnT", [128, 8 * T_ST], BF16), ("qmov", [67, 32 * 128], BF16),
        ("kstat", [67, 64 * 128], BF16), ("bankA", [128, 512], F32),
        ("bankF", [128, 512], F32), ("ptil", [128, 512], BF16),
        ("attnb", [128, 512], BF16), ("o_dT", [128, 8 * T_ST], BF16),
        ("a_tm", [128, PT * D], BF16),         ("oh_em", [NE, T_ST], BF16), ("hm_sb", [128, 16 * T_ST], BF16),
        ("x_res", [128, PT * D], BF16),
        ("smalls", [128, 64], BF16), ("hmT", [128, 64], BF16),
    ]
    dbg_outs = {}
    if debug:
        for nm, shp, dt_ in dbg_specs:
            dbg_outs[nm] = nc.dram_tensor("dbg_" + nm, shp, dt_,
                                          kind="ExternalOutput")

    s = Sched()

    from contextlib import ExitStack
    with ExitStack() as ctx, nc.Block() as block:
        wqkv_sb = ctx.enter_context(nc.sbuf_tensor("wqkv_sb", [128, 8, 3 * D], BF16))
        wout_sb = ctx.enter_context(nc.sbuf_tensor("wout_sb", [128, 8, D], BF16))
        w1_sb = ctx.enter_context(nc.sbuf_tensor("w1_sb", [128, 8, NE, EH], BF16))
        w2_sb = ctx.enter_context(nc.sbuf_tensor("w2_sb", [128, 16, D], BF16))
        wg_sb = ctx.enter_context(nc.sbuf_tensor("wg_sb", [128, 8, NE], BF16))
        lns_sb = ctx.enter_context(nc.sbuf_tensor("lns_sb", [128, 8, NE], BF16))
        lnb_sb = ctx.enter_context(nc.sbuf_tensor("lnb_sb", [128, 8, NE], BF16))
        b1r_sb = ctx.enter_context(nc.sbuf_tensor("b1r_sb", [128, 16], F32))
        b1c_sb = ctx.enter_context(nc.sbuf_tensor("b1c_sb", [128, 16], F32))
        b2_sb = ctx.enter_context(nc.sbuf_tensor("b2_sb", [33, D], BF16))
        bqm2_sb = ctx.enter_context(nc.sbuf_tensor("bqm2_sb", [64, 16], F32))
        bk_lo = ctx.enter_context(nc.sbuf_tensor("bk_lo", [64, 16], F32))
        bout_sb = ctx.enter_context(nc.sbuf_tensor("bout_sb", [128, 8], F32))
        ones_col = ctx.enter_context(nc.sbuf_tensor("ones_col", [33, 128], BF16))
        sel8 = ctx.enter_context(nc.sbuf_tensor("sel8", [NE, NE, 128], BF16))
        SEL2 = ctx.enter_context(nc.sbuf_tensor("SEL2", [128, 128], BF16))
        tmask = ctx.enter_context(nc.sbuf_tensor("tmask", [128, 512], BF16))
        consts = ctx.enter_context(nc.sbuf_tensor("consts", [128, 12], F32))
        x_res0 = ctx.enter_context(nc.sbuf_tensor("x_res0", [128, PT, D], BF16))
        x_res1 = ctx.enter_context(nc.sbuf_tensor("x_res1", [128, PT, D], BF16))
        xlnT2 = ctx.enter_context(nc.sbuf_tensor("xlnT2", [128, 8, T_ST], BF16))
        st6b = ctx.enter_context(nc.sbuf_tensor("st6b", [128, 12], F32))
        mv2b = ctx.enter_context(nc.sbuf_tensor("mv2b", [128, 16], F32))
        acolb = ctx.enter_context(nc.sbuf_tensor("acolb", [128, 1], F32))
        bcolb = ctx.enter_context(nc.sbuf_tensor("bcolb", [128, 1], F32))
        sq4b = ctx.enter_context(nc.sbuf_tensor("sq4b", [128, 4], F32))
        sq4p = ctx.enter_context(nc.sbuf_tensor("sq4p", [128, 16], F32))
        kstat = ctx.enter_context(nc.sbuf_tensor("kstat", [67, 64, 128], BF16))
        qmov = ctx.enter_context(nc.sbuf_tensor("qmov", [67, 32, 128], BF16))
        v_tg = ctx.enter_context(nc.sbuf_tensor("v_tg", [128, 16, 64], BF16))
        sq4 = ctx.enter_context(nc.sbuf_tensor("sq4", [128, 16], F32))
        smalls = ctx.enter_context(nc.sbuf_tensor("smalls", [128, 64], BF16))
        hmT = ctx.enter_context(nc.sbuf_tensor("hmT", [128, 64], BF16))
        bankA = ctx.enter_context(nc.sbuf_tensor("bankA", [128, 512], F32))
        bankD = ctx.enter_context(nc.sbuf_tensor("bankD", [128, 512], F32))
        bankE = ctx.enter_context(nc.sbuf_tensor("bankE", [128, 512], F32))
        bankB = ctx.enter_context(nc.sbuf_tensor("bankB", [128, 512], F32))
        bankC = ctx.enter_context(nc.sbuf_tensor("bankC", [128, 512], F32))
        bankF = ctx.enter_context(nc.sbuf_tensor("bankF", [128, 512], F32))
        ptil = ctx.enter_context(nc.sbuf_tensor("ptil", [128, 512], BF16))
        attnb = ptil
        o_dT = ctx.enter_context(nc.sbuf_tensor("o_dT", [128, 8, T_ST], BF16))
        afm = ctx.enter_context(nc.sbuf_tensor("afm", [128, 2, T_ST], BF16))
        a_tm = ctx.enter_context(nc.sbuf_tensor("a_tm", [128, PT, D], BF16))
        xhatT = ctx.enter_context(nc.sbuf_tensor("xhatT", [128, 8, T_ST], BF16))
        st6 = ctx.enter_context(nc.sbuf_tensor("st6", [128, 12], F32))
        mv2 = ctx.enter_context(nc.sbuf_tensor("mv2", [128, 16], F32))
        acol = ctx.enter_context(nc.sbuf_tensor("acol", [128, 1], F32))
        bcol = ctx.enter_context(nc.sbuf_tensor("bcol", [128, 1], F32))
        mx8 = ctx.enter_context(nc.sbuf_tensor("mx8", [32, 8], F32))
        oh_em = ctx.enter_context(nc.sbuf_tensor("oh_em", [NE, T_ST], BF16))
        hm_sb = ctx.enter_context(nc.sbuf_tensor("hm_sb", [128, 16, T_ST], BF16))
        lgp = bankA[0:32, 0:T_ST]
        lgT = bankB[0:32, 0:T_ST]
        ohT = bankC[0:32, 0:T_ST]
        ohp = bankB[32:64, 0:T_ST]
        pQK0 = ctx.enter_context(nc.psum_tensor("pQK0", [128, T_ST], F32))
        pQK1 = ctx.enter_context(nc.psum_tensor("pQK1", [128, T_ST], F32))
        pV = ctx.enter_context(nc.psum_tensor("pV", [128, D], F32))
        pRN = ctx.enter_context(nc.psum_tensor("pRN", [128, 512], F32))
        pRN2 = ctx.enter_context(nc.psum_tensor("pRN2", [128, 512], F32))
        pS = ctx.enter_context(nc.psum_tensor("pS", [128, 512], F32))
        pOT = ctx.enter_context(nc.psum_tensor("pOT", [128, 512], F32))
        sm_t = ctx.enter_context(nc.semaphore("sm_t"))
        sm_v = ctx.enter_context(nc.semaphore("sm_v"))
        sm_s = ctx.enter_context(nc.semaphore("sm_s"))
        sm_sy = ctx.enter_context(nc.semaphore("sm_sy"))
        sm_g = ctx.enter_context(nc.semaphore("sm_g"))
        dm_sy = ctx.enter_context(nc.semaphore("dm_sy"))
        dm_g = ctx.enter_context(nc.semaphore("dm_g"))
        dm_sc = ctx.enter_context(nc.semaphore("dm_sc"))

        SEM = {"tensor": sm_t, "vector": sm_v, "scalar": sm_s,
               "sync": sm_sy, "gpsimd": sm_g,
               "sync_dma": dm_sy, "gpsimd_dma": dm_g, "scalar_dma": dm_sc}

        (C_EPS5, C_EPS6, C_PI2, C_M10, C_TINY, C_2, C_Q25, C_M2,
         C_M1, C_MQ25, C_HALF) = range(11)

        def cc(i, n=128):
            return consts[0:n, i:i + 1]

        tk_w = []

        def gld(dst_ap, src_ap):
            tk_w.append(s.dma("gpsimd",
                              lambda e, d=dst_ap, sr=src_ap: e.dma_start(
                                  out=d, in_=sr, allow_slow_non_contiguous=True)))

        for h in range(NH):
            for sec in range(2):
                gld(wqkv_sb[:, :, h * 128 + sec * 64:h * 128 + sec * 64 + 64],
                    AP(wqkv_d, sec * 1024 + h * 64,
                       [[3 * D, 128], [128 * 3 * D, 8], [1, 64]]))
        gld(wqkv_sb[:, :, 2 * D:3 * D],
            AP(wqkv_d, 2 * D, [[3 * D, 128], [128 * 3 * D, 8], [1, D]]))
        gld(wout_sb.ap(), wout_d.ap().rearrange("(kc p) o -> p kc o", p=128))
        for e_ in range(NE):
            gld(w1_sb[:, :, e_, :],
                w1_d.ap()[e_].rearrange("(kc p) hh -> p kc hh", p=128))
        gld(w2_sb.ap(), w2_d.ap().rearrange("e (hc p) o -> p (e hc) o", p=128))
        for e_ in range(NE):
            gld(wg_sb[:, :, e_], wg_d.ap()[e_].rearrange("(kc p) -> p kc", p=128))
            gld(lns_sb[:, :, e_], lns_d.ap()[e_].rearrange("(kc p) -> p kc", p=128))
            gld(lnb_sb[:, :, e_], lnb_d.ap()[e_].rearrange("(kc p) -> p kc", p=128))
        gld(b1r_sb.ap(), b1_d.ap().rearrange("e (hc p) -> p (e hc)", p=128))
        gld(b2_sb[0:NE, :], b2_d.ap())
        gld(bqm2_sb.ap(), bqkv_d.ap()[0:D].rearrange("(h d) -> d h", d=64))
        gld(bk_lo.ap(), bqkv_d.ap()[D:2 * D].rearrange("(h d) -> d h", d=64))
        gld(b2_sb[32:33, :], bqkv_d.ap()[2 * D:3 * D][None, :])
        gld(bout_sb.ap(), bout_d.ap().rearrange("(oc p) -> p oc", p=128))

        tk_ms = []

        def ms(ap_, val):
            tk_ms.append(s.op("vector", lambda e, a=ap_, v=val: e.memset(a, v)))

        ms(cc(C_EPS5), 1e-5)
        ms(cc(C_EPS6), 1e-6)
        ms(cc(C_PI2), float(np.pi / 2))
        ms(cc(C_M10), -10.0)
        ms(cc(C_TINY), 1e-9)
        ms(cc(C_2), 2.0)
        ms(cc(C_Q25), 0.25)
        ms(cc(C_M2), -2.0)
        ms(cc(C_M1), -1.0)
        ms(cc(C_MQ25), -0.25)
        ms(cc(C_HALF), 0.5)
        ms(ones_col[32:33, :], 1.0)
        gld(sel8.ap(), sel8_d.ap())
        gld(SEL2.ap(), sel2_d.ap())
        gld(tmask.ap(), tmask_d.ap())
        ms(kstat.ap(), 0.0)
        tk_w.append(s.dma("gpsimd", lambda e: e.dma_start(
            out=kstat[66:67, 0:32, :],
            in_=ones_d.ap().rearrange("(a c b) -> a c b", a=1, c=32))))
        ms(qmov.ap(), 0.0)
        ms(qmov[64:65, :, :], 1.0)
        ms(bankF.ap(), 0.0)

        def mk_b1c(te):
            last = None
            for hc in range(16):
                for kc in range(8):
                    last = te.matmul(
                        pQK0[:, hc:hc + 1],
                        w1_sb[:, kc, hc // 2, (hc % 2) * 128:(hc % 2) * 128 + 128],
                        lnb_sb[:, kc, hc // 2:hc // 2 + 1],
                        start=(kc == 0), stop=(kc == 7))
            return last
        tk_b1cm = s.op("tensor", mk_b1c, deps=tk_w)
        tk_b1c = s.op("vector", lambda e: e.tensor_add(
            b1c_sb[:, :], pQK0[:, 0:16], b1r_sb[:, :]), deps=[tk_b1cm])
        tk_w1f = s.op("vector", lambda e: e.tensor_mul(
            w1_sb.ap(), w1_sb.ap(),
            lns_sb[:, :, :, None].broadcast_to([128, 8, NE, EH])),
            deps=[tk_b1cm])

        setup_done = tk_w + tk_ms + [tk_b1c, tk_w1f]

        pQK = [pQK0, pQK1]
        pRNs = [pRN, pRN2]
        xres = [x_res0, x_res1]
        PVOFF = [0, 512]
        hsc_guard = {}
        last_tdmp = [None]
        ST = [dict() for _ in range(n_st)]

        scrA = (st6, sq4, mv2, acol, bcol)
        scrB = (st6b, sq4b, mv2b, acolb, bcolb)

        def layer_norm(scr, slot, src, dst, dep0, extra=()):
            st6x, sq4x, mv2x, acolx, bcolx = scr
            c0 = 8 * slot

            def col(i):
                return mv2x[:, c0 + i:c0 + i + 1]
            t1 = s.op("vector", lambda e: e.bn_stats(st6x[:, 0:6], src[:, 0:512]),
                      deps=dep0)
            t1 = s.op("vector", lambda e: e.bn_stats(st6x[:, 6:12], src[:, 512:1024]),
                      deps=[t1])
            t1 = s.op("vector", lambda e: e.tensor_reduce(
                col(0), AP(st6x, 1, [[12, 128], [3, 4]]), AX.X, ALU.add),
                deps=[t1])
            t1 = s.op("vector", lambda e: e.tensor_mul(
                sq4x[:, 0:4],
                AP(st6x, 1, [[12, 128], [3, 4]]),
                AP(st6x, 1, [[12, 128], [3, 4]])), deps=[t1])
            t1 = s.op("vector", lambda e: e.tensor_reduce(
                col(1), sq4x[:, 0:4], AX.X, ALU.add), deps=[t1])
            t1 = s.op("vector", lambda e: e.tensor_reduce(
                col(2), AP(st6x, 2, [[12, 128], [3, 4]]), AX.X, ALU.add),
                deps=[t1])
            t1 = s.op("vector", lambda e: e.tensor_scalar_mul(
                col(3), col(2), 1.0 / 1024.0), deps=[t1])
            t1 = s.op("vector", lambda e: e.scalar_tensor_tensor(
                col(3), col(1), 0.25, col(3), ALU.mult, ALU.add), deps=[t1])
            t1 = s.op("vector", lambda e: e.scalar_tensor_tensor(
                col(4), col(0), -0.0625, col(0), ALU.mult, ALU.mult),
                deps=[t1])
            t1 = s.op("vector", lambda e: e.tensor_add(
                col(5), col(3), col(4)), deps=[t1])
            t3 = s.op("scalar", lambda e: e.activation(
                acolx[:, :], col(5), AF.Sqrt, bias=cc(C_EPS5),
                scale=1.0), deps=[t1])
            t3 = s.op("vector", lambda e: e.reciprocal(
                acolx[:, :], acolx[:, :]), deps=[t3])
            t4 = s.op("vector", lambda e: e.scalar_tensor_tensor(
                bcolx[:, :], col(0), -0.25, acolx[:, :],
                ALU.mult, ALU.mult), deps=[t3])
            return s.op("scalar", lambda e: e.activation(
                dst, src, AF.Identity, bias=bcolx[:, :], scale=acolx[:, :]),
                deps=[t4] + list(extra))

        def pP0(st):
            xr = xres[st % 2]
            dep = list(setup_done) if st == 0 else (
                [] if st == 1 else [ST[st - 2]["store"]])
            ST[st]["x"] = s.dma("gpsimd", lambda e, o=st * T_ST, xr=xr:
                e.dma_start(
                    out=xr.ap(),
                    in_=x_in.ap()[o:o + T_ST, :]
                        .rearrange("(pt p) d -> p pt d", p=128)),
                deps=dep)

        def ln_stats(scr, slot, src, dep0):
            st6x, sq4x, mv2x, acolx, bcolx = scr
            c0 = 8 * slot

            def col(i):
                return mv2x[:, c0 + i:c0 + i + 1]
            t1 = s.op("vector", lambda e: e.bn_stats(st6x[:, 0:6], src[:, 0:512]),
                      deps=dep0)
            t1 = s.op("vector", lambda e: e.bn_stats(st6x[:, 6:12], src[:, 512:1024]),
                      deps=[t1])
            t1 = s.op("vector", lambda e: e.tensor_reduce(
                col(0), AP(st6x, 1, [[12, 128], [3, 4]]), AX.X, ALU.add),
                deps=[t1])
            t1 = s.op("vector", lambda e: e.tensor_mul(
                sq4x[:, 0:4],
                AP(st6x, 1, [[12, 128], [3, 4]]),
                AP(st6x, 1, [[12, 128], [3, 4]])), deps=[t1])
            t1 = s.op("vector", lambda e: e.tensor_reduce(
                col(1), sq4x[:, 0:4], AX.X, ALU.add), deps=[t1])
            t1 = s.op("vector", lambda e: e.tensor_reduce(
                col(2), AP(st6x, 2, [[12, 128], [3, 4]]), AX.X, ALU.add),
                deps=[t1])
            t1 = s.op("vector", lambda e: e.tensor_scalar_mul(
                col(3), col(2), 1.0 / 1024.0), deps=[t1])
            t1 = s.op("vector", lambda e: e.scalar_tensor_tensor(
                col(3), col(1), 0.25, col(3), ALU.mult, ALU.add), deps=[t1])
            t1 = s.op("vector", lambda e: e.scalar_tensor_tensor(
                col(4), col(0), -0.0625, col(0), ALU.mult, ALU.mult),
                deps=[t1])
            t1 = s.op("vector", lambda e: e.tensor_add(
                col(5), col(3), col(4)), deps=[t1])
            t3 = s.op("scalar", lambda e: e.activation(
                col(6), col(5), AF.Sqrt, bias=cc(C_EPS5),
                scale=1.0), deps=[t1])
            t3 = s.op("vector", lambda e: e.reciprocal(
                col(6), col(6)), deps=[t3])
            t4 = s.op("vector", lambda e: e.scalar_tensor_tensor(
                col(7), col(0), -0.25, col(6),
                ALU.mult, ALU.mult), deps=[t3])
            return t4

        def pP1a(st):
            xr = xres[st % 2]
            t4s = []
            for pt in range(PT):
                dep0 = [ST[st]["x"]]
                if pt == 0 and st > 0:
                    dep0 += [ST[st - 1]["ln1_last"]]
                t4s.append(ln_stats(scrB, pt, xr[:, pt, :], dep0))
            ST[st]["ln1t4"] = t4s

        def pP1b(st):
            xr = xres[st % 2]
            tk_xlnT = []
            t5 = None
            for pt in range(PT):
                extra = [ST[st]["ln1t4"][pt]]
                if st > 0 and pt == 0:
                    extra += [ST[st - 1]["mm_o_last"]]
                c0 = 8 * pt
                t5 = s.op("scalar", lambda e, pt=pt, xr=xr, c0=c0: e.activation(
                    AP(o_dT, pt * D, [[2048, 128], [1, D]]), xr[:, pt, :],
                    AF.Identity, bias=mv2b[:, c0 + 7:c0 + 8],
                    scale=mv2b[:, c0 + 6:c0 + 7]),
                    deps=extra)
                gx = [] if st == 0 else [ST[st - 1]["mm_v_last"]]
                tk_xlnT.append(s.dma("sync", lambda e, pt=pt:
                    e.dma_start_transpose(
                        AP(xlnT2, pt * 128, [[8 * T_ST, 128], [T_ST, 8], [1, 128]]),
                        AP(o_dT, pt * D, [[2048, 128], [128, 8], [1, 128]])),
                    deps=[t5] + gx))
            ST[st]["ln1_last"] = t5
            ST[st]["xlnT"] = tk_xlnT

        def pP2(st):
            if st == 0:
                ev_tok = [None, None]
            else:
                ev_tok = [list(ST[st - 1]["evac_eA"]),
                          list(ST[st - 1]["evac_eB"])]
            gpv = [] if st == 0 else list(ST[st - 1]["vtm"])
            gab = [] if st == 0 else [ST[st - 1]["mm_ab_last"]]
            tk_qk_evac = []
            for h in range(NH):
                off = PVOFF[h % 2]

                def mm_pair(te, h=h, off=off):
                    last = None
                    for kc in range(8):
                        last = te.matmul(
                            AP(pV, off, [[1024, 128], [1, 256]]),
                            wqkv_sb[:, kc, h * 128:h * 128 + 128],
                            xlnT2[:, kc, :],
                            start=(kc == 0), stop=(kc == 7))
                    return last
                dd = (ST[st]["xlnT"] if h < 2 else []) + \
                    (list(ev_tok[h % 2]) if ev_tok[h % 2] else []) + \
                    (gpv if h < 2 else [])
                tmm = s.op("tensor", mm_pair, deps=dd)
                e1 = s.op("vector", lambda e, h=h, off=off: e.tensor_scalar(
                    AP(qmov, h * 128, [[32 * 128, 64], [16 * 128, 2], [1, 128]]),
                    AP(pV, off, [[1024, 64], [1, 2], [2, 128]]),
                    -2.0, bqm2_sb[:, h:h + 1], ALU.mult, ALU.add),
                    deps=[tmm] + (gab if h < 2 else []))
                e2 = s.op("vector", lambda e, h=h, off=off: e.tensor_scalar_add(
                    AP(kstat, h * 128, [[64 * 128, 64], [16 * 128, 2], [1, 128]]),
                    AP(pV, 64 * 1024 + off, [[1024, 64], [1, 2], [2, 128]]),
                    bk_lo[:, h:h + 1]),
                    deps=[tmm] + (gab if h < 2 else []))
                ev_tok[h % 2] = [e1, e2]
                tk_qk_evac += [e1, e2]
            ST[st]["evac_eA"] = ev_tok[0]
            ST[st]["evac_eB"] = ev_tok[1]
            ST[st]["evac_all"] = tk_qk_evac

        def pP2b(st):
            tk_vtm = []
            tmmv = None
            for pt in range(PT):
                def mm_v(te, pt=pt):
                    last = None
                    for j in range(2):
                        for kc in range(8):
                            last = te.matmul(
                                pV[:, j * 512:j * 512 + 512],
                                xlnT2[:, kc, pt * 128:pt * 128 + 128],
                                wqkv_sb[:, kc, 2 * D + j * 512:2 * D + j * 512 + 512],
                                start=(kc == 0), stop=False)
                        last = te.matmul(
                            pV[:, j * 512:j * 512 + 512],
                            ones_col[32:33, :],
                            b2_sb[32:33, j * 512:j * 512 + 512],
                            start=False, stop=True)
                    return last
                if tk_vtm:
                    dpv = [tk_vtm[-1]]
                else:
                    dpv = ST[st]["evac_all"][-2:] + (
                        [] if st == 0 else ST[st - 1]["yfm"][-2:])
                tmmv = s.op("tensor", mm_v, deps=ST[st]["xlnT"] + dpv)
                gv = [] if st == 0 else [ST[st - 1]["mm_y_last"]]
                tk_vtm.append(s.op("scalar", lambda e, pt=pt: e.activation(
                    AP(hm_sb, pt * 1024, [[4096, 128], [1, 1024]]),
                    pV[:, :], AF.Identity, bias=0.0, scale=1.0),
                    deps=[tmmv] + (gv if pt == 0 else [])))
            ST[st]["vtm"] = tk_vtm
            ST[st]["mm_v_last"] = tmmv
            vbuf = (st % 2) * T_ST * D
            gvd = [] if st < 2 else [ST[st - 2]["vg_last"]]
            ST[st]["vdump"] = s.dma("gpsimd", lambda e, vb=vbuf: e.dma_start(
                out=AP(vscr, vb, [[1024, 128], [131072, 2], [1, 1024]]),
                in_=AP(hm_sb, 0, [[4096, 128], [1024, 2], [1, 1024]])),
                deps=tk_vtm + gvd)

        def pP3(st):
            tk_rows = []
            evac = ST[st]["evac_all"]
            for par in range(2):
                godt = ([] if st == 0 else [ST[st - 1]["mm_o_last"]]) \
                    if par == 0 else []
                tq_x = [s.dma("sync", lambda e, par=par:
                    e.dma_start_transpose(
                        AP(o_dT, 0, [[2048, 128], [64, 16], [1, 64]]),
                        AP(qmov, (par * 16) * 128,
                           [[32 * 128, 64], [128, 16], [1, 128]])),
                    deps=evac + godt)]
                tsq = s.op("scalar", lambda e: e.activation(
                    AP(o_dT, 1024, [[2048, 128], [1, 1024]]),
                    AP(o_dT, 0, [[2048, 128], [1, 1024]]),
                    AF.Square, bias=0.0, scale=1.0), deps=tq_x)
                trd = s.op("vector", lambda e: e.tensor_reduce(
                    sq4p[:, :], AP(o_dT, 1024, [[2048, 128], [64, 16], [1, 64]]),
                    AX.X, ALU.add), deps=[tsq])
                tqs = s.op("vector", lambda e: e.tensor_scalar_mul(
                    smalls[:, 48:64], sq4p[:, :], 0.25), deps=[trd])
                tmq = s.op("scalar", lambda e: e.activation(
                    smalls[:, 32:48], sq4p[:, :], AF.Sqrt, bias=0.0,
                    scale=cc(C_Q25)), deps=[trd])
                tk_x2 = [s.dma("sync", lambda e, par=par:
                    e.dma_start_transpose(
                        AP(o_dT, 0, [[2048, 128], [64, 16], [1, 64]]),
                        AP(kstat, (par * 16) * 128,
                           [[64 * 128, 64], [128, 16], [1, 128]])),
                    deps=[tqs, tmq])]
                tsk = s.op("scalar", lambda e: e.activation(
                    AP(o_dT, 1024, [[2048, 128], [1, 1024]]),
                    AP(o_dT, 0, [[2048, 128], [1, 1024]]),
                    AF.Square, bias=0.0, scale=1.0), deps=tk_x2)
                trk = s.op("vector", lambda e: e.tensor_reduce(
                    sq4p[:, :], AP(o_dT, 1024, [[2048, 128], [64, 16], [1, 64]]),
                    AX.X, ALU.add), deps=[tsk])
                tks = s.op("vector", lambda e: e.tensor_copy(
                    smalls[:, 0:16], sq4p[:, :]), deps=[trk])
                tmk = s.op("scalar", lambda e: e.activation(
                    smalls[:, 16:32], sq4p[:, :], AF.Sqrt, bias=0.0, scale=1.0),
                    deps=[trk])
                gtd = [last_tdmp[0]] if last_tdmp[0] else []
                thm = s.op("vector", lambda e: e.transpose(
                    hmT[:, :], smalls[:, :]), deps=[tqs, tmq, tks, tmk] + gtd)
                hslot = 2 + par
                grw = ([] if st == 0 else [ST[st - 1]["mm_ab_last"]]) \
                    if par == 0 else []
                tdmp = s.dma("gpsimd", lambda e, hslot=hslot: e.dma_start(
                    out=AP(hscr, hslot * 8192, [[64, 128], [1, 64]]),
                    in_=hmT.ap()),
                    deps=[thm] + hsc_guard.get(hslot, []) + grw)
                rows_here = []
                for i in range(4):
                    rows_here.append(s.dma("gpsimd", lambda e, i=i, par=par,
                                           hslot=hslot: e.dma_start(
                        out=AP(qmov, 65 * 4096 + par * 2048 + 32 * i,
                               [[4096, 1], [128, 16], [1, 32]]),
                        in_=AP(hscr, hslot * 8192 + (32 * i) * 64 + 32,
                               [[64, 16], [1, 32]])),
                        deps=[tdmp]))
                    rows_here.append(s.dma("gpsimd", lambda e, i=i, par=par,
                                           hslot=hslot: e.dma_start(
                        out=AP(qmov, 66 * 4096 + par * 2048 + 32 * i,
                               [[4096, 1], [128, 16], [1, 32]]),
                        in_=AP(hscr, hslot * 8192 + (32 * i + 16) * 64 + 32,
                               [[64, 16], [1, 32]])),
                        deps=[tdmp]))
                    rows_here.append(s.dma("gpsimd", lambda e, i=i, par=par,
                                           hslot=hslot: e.dma_start(
                        out=AP(kstat, 64 * 8192 + par * 2048 + 32 * i,
                               [[8192, 1], [128, 16], [1, 32]]),
                        in_=AP(hscr, hslot * 8192 + (32 * i) * 64,
                               [[64, 16], [1, 32]])),
                        deps=[tdmp]))
                    rows_here.append(s.dma("gpsimd", lambda e, i=i, par=par,
                                           hslot=hslot: e.dma_start(
                        out=AP(kstat, 65 * 8192 + 4096 + par * 2048 + 32 * i,
                               [[8192, 1], [128, 16], [1, 32]]),
                        in_=AP(hscr, hslot * 8192 + (32 * i + 16) * 64,
                               [[64, 16], [1, 32]])),
                        deps=[tdmp]))
                hsc_guard[hslot] = rows_here
                last_tdmp[0] = tdmp
                tk_rows += rows_here
            ST[st]["rows"] = tk_rows

        def pP4(st):
            vbuf = (st % 2) * T_ST * D
            tk_odT = []
            tk_mm2_last = None if st == 0 else ST[st - 1]["mm2_last"]
            prevband = dict() if st == 0 else dict(ST[st - 1]["prevband"])
            mm2_extra = [] if st == 0 else [ST[st - 1]["hm14"]]
            tmpBs = [bankB, bankD]
            tmpCs = [bankC, bankE]
            tk_vg = []

            def mk_ab(SB):
                pb_rn = pRNs[SB % 2]

                def mm_ab(te, SB=SB, pb_rn=pb_rn):
                    last = None
                    for b2i in range(2):
                        for f in range(16):
                            blk0 = SB * 32 + b2i * 16 + f
                            last = te.matmul(
                                AP(pb_rn, (b2i * 64) * 512 + f * 32,
                                   [[512, 64], [1, 32]]),
                                AP(kstat, blk0, [[8192, 67], [128, 64]]),
                                AP(qmov, blk0, [[4096, 67], [128, 32]]),
                                start=True, stop=True,
                                tile_position=(0, b2i * 64))
                    return last
                return mm_ab

            ab = [None] * NSB
            ab[0] = s.op("tensor", mk_ab(0),
                         deps=ST[st]["rows"] + ST[st]["evac_all"] +
                         list(prevband.get(("rn", 0), [])))
            ab[1] = s.op("tensor", mk_ab(1),
                         deps=list(prevband.get(("rn", 1), [])))
            for p in range(2):
                tmpB = tmpBs[p]
                tmpC = tmpCs[p]
                tAcc = {0: [], 1: []}
                tA = []
                tN = []
                for sh in range(2):
                    for b2i in range(2):
                        t_ = s.op("vector", lambda e, sh=sh, b2i=b2i:
                            e.tensor_scalar_max(
                                bankA[b2i * 64 + sh * 32:b2i * 64 + sh * 32 + 32, :],
                                pRNs[sh][b2i * 64:b2i * 64 + 32, :], 0.0),
                            deps=[ab[2 * p + sh]] + prevband.get("A", []))
                        tA.append(t_)
                        tAcc[sh].append(t_)
                        t_ = s.op("vector", lambda e, sh=sh, b2i=b2i:
                            e.tensor_copy(
                                bankF[b2i * 64 + sh * 32:b2i * 64 + sh * 32 + 32, :],
                                pRNs[sh][b2i * 64 + 32:b2i * 64 + 64, :]),
                            deps=[ab[2 * p + sh]] + prevband.get("F", []))
                        tN.append(t_)
                        tAcc[sh].append(t_)
                if p == 0:
                    ab[2] = s.op("tensor", mk_ab(2), deps=tAcc[0])
                    ab[3] = s.op("tensor", mk_ab(3), deps=tAcc[1])
                tB = s.op("scalar", lambda e, tmpB=tmpB: e.activation(
                    tmpB[:, :], bankA[:, :], AF.Sqrt, bias=cc(C_EPS6), scale=1.0),
                    deps=tA + prevband.get(("B", p), []))
                tC = s.op("scalar", lambda e, tmpB=tmpB, tmpC=tmpC: e.activation(
                    tmpC[:, :], tmpB[:, :], AF.Sin, bias=cc(C_PI2),
                    scale=cc(C_MQ25)), deps=[tB] + prevband.get(("C", p), []))
                tU = s.op("vector", lambda e, tmpB=tmpB, tmpC=tmpC: e.tensor_mul(
                    tmpB[:, :], tmpC[:, :], tmpC[:, :]), deps=[tC])
                tW = s.op("vector", lambda e, tmpB=tmpB, tmpC=tmpC: e.tensor_scalar(
                    tmpC[:, :], tmpB[:, :], 2.0, -1.0, ALU.mult, ALU.add),
                    deps=[tU])
                tW = s.op("vector", lambda e, tmpC=tmpC: e.tensor_mul(
                    tmpC[:, :], tmpC[:, :], tmpC[:, :]), deps=[tW])
                tM = s.op("vector", lambda e, tmpB=tmpB, tmpC=tmpC: e.tensor_scalar(
                    tmpB[:, :], tmpC[:, :], 0.3, 0.85, ALU.mult, ALU.add),
                    deps=[tW])
                tDm = s.op("vector", lambda e, tmpB=tmpB, tmpC=tmpC: e.tensor_mul(
                    tmpC[:, :], tmpB[:, :], bankA[:, :]), deps=[tM])
                tR = s.op("scalar", lambda e, tmpB=tmpB, tmpC=tmpC: e.activation(
                    tmpB[:, :], tmpC[:, :], AF.Ln, bias=cc(C_EPS6),
                    scale=1.0), deps=[tDm])
                tR = s.op("scalar", lambda e, tmpB=tmpB: e.activation(
                    tmpB[:, :], tmpB[:, :], AF.Exp, bias=0.0,
                    scale=cc(C_M1)), deps=[tR])
                tF = s.op("vector", lambda e, tmpB=tmpB: e.tensor_mul(
                    bankF[:, :], bankF[:, :], tmpB[:, :]), deps=[tR] + tN)
                tP = s.op("scalar", lambda e: e.activation(
                    ptil[:, :], bankF[:, :], AF.Exp, bias=cc(C_M10), scale=1.0),
                    deps=[tF] + prevband.get("P", []))
                tSel = s.op("tensor", lambda e: e.matmul(
                    pS[:, :], SEL2[:, :], ptil[:, :], start=True, stop=True),
                    deps=[tP] + prevband.get("S", []))
                tRec = s.op("scalar", lambda e, tmpB=tmpB: e.activation(
                    tmpB[:, :], pS[:, :], AF.Ln, bias=cc(C_TINY),
                    scale=1.0), deps=[tSel])
                tAt0 = s.op("vector", lambda e, tmpB=tmpB, tmpC=tmpC: e.tensor_sub(
                    tmpC[:, :], bankF[:, :], tmpB[:, :]), deps=[tRec])
                tAt1 = s.op("scalar", lambda e, tmpC=tmpC: e.activation(
                    attnb[:, :], tmpC[:, :], AF.Exp, bias=cc(C_M10),
                    scale=1.0), deps=[tAt0] + prevband.get("At", []))
                tAt = s.op("vector", lambda e: e.tensor_mul(
                    attnb[:, :], attnb[:, :], tmask[:, :]), deps=[tAt1])

                tk_vg = []
                for sh in range(2):
                    for b2i in range(2):
                        for t2 in range(2):
                            tk_vg.append(s.dma("gpsimd", lambda e, p=p, sh=sh,
                                               b2i=b2i, t2=t2, vb=vbuf:
                                e.dma_start(
                                    out=AP(v_tg,
                                           (b2i * 64 + sh * 32 + 16 * t2) * 1024,
                                           [[1024, 16], [64, 16], [1, 64]]),
                                    in_=AP(vscr, vb + ((2 * p + sh) * 64 +
                                                       b2i * 32 + t2) * 1024,
                                           [[64, 16], [2048, 16], [1, 64]])),
                                deps=[ST[st]["vdump"], tk_mm2_last]))

                for sh in range(2):
                    def mm_2(te, sh=sh):
                        last = None
                        for b2i in range(2):
                            for f in range(16):
                                last = te.matmul(
                                    AP(pOT, (b2i * 64) * 512 + f * 32,
                                       [[512, 64], [1, 32]]),
                                    AP(v_tg, (b2i * 64 + sh * 32) * 1024 + f * 64,
                                       [[1024, 32], [1, 64]]),
                                    AP(attnb, (b2i * 64 + sh * 32) * 512 + f * 32,
                                       [[512, 32], [1, 32]]),
                                    start=True, stop=True,
                                    tile_position=(b2i * 64 + sh * 32, b2i * 64))
                        return last
                    godt4 = tk_odT[-4:] if tk_odT else (
                        [] if st == 0 else ST[st - 1]["odt4"])
                    tk_mm2_last = s.op("tensor", mm_2,
                                       deps=[tAt] + tk_vg + godt4 +
                                       (mm2_extra if (p == 0 and sh == 0) else []))
                    SB = 2 * p + sh
                    for b2i in range(2):
                        for hp in range(2):
                            tk_odT.append(s.op("vector",
                                lambda e, SB=SB, b2i=b2i, hp=hp: e.tensor_copy(
                                    AP(o_dT, (hp * 64) * (8 * T_ST) + SB * 64 + b2i * 32,
                                       [[8 * T_ST, 64], [2, 16], [1, 2], [T_ST, 8]]),
                                    AP(pOT, (b2i * 64) * 512 + hp,
                                       [[512, 64], [32, 16], [16, 2], [2, 8]])),
                                deps=[tk_mm2_last]))

                prevband.update({
                    ("rn", 0): tAcc[0],
                    ("rn", 1): tAcc[1],
                    ("B", p): [tAt0],
                    ("C", p): [tAt1],
                    "A": [tDm],
                    "F": [tAt0],
                    "P": [tSel, tk_mm2_last], "S": [tRec], "At": [tk_mm2_last],
                })
            ST[st]["mm_ab_last"] = ab[NSB - 1]
            ST[st]["mm2_last"] = tk_mm2_last
            ST[st]["odT"] = tk_odT
            ST[st]["odt4"] = tk_odT[-4:]
            ST[st]["vg_last"] = tk_vg[-1]
            ST[st]["prevband"] = prevband

        def pP5(st):
            xr = xres[st % 2]
            tk_afm = []
            tk_atm = []
            tmo = None
            hg2 = [] if st == 0 else ST[st - 1]["hg_last2"]
            for oc in range(8):
                pb_ao = pQK[oc % 2]

                def mm_o(te, oc=oc, pb_ao=pb_ao):
                    last = None
                    for h2 in range(8):
                        last = te.matmul(
                            pb_ao[:, :],
                            wout_sb[:, h2, oc * 128:oc * 128 + 128],
                            o_dT[:, h2, :],
                            start=(h2 == 0), stop=(h2 == 7))
                    return last
                dmo = ST[st]["odT"] + hg2 if oc < 2 else [tk_afm[oc - 2]]
                tmo = s.op("tensor", mm_o, deps=dmo)
                dslot = tk_atm[oc - 2:oc - 1] if oc >= 2 else \
                    ([] if st == 0 else [ST[st - 1]["ytm_last2"][oc]])
                tk_afm.append(s.op("scalar", lambda e, oc=oc, pb_ao=pb_ao:
                    e.activation(
                        afm[:, oc % 2, :], pb_ao[:, :], AF.Identity,
                        bias=bout_sb[:, oc:oc + 1], scale=1.0),
                    deps=[tmo] + dslot))
                gres = [] if (oc >= 2 or st == 0) else [ST[st - 1]["res2"][0],
                                                        ST[st - 1]["res2"][1]]
                tk_atm.append(s.dma("sync", lambda e, oc=oc:
                    e.dma_start_transpose(
                        AP(a_tm, oc * 128, [[PT * D, 128], [D, 2], [1, 128]]),
                        AP(afm, (oc % 2) * T_ST, [[2 * T_ST, 128], [128, 2], [1, 128]])),
                    deps=[tk_afm[oc]] + gres))
            tk_res1 = []
            for pt in range(PT):
                tk_res1.append(s.op("vector", lambda e, pt=pt, xr=xr:
                    e.tensor_add(
                        xr[:, pt, :], xr[:, pt, :], a_tm[:, pt, :]),
                    deps=tk_atm))
            ST[st]["mm_o_last"] = tmo
            ST[st]["res1"] = tk_res1

        def pP67(st):
            xr = xres[st % 2]
            st6x, sq4x, mv2x, acolx, bcolx = scrA
            tk_xhatT = []
            for pt in range(PT):
                c0 = 8 * pt

                def col(i, c0=c0):
                    return mv2x[:, c0 + i:c0 + i + 1]
                src = xr[:, pt, :]
                t1 = s.op("vector", lambda e, src=src: e.bn_stats(
                    st6x[:, 0:6], src[:, 0:512]), deps=[ST[st]["res1"][pt]])
                t1 = s.op("vector", lambda e, src=src: e.bn_stats(
                    st6x[:, 6:12], src[:, 512:1024]), deps=[t1])
                t1 = s.op("vector", lambda e, col=col: e.tensor_reduce(
                    col(0), AP(st6x, 1, [[12, 128], [3, 4]]), AX.X, ALU.add),
                    deps=[t1])
                t1 = s.op("vector", lambda e: e.tensor_mul(
                    sq4x[:, 0:4],
                    AP(st6x, 1, [[12, 128], [3, 4]]),
                    AP(st6x, 1, [[12, 128], [3, 4]])), deps=[t1])
                t1 = s.op("vector", lambda e, col=col: e.tensor_reduce(
                    col(1), sq4x[:, 0:4], AX.X, ALU.add), deps=[t1])
                t1 = s.op("vector", lambda e, col=col: e.tensor_reduce(
                    col(2), AP(st6x, 2, [[12, 128], [3, 4]]), AX.X, ALU.add),
                    deps=[t1])
                t1 = s.op("vector", lambda e, col=col: e.tensor_scalar_mul(
                    col(3), col(2), 1.0 / 1024.0), deps=[t1])
                t1 = s.op("vector", lambda e, col=col: e.scalar_tensor_tensor(
                    col(3), col(1), 0.25, col(3), ALU.mult, ALU.add), deps=[t1])
                t1 = s.op("vector", lambda e, col=col: e.scalar_tensor_tensor(
                    col(4), col(0), -0.0625, col(0), ALU.mult, ALU.mult),
                    deps=[t1])
                t1 = s.op("vector", lambda e, col=col: e.tensor_add(
                    col(5), col(3), col(4)), deps=[t1])
                t3 = s.op("scalar", lambda e, col=col: e.activation(
                    acolx[:, :], col(5), AF.Sqrt, bias=cc(C_EPS5),
                    scale=1.0), deps=[t1])
                t3 = s.op("vector", lambda e: e.reciprocal(
                    acolx[:, :], acolx[:, :]), deps=[t3])
                t3 = s.op("vector", lambda e, col=col: e.tensor_mul(
                    col(6), col(5), acolx[:, :]), deps=[t3])
                t3 = s.op("vector", lambda e, col=col: e.tensor_mul(
                    col(6), col(6), acolx[:, :]), deps=[t3])
                t3 = s.op("scalar", lambda e, col=col: e.activation(
                    col(7), col(6), AF.Sqrt, bias=cc(C_EPS5), scale=1.0),
                    deps=[t3])
                t3 = s.op("vector", lambda e, col=col: e.reciprocal(
                    col(7), col(7)), deps=[t3])
                t3 = s.op("vector", lambda e, col=col: e.tensor_mul(
                    acolx[:, :], acolx[:, :], col(7)), deps=[t3])
                t4 = s.op("vector", lambda e, col=col: e.scalar_tensor_tensor(
                    bcolx[:, :], col(0), -0.25, acolx[:, :],
                    ALU.mult, ALU.mult), deps=[t3])
                u5 = s.op("scalar", lambda e, pt=pt, src=src: e.activation(
                    a_tm[:, pt, :], src, AF.Identity, bias=bcolx[:, :],
                    scale=acolx[:, :]), deps=[t4])
                tk_xhatT.append(s.dma("scalar", lambda e, pt=pt:
                    e.dma_start_transpose(
                        AP(xhatT, pt * 128, [[8 * T_ST, 128], [T_ST, 8], [1, 128]]),
                        AP(a_tm, pt * D, [[PT * D, 128], [128, 8], [1, 128]])),
                    deps=[u5]))

            def mm_g(te):
                last = None
                for kc in range(8):
                    last = te.matmul(pS[0:8, 0:T_ST], wg_sb[:, kc, :],
                                     xhatT[:, kc, :], start=(kc == 0), stop=(kc == 7))
                return last
            tg = s.op("tensor", mm_g, deps=tk_xhatT)
            g0 = s.op("vector", lambda e: e.memset(lgp, -1e30), deps=[tg])
            g1 = s.op("vector", lambda e: e.tensor_copy(
                lgp[0:8, :], pS[0:8, 0:T_ST]), deps=[tg, g0])
            g2 = s.op("vector", lambda e: e.transpose(lgT, lgp), deps=[g1])
            g3 = s.op("vector", lambda e: e.reduce_max(
                mx8[:, :], AP(bankB, 0, [[512, 32], [32, 8], [1, 8]]), AX.X),
                deps=[g2])
            g4 = s.op("vector", lambda e: e.tensor_tensor(
                AP(bankC, 0, [[512, 32], [32, 8], [1, 8]]),
                AP(bankB, 0, [[512, 32], [32, 8], [1, 8]]),
                mx8[:, :, None].broadcast_to([32, 8, 8]), ALU.is_ge), deps=[g3])
            g5 = s.op("vector", lambda e: e.transpose(ohp, ohT), deps=[g4])
            g6 = s.op("vector", lambda e: e.tensor_copy(
                oh_em[:, :], AP(bankB, 32 * 512, [[512, 8], [1, T_ST]])),
                deps=[g5])
            pb_ = ST[st]["prevband"]
            pb_["A"] = pb_.get("A", []) + [g2]
            pb_[("B", 0)] = pb_.get(("B", 0), []) + [g6]
            pb_[("C", 0)] = pb_.get(("C", 0), []) + [g5]
            pb_["S"] = pb_.get("S", []) + [g1]
            ST[st]["xhatT"] = tk_xhatT
            ST[st]["g6"] = g6
            ST[st]["tg"] = tg

        def pP8(st, has_next):
            xr = xres[st % 2]
            tk_hm = []
            tk_hg = []
            pMs = [pOT, pS]
            for hc in range(16):
                e_ = hc // 2
                pb_h = pQK[hc % 2]
                pb_m = pMs[hc % 2]

                def mm_h(te, hc=hc, e_=e_, pb_h=pb_h, pb_m=pb_m):
                    for kc in range(8):
                        te.matmul(pb_h[:, :],
                                  w1_sb[:, kc, e_, (hc % 2) * 128:(hc % 2) * 128 + 128],
                                  xhatT[:, kc, :], start=(kc == 0), stop=(kc == 7))
                    return te.matmul(pb_m[:, 0:T_ST], sel8[:, e_, :],
                                     oh_em[:, :], start=True, stop=True)
                if hc < 2:
                    dh = ST[st]["xhatT"] + [ST[st]["g6"]]
                else:
                    dh = [tk_hm[hc - 2], tk_hg[hc - 2]]
                tmh = s.op("tensor", mm_h, deps=dh)
                hgo = (hc % 2) * T_ST
                th1 = s.op("scalar", lambda e, hc=hc, pb_h=pb_h, hgo=hgo:
                    e.activation(
                        AP(v_tg, hgo, [[1024, 128], [1, T_ST]]), pb_h[:, :],
                        AF.Gelu, bias=b1c_sb[:, hc:hc + 1], scale=1.0),
                    deps=[tmh])
                th2 = s.op("vector", lambda e, hc=hc, pb_m=pb_m, hgo=hgo:
                    e.tensor_mul(
                        hm_sb[:, hc, :], AP(v_tg, hgo, [[1024, 128], [1, T_ST]]),
                        pb_m[:, 0:T_ST]),
                    deps=[th1])
                tk_hg.append(th1)
                tk_hm.append(th2)
            pb_ = ST[st]["prevband"]
            pb_["S"] = pb_.get("S", []) + [tk_hm[15]]
            ST[st]["hm14"] = tk_hm[14]
            ST[st]["hm15"] = tk_hm[15]

            tk_yfm = []
            tk_ytm = []
            tmy = None
            for oc in range(8):
                py_sl = (oc % 2) * 512

                def mm_y(te, oc=oc, py_sl=py_sl):
                    for hc in range(16):
                        te.matmul(pV[:, py_sl:py_sl + T_ST],
                                  w2_sb[:, hc, oc * 128:oc * 128 + 128],
                                  hm_sb[:, hc, :], start=(hc == 0), stop=False)
                    return te.matmul(pV[:, py_sl:py_sl + T_ST],
                                     b2_sb[0:NE, oc * 128:oc * 128 + 128],
                                     oh_em[:, :], start=False, stop=True)
                if oc < 2:
                    dy = tk_hm + [ST[st]["g6"]]
                    if has_next:
                        dy = dy + (ST[st + 1]["evac_eA"] if oc == 0
                                   else ST[st + 1]["evac_eB"])
                else:
                    dy = [tk_yfm[oc - 2]]
                tmy = s.op("tensor", mm_y, deps=dy)
                dslot = tk_ytm[oc - 2:oc - 1] if oc >= 2 else []
                tk_yfm.append(s.op("scalar", lambda e, oc=oc, py_sl=py_sl:
                    e.activation(
                        afm[:, oc % 2, :], pV[:, py_sl:py_sl + T_ST],
                        AF.Identity, bias=0.0, scale=1.0),
                    deps=[tmy] + dslot))
                tk_ytm.append(s.dma("scalar", lambda e, oc=oc:
                    e.dma_start_transpose(
                        AP(a_tm, oc * 128, [[PT * D, 128], [D, 2], [1, 128]]),
                        AP(afm, (oc % 2) * T_ST, [[2 * T_ST, 128], [128, 2], [1, 128]])),
                    deps=[tk_yfm[oc]]))
            tk_res2 = []
            for pt in range(PT):
                tk_res2.append(s.op("vector", lambda e, pt=pt, xr=xr:
                    e.tensor_add(
                        xr[:, pt, :], xr[:, pt, :], a_tm[:, pt, :]),
                    deps=tk_ytm))
            ST[st]["hg_last2"] = tk_hg[-2:]
            ST[st]["mm_y_last"] = tmy
            ST[st]["yfm"] = tk_yfm
            ST[st]["ytm_last2"] = {0: tk_ytm[-2], 1: tk_ytm[-1]}
            ST[st]["res2"] = tk_res2

        def pP9(st):
            xr = xres[st % 2]
            ST[st]["store"] = s.dma("gpsimd", lambda e, o=st * T_ST, xr=xr:
                e.dma_start(
                    out=y_out.ap()[o:o + T_ST, :]
                        .rearrange("(pt p) d -> p pt d", p=128),
                    in_=xr.ap()), deps=ST[st]["res2"])

        pP0(0); pP1a(0); pP1b(0); pP2(0); pP2b(0); pP3(0)
        for st in range(n_st):
            has_next = st + 1 < n_st
            if has_next:
                pP0(st + 1)
            pP4(st)
            if has_next:
                pP1a(st + 1)
            pP5(st)
            if has_next:
                pP1b(st + 1)
                pP2(st + 1)
            pP67(st)
            if has_next:
                pP3(st + 1)
            pP8(st, has_next)
            if has_next:
                pP2b(st + 1)
            pP9(st)

        tk_store_last = ST[n_st - 1]["store"]
        if debug:
            x_res = xres[(n_st - 1) % 2]
            dbg_map = {"xlnT": xlnT2, "qmov": qmov, "kstat": kstat,
                       "bankA": bankA, "bankF": bankF, "ptil": ptil,
                       "attnb": attnb, "o_dT": o_dT, "a_tm": a_tm,
                       "oh_em": oh_em, "hm_sb": hm_sb,
                       "x_res": x_res, "v_tg": v_tg, "smalls": smalls,
                       "hmT": hmT}
            last = ST[n_st - 1]
            for nm in dbg_outs:
                t_ = dbg_map[nm]
                tk_store_last = s.dma("gpsimd", lambda e, nm=nm, t_=t_:
                    e.dma_start(out=dbg_outs[nm].ap(),
                                in_=AP(t_, 0, [[int(np.prod(t_.shape[1:])),
                                                t_.shape[0]],
                                               [1, int(np.prod(t_.shape[1:]))]])),
                    deps=[tk_store_last, last["mm2_last"], last["res2"][0],
                          last["res2"][1]])

        s.q["gpsimd"].append(("w", s._deps([tk_store_last]), s.cnt["gpsimd"]))

        incmap = {}
        for name in ENGS:
            nincs = 0
            m = {}
            for item in s.q[name]:
                if item[0] == "i":
                    if (name, item[2]) in s.used:
                        nincs += 1
                    m[item[2]] = nincs
            incmap[name] = m

        def emitter(name):
            def run(eng):
                seen = {}
                m_self = incmap[name]
                for item in s.q[name]:
                    if item[0] == "w":
                        emitted = item[2]
                        for (de, dn) in item[1]:
                            if de.endswith("_dma"):
                                val = dn
                            else:
                                if de == name and dn <= emitted - 16:
                                    continue
                                val = incmap[de][dn]
                            if seen.get(de, -1) >= val:
                                continue
                            seen[de] = val
                            eng.wait_ge(SEM[de], val)
                    elif item[0] == "i":
                        inst = item[1](eng)
                        if (name, item[2]) in s.used:
                            inst.then_inc(SEM[name], 1)
                    else:
                        item[1](eng).then_inc(SEM[name + "_dma"], 16)
            return run

        @block.tensor
        def _(e):
            emitter("tensor")(e)

        @block.vector
        def _(e):
            emitter("vector")(e)

        @block.scalar
        def _(e):
            emitter("scalar")(e)

        @block.sync
        def _(e):
            emitter("sync")(e)

        @block.gpsimd
        def _(e):
            emitter("gpsimd")(e)

    return nc


_NC_CACHE = {}


def kernel(L0_attn_x, L0_attn_W_qkv, L0_attn_b_qkv, L0_attn_W_out, L0_attn_b_out,
           L0_ffn_Wg, L0_ffn_ln_scale, L0_ffn_ln_bias, L0_ffn_W1, L0_ffn_b1,
           L0_ffn_W2, L0_ffn_b2):
    x = np.ascontiguousarray(np.asarray(L0_attn_x, dtype=np.float32)
                             .reshape(-1, D))
    n_tokens = x.shape[0]
    per = n_tokens // N_CORES

    key = per
    if key not in _NC_CACHE:
        _NC_CACHE[key] = build_program(per)
    nc = _NC_CACHE[key]

    tmaskc = np.zeros((128, 512), np.float32)
    for b2i in range(2):
        for sh in range(2):
            for t2 in range(2):
                r = b2i * 64 + sh * 32 + t2 * 16
                for f in range(16):
                    tmaskc[r:r + 16,
                           f * 32 + t2 * 16:f * 32 + t2 * 16 + 16] = 1.0
    sel2c = np.zeros((128, 128), np.float32)
    for r0 in range(0, 128, 16):
        sel2c[r0:r0 + 16, r0:r0 + 16] = 1.0
    sel8c = np.zeros((NE, NE, 128), np.float32)
    for e_ in range(NE):
        sel8c[e_, e_, :] = 1.0
    onesc = np.ones((4096,), np.float32)

    w = {
        "tmaskc": tmaskc, "sel2c": sel2c, "sel8c": sel8c, "onesc": onesc,
        "wqkv": np.ascontiguousarray(np.asarray(L0_attn_W_qkv, np.float32)),
        "bqkv": np.ascontiguousarray(np.asarray(L0_attn_b_qkv, np.float32)),
        "wout": np.ascontiguousarray(np.asarray(L0_attn_W_out, np.float32)),
        "bout": np.ascontiguousarray(np.asarray(L0_attn_b_out, np.float32)),
        "wg": np.ascontiguousarray(np.asarray(L0_ffn_Wg, np.float32)),
        "lns": np.ascontiguousarray(np.asarray(L0_ffn_ln_scale, np.float32)),
        "lnb": np.ascontiguousarray(np.asarray(L0_ffn_ln_bias, np.float32)),
        "w1": np.ascontiguousarray(np.asarray(L0_ffn_W1, np.float32)),
        "b1": np.ascontiguousarray(np.asarray(L0_ffn_b1, np.float32)),
        "w2": np.ascontiguousarray(np.asarray(L0_ffn_W2, np.float32)),
        "b2": np.ascontiguousarray(np.asarray(L0_ffn_b2, np.float32)),
    }
    in_maps = []
    for c in range(N_CORES):
        m = {"x": x[c * per:(c + 1) * per]}
        m.update(w)
        in_maps.append(m)

    res = run_bass_kernel_spmd(nc, in_maps, core_ids=list(range(N_CORES)))
    out = np.concatenate([r["y"] for r in res.results], axis=0)
    return out.reshape(np.asarray(L0_attn_x).shape).astype(np.float32)
